# version 01
# baseline (speedup 1.0000x reference)
import numpy as np
import ml_dtypes
from contextlib import ExitStack
import concourse.bass as bass
import concourse.mybir as mybir
from concourse.bass_utils import run_bass_kernel_spmd
from concourse.aot_env import allow_ici_all_to_all

F32 = mybir.dt.float32
BF16 = mybir.dt.bfloat16
I32 = mybir.dt.int32
AF = mybir.ActivationFunctionType
ALU = mybir.AluOpType
BF = ml_dtypes.bfloat16

P = 128
T = 2048
D = 1024
NC = 8
DCH = D // P
TCH = T // P
NQ = T // 512
H = 16
HD = 64

ARENA = 69504

N_SUB = 8
PHASE = 99
DBG_COLS = None
DBG_OVERWAIT = False
LNPHASE = 99
L1BPHASE = 99
ONLY_SUB = None

class _RecInst:
    __slots__ = ("incs",)
    def __init__(self):
        self.incs = []
    def then_inc(self, sem, n=1):
        self.incs.append((sem, n))
        sem.count += n
        return self

class _EngRec:
    def __init__(self, name):
        self.name = name
        self.calls = []
    def __getattr__(self, meth):
        def call(*a, **k):
            inst = _RecInst()
            self.calls.append((meth, a, k, inst))
            return inst
        return call

class Sem:
    def __init__(self, name):
        self.name = name
        self.count = 0
        self.handle = None

class Recorder:
    def __init__(self):
        self.sync = _EngRec("sync")
        self.tensor = _EngRec("tensor")
        self.vector = _EngRec("vector")
        self.scalar = _EngRec("scalar")
        self.gpsimd = _EngRec("gpsimd")

def _replay(eng, calls):
    for meth, a, k, inst in calls:
        a = tuple(x.handle if isinstance(x, Sem) else x for x in a)
        k = {kk: (vv.handle if isinstance(vv, Sem) else vv) for kk, vv in k.items()}
        real = getattr(eng, meth)(*a, **k)
        for sem, n in inst.incs:
            real.then_inc(sem.handle, n)

class Buf:
    def __init__(self, t, off_b, w, n=1, dtype=BF16):
        self.t, self.off_b, self.w, self.n, self.dtype = t, off_b, w, n, dtype
        self.m = 2 if dtype == F32 else 1
    def at(self, i, c0, c1):
        a = self.off_b + self.m * (i * self.w + c0)
        b = self.off_b + self.m * (i * self.w + c1)
        ap = self.t[:, a:b]
        return ap.bitcast(F32) if self.m == 2 else ap
    def __getitem__(self, i):
        return self.at(i, 0, self.w)
    def rows(self, r0, r1, i=0, c0=0, c1=None):
        c1 = self.w if c1 is None else c1
        a = self.off_b + self.m * (i * self.w + c0)
        b = self.off_b + self.m * (i * self.w + c1)
        ap = self.t[r0:r1, a:b]
        return ap.bitcast(F32) if self.m == 2 else ap

class Arena:
    def __init__(self, t, cols):
        self.t, self.cols, self.off = t, cols, 0
    def take(self, w, n=1, dtype=BF16):
        m = 2 if dtype == F32 else 1
        b = Buf(self.t, self.off, w, n, dtype)
        self.off += w * n * m
        assert self.off <= self.cols, f"arena overflow: {self.off} > {self.cols}"
        return b
    def reset(self, off=0):
        self.off = off

def _wchunks(w):
    K, C = w.shape
    assert K % P == 0
    return np.ascontiguousarray(w.reshape(K // P, P, C).astype(np.float32))

def _rope_tables():
    inv = 1.0 / (10000.0 ** (np.arange(0, HD, 2)[: HD // 2].astype(np.float32) / HD))
    ang = np.outer(np.arange(T, dtype=np.float32), inv)
    cos, sin = np.cos(ang), np.sin(ang)
    cosT = np.tile(cos.T, (4, 1)).astype(np.float32)
    sinT = np.tile(sin.T, (4, 1)).astype(np.float32)
    sgn = np.ones((P, 1), np.float32)
    sgn[0:32] = -1.0
    sgn[64:96] = -1.0
    return cosT, np.ascontiguousarray(sinT * sgn)

def _perm_eo():
    perm = []
    for h in range(H):
        perm += list(range(64 * h, 64 * h + 64, 2))
        perm += list(range(64 * h + 1, 64 * h + 64, 2))
    return np.asarray(perm)

def _masks():
    k = np.arange(P)[:, None]
    q = np.arange(512)[None, :]
    mm = np.zeros((P, 4, 512), np.float32)
    for di in range(4):
        mm[:, di, :] = (q >= k + 128 * di)
    return np.ascontiguousarray(mm.astype(BF))

BOFF = {
    "wo_b": 0, "bq": 8, "bk": 16, "bv": 24,
    "bfic": 32, "bo_f": 56,
    "bqkv": 64, "bproj": 88,
    "b_fc": 96, "b_proj": 128,
    "Bq": 136, "Bk": 144, "Bv": 152, "Bo": 160, "Bl": 168,
    "eps": 170, "epsa": 171,
}

def build_nc(n_sub=N_SUB, detect_races=False):
    nc = bass.Bass("TRN2", target_bir_lowering=False, debug=False,
                   detect_race_conditions=detect_races)
    dt = {}

    def din(name, shape, dtype=F32):
        ap = nc.dram_tensor(name, list(shape), dtype, kind="ExternalInput").ap()
        dt[name] = ap
        return ap

    din("xT", (DCH, P, T))
    din("wq0", (DCH, P, D)); din("wk0", (DCH, P, D)); din("wv0", (DCH, P, D)); din("wo0", (DCH, P, D))
    din("wqb", (DCH, P, D)); din("wkb", (DCH, P, D)); din("wvb", (DCH, P, D))
    din("wfic", (DCH, P, 3 * D)); din("ufic", (DCH, P, 3 * D))
    din("wovT", (DCH, P, D)); din("uovT", (DCH, P, D)); din("incb", (DCH, P, 256))
    din("wqkv", (DCH, P, 3 * D)); din("wprj", (DCH, P, D))
    din("wfcT", (DCH, P, 4096)); din("wprT", (4096 // P, P, D))
    din("afcT", (DCH, P, 16)); din("bfcT", (1, 16, 4096))
    din("aprT", (4096 // P, P, 16)); din("bprT", (1, 16, D))
    din("wqB", (DCH, P, D)); din("wlB", (DCH, P, 256))
    din("wkB", (2, P, D)); din("wvB", (2, P, D)); din("woB", (DCH, P, D))
    din("biases", (P, 200))
    din("cosT", (P, T)); din("sinT", (P, T))
    din("maskmul", (P, 4, 512), BF16)
    din("onesb", (P, T), BF16); din("onesrow", (1, 512), BF16)
    din("headsum", (P, 2), BF16); din("scal", (1, 8)); din("scalb", (P, 1))
    din("bvrow", (1, 2 * D), BF16)

    yT = nc.dram_tensor("yT", [DCH, P, T], F32, kind="ExternalOutput").ap()
    ydbg = nc.dram_tensor("ydbg", [P, 16384], BF16, kind="ExternalOutput").ap()

    ds = {}
    def dints(name, shape, dtype):
        ds[name] = nc.dram_tensor(name, list(shape), dtype).ap()
    dints("cc_in", (NC, DCH, P, 256), BF16)
    dints("cc_out", (NC, DCH, P, 256), BF16)
    dints("xlnd", (DCH, P, T), BF16)
    dints("cc2_in", (DCH, P, 256), BF16)
    dints("cc2_out", (NC * DCH, P, 256), BF16)
    dints("stat_dram", (4, T), F32)
    dints("den_dram", (16, T), F32)
    dints("xln_dram", (DCH, P, T), BF16)
    dints("q_dram", (DCH, P, T), BF16)
    dints("wqfc_dram", (32, P, 1024), BF16)
    dints("wqpr_dram", (4096 // P, P, D), BF16)
    dints("ffnw_dram", (DCH, P, 8 * D), BF16)

    es = ExitStack()
    r = Recorder()
    s_w = Sem("s_w"); s_g = Sem("s_g"); s_s = Sem("s_s")
    s_pe = Sem("s_pe"); s_v = Sem("s_v"); s_sc = Sem("s_sc"); s_cc = Sem("s_cc")
    s_b = Sem("s_b"); s_gc = Sem("s_gc")
    sems = [s_w, s_g, s_s, s_pe, s_v, s_sc, s_cc, s_b, s_gc]

    with es:
        sbuf = lambda name, shape, dtype: es.enter_context(nc.sbuf_tensor(name, list(shape), dtype))
        env = dict(dt=dt, ds=ds, yT=yT, ydbg=ydbg)
        env["xr"] = sbuf("xr", (P, DCH, T), F32)
        env["ones_t"] = sbuf("ones_t", (P, T), BF16)
        env["ones_r"] = sbuf("ones_r", (1, 512), BF16)
        env["hsum"] = sbuf("hsum", (P, 2), BF16)
        env["bia"] = sbuf("bia", (P, 200), F32)
        env["scl"] = sbuf("scl", (1, 8), F32)
        env["sclb"] = sbuf("sclb", (P, 1), F32)
        env["dsb"] = sbuf("dsb", (P, 512), F32)
        env["AB"] = Arena(sbuf("AB", (P, ARENA), BF16), ARENA)
        env["pb"] = [es.enter_context(nc.psum_tensor(f"pb{i}", [P, 512], F32)) for i in range(8)]

        emit_program(r, sems, env, n_sub)

        for s in sems:
            s.handle = es.enter_context(nc.semaphore(s.name))
        es.enter_context(allow_ici_all_to_all())
        block = es.enter_context(nc.Block())
        block.sync(lambda eng: _replay(eng, r.sync.calls))
        block.tensor(lambda eng: _replay(eng, r.tensor.calls))
        block.vector(lambda eng: _replay(eng, r.vector.calls))
        block.scalar(lambda eng: _replay(eng, r.scalar.calls))
        block.gpsimd(lambda eng: _replay(eng, r.gpsimd.calls))
    return nc


def emit_program(r, sems, env, n_sub):
    global PHASE, LNPHASE, DBG_COLS, DBG_OVERWAIT, L1BPHASE, ONLY_SUB
    s_w, s_g, s_s, s_pe, s_v, s_sc, s_cc, s_b, s_gc = sems
    dt, ds, yT = env["dt"], env["ds"], env["yT"]
    xr = env["xr"]
    ones_t, ones_r, hsum = env["ones_t"], env["ones_r"], env["hsum"]
    bia, scl, sclb, dsb = env["bia"], env["scl"], env["sclb"], env["dsb"]
    AB = env["AB"]
    pb = env["pb"]

    bcol = lambda name, j=0: bia[:, BOFF[name] + j: BOFF[name] + j + 1]

    def mark(eng_rec, sem, n=1):
        last = eng_rec.calls[-1] if eng_rec.calls else None
        if last is not None and not last[3].incs and last[0] != "wait_ge":
            last[3].then_inc(sem, n)
        else:
            nop = "engine_nop" if eng_rec.name in ("vector", "gpsimd") else "nop"
            getattr(eng_rec, nop)().then_inc(sem, n)

    ab = AB.take(T, 2, F32)
    gsc = AB.take(T, 1, F32)
    AB_BASE = AB.off

    def barrier():
        r.sync.wait_ge(s_w, s_w.count)
        r.sync.nop().then_inc(s_b, 1)
        r.gpsimd.wait_ge(s_g, s_g.count)
        r.gpsimd.engine_nop().then_inc(s_b, 1)
        r.scalar.wait_ge(s_s, s_s.count)
        r.scalar.nop().then_inc(s_b, 1)
        r.vector.engine_nop().then_inc(s_b, 1)
        r.tensor.nop().then_inc(s_b, 1)
        v = s_b.count
        for e in (r.tensor, r.vector, r.scalar, r.gpsimd, r.sync):
            e.wait_ge(s_b, v)

    r.sync.dma_start(out=ones_t[:, :], in_=dt["onesb"][:, :]).then_inc(s_w, 16)
    r.sync.dma_start(out=ones_r[0:1, :], in_=dt["onesrow"][0:1, :]).then_inc(s_w, 16)
    r.sync.dma_start(out=hsum[:, :], in_=dt["headsum"][:, :]).then_inc(s_w, 16)
    r.sync.dma_start(out=bia[:, :], in_=dt["biases"][:, :]).then_inc(s_w, 16)
    r.sync.dma_start(out=scl[0:1, :], in_=dt["scal"][0:1, :]).then_inc(s_w, 16)
    r.sync.dma_start(out=sclb[:, :], in_=dt["scalb"][:, :]).then_inc(s_w, 16)
    for p in range(DCH):
        r.sync.dma_start(out=xr[:, p, :], in_=dt["xT"][p, :, :]).then_inc(s_w, 16)
    tok = s_w.count
    for e in (r.tensor, r.vector, r.scalar, r.gpsimd):
        e.wait_ge(s_w, tok)

    def emit_ln(xln, sq, consume=None, nslots=None):
        ns = nslots or DCH
        ones1 = ones_t[:, 0:1]
        for p in range(DCH):
            if p >= ns:
                r.vector.wait_ge(s_pe, s_pe.count)
            r.vector.tensor_copy(out=xln[p % ns], in_=xr[:, p, :])
            mark(r.vector, s_v, 1)
            r.scalar.wait_ge(s_v, s_v.count)
            if p >= 2:
                r.scalar.wait_ge(s_pe, s_pe.count)
            r.scalar.activation(out=sq[p % 2], in_=xln[p % ns], func=AF.Square)
            mark(r.scalar, s_sc, 1)
            r.tensor.wait_ge(s_sc, s_sc.count)
            r.tensor.wait_ge(s_v, s_v.count)
            for n in range(NQ):
                r.tensor.matmul(pb[n][0:1, :], ones1, xln.at(p % ns, 512 * n, 512 * n + 512),
                                start=(p == 0), stop=(p == DCH - 1))
            for n in range(NQ):
                r.tensor.matmul(pb[4 + n][0:1, :], ones1, sq.at(p % 2, 512 * n, 512 * n + 512),
                                start=(p == 0), stop=(p == DCH - 1))
            mark(r.tensor, s_pe, 1)
        mmtok = s_pe.count
        if LNPHASE < 1:
            return s_v.count
        m = ab.rows(0, 1, 0)
        s2 = ab.rows(0, 1, 1)
        tmp = gsc.rows(0, 1, 0)
        r.vector.wait_ge(s_pe, mmtok)
        for n in range(NQ):
            r.vector.tensor_copy(out=ab.rows(0, 1, 0, 512 * n, 512 * n + 512), in_=pb[n][0:1, :])
            r.vector.tensor_copy(out=ab.rows(0, 1, 1, 512 * n, 512 * n + 512), in_=pb[4 + n][0:1, :])
        mark(r.vector, s_v, 1)
        r.tensor.wait_ge(s_v, s_v.count)
        r.vector.tensor_scalar(out=m, in0=m, scalar1=1.0 / D, scalar2=None, op0=ALU.mult)
        r.vector.tensor_scalar(out=s2, in0=s2, scalar1=1.0 / D, scalar2=None, op0=ALU.mult)
        r.vector.tensor_mul(tmp, m, m)
        r.vector.tensor_sub(s2, s2, tmp)
        mark(r.vector, s_v, 1)
        if LNPHASE < 2:
            return s_v.count
        r.scalar.wait_ge(s_v, s_v.count)
        r.scalar.activation(out=s2, in_=s2, func=AF.Sqrt, bias=bcol("eps")[0:1, :])
        mark(r.scalar, s_sc, 1)
        r.vector.wait_ge(s_sc, s_sc.count)
        r.vector.reciprocal(out=s2, in_=s2).then_inc(s_v, 1)
        r.vector.wait_ge(s_v, s_v.count)
        r.vector.scalar_tensor_tensor(out=m, in0=m, scalar=-1.0, in1=s2,
                                      op0=ALU.mult, op1=ALU.mult)
        mark(r.vector, s_v, 1)
        if LNPHASE < 3:
            return s_v.count
        r.gpsimd.wait_ge(s_v, s_v.count)
        r.gpsimd.dma_start(out=ds["stat_dram"][0:1, :], in_=s2).then_inc(s_g, 16)
        r.gpsimd.dma_start(out=ds["stat_dram"][1:2, :], in_=m).then_inc(s_g, 16)
        r.gpsimd.wait_ge(s_g, s_g.count)
        r.gpsimd.dma_start(out=ab[0], in_=ds["stat_dram"][0:1, :].broadcast_to((P, T))).then_inc(s_g, 16)
        r.gpsimd.dma_start(out=ab[1], in_=ds["stat_dram"][1:2, :].broadcast_to((P, T))).then_inc(s_g, 16)
        r.vector.wait_ge(s_g, s_g.count)
        if LNPHASE < 4:
            return s_v.count
        for p in range(DCH):
            r.vector.tensor_mul(gsc[0], xr[:, p, :], ab[0])
            r.vector.tensor_tensor(out=xln[p % ns], in0=gsc[0], in1=ab[1], op=ALU.add)
            if consume is not None:
                mark(r.vector, s_v, 1)
                consume(p, p % ns)
        mark(r.vector, s_v, 1)
        return s_v.count

    def emit_l0a():
        if PHASE < 1:
            return
        AB.reset(AB_BASE)
        xln = AB.take(T, DCH)
        u = AB.take(T, DCH)
        gq = AB.take(T, 1); gk = AB.take(T, 1)
        shb = AB.take(T, 1)
        pT = AB.take(512, 2)
        vaug = AB.take(130, 16)
        wslot = AB.take(D, 3)
        wstg = AB.take(512, 1, F32)
        mskm = AB.take(512, 4)
        wof = Buf(xln.t, xln.off_b, D, DCH)
        csb = AB.take(T, 2, F32)
        stg32 = lambda i: wstg[i]

        ln_tok = emit_ln(xln, Buf(u.t, u.off_b, T, 2))
        r.tensor.wait_ge(s_v, ln_tok)
        r.sync.wait_ge(s_v, ln_tok)
        if PHASE < 2:
            return
        r.sync.dma_start(out=csb[0], in_=dt["cosT"][:, :]).then_inc(s_w, 16)
        r.sync.dma_start(out=csb[1], in_=dt["sinT"][:, :]).then_inc(s_w, 16)
        for di in range(4):
            r.sync.dma_start(out=mskm[di], in_=dt["maskmul"][:, di, :]).then_inc(s_w, 16)
        cm_tok = s_w.count
        r.vector.wait_ge(s_w, cm_tok)

        for g in range(8):
            for wi, wname in enumerate(("wq0", "wk0", "wv0")):
                for kp in range(DCH):
                    stg = stg32(0)[:, 0:128]
                    r.sync.wait_ge(s_v, s_v.count)
                    r.sync.dma_start(out=stg, in_=dt[wname][kp, :, 128 * g:128 * g + 128]).then_inc(s_w, 16)
                    r.vector.wait_ge(s_w, s_w.count)
                    r.vector.tensor_copy(out=wslot.at(wi, 128 * kp, 128 * kp + 128), in_=stg)
                    mark(r.vector, s_v, 1)
            for tc in range(TCH):
                r.vector.tensor_copy(out=vaug.at(tc, 64, 65), in_=ones_t[:, 0:1])
                r.vector.tensor_copy(out=vaug.at(tc, 129, 130), in_=ones_t[:, 0:1])
            mark(r.vector, s_v, 1)
            r.tensor.wait_ge(s_v, s_v.count)
            for which, gbuf in ((0, gq), (1, gk)):
                for n in range(NQ):
                    ps = pb[n % 2]
                    for kp in range(DCH):
                        r.tensor.matmul(ps[:, :], wslot.at(which, 128 * kp, 128 * kp + 128),
                                        xln.at(0 + kp, 512 * n, 512 * n + 512),
                                        start=(kp == 0), stop=(kp == DCH - 1))
                    mark(r.tensor, s_pe, 1)
                    r.vector.wait_ge(s_pe, s_pe.count)
                    r.vector.tensor_copy(out=gbuf.at(0, 512 * n, 512 * n + 512), in_=ps[:, :])
                    mark(r.vector, s_v, 1)
                    r.tensor.wait_ge(s_v, s_v.count)
            for tc in range(TCH):
                ps = pb[2 + tc % 2]
                for kp in range(DCH):
                    r.tensor.matmul(ps[:, 0:128], xln.at(kp, 128 * tc, 128 * tc + 128),
                                    wslot.at(2, 128 * kp, 128 * kp + 128),
                                    start=(kp == 0), stop=(kp == DCH - 1))
                mark(r.tensor, s_pe, 1)
                r.vector.wait_ge(s_pe, s_pe.count)
                r.vector.tensor_copy(out=vaug.at(tc, 0, 64), in_=ps[:, 0:64])
                r.vector.tensor_copy(out=vaug.at(tc, 65, 129), in_=ps[:, 64:128])
                mark(r.vector, s_v, 1)
                r.tensor.wait_ge(s_v, s_v.count)
            for gi, gbuf in enumerate((gq, gk)):
                r.gpsimd.wait_ge(s_v, s_v.count)
                if g > 0 or gi > 0:
                    r.gpsimd.wait_ge(s_pe, s_pe.count)
                r.gpsimd.dma_start(out=shb.rows(0, 32, 0), in_=gbuf.rows(32, 64)).then_inc(s_g, 16)
                r.gpsimd.dma_start(out=shb.rows(64, 96, 0), in_=gbuf.rows(96, 128)).then_inc(s_g, 16)
                r.gpsimd.dma_start(out=shb.rows(32, 64, 0), in_=gbuf.rows(0, 32)).then_inc(s_g, 16)
                r.gpsimd.dma_start(out=shb.rows(96, 128, 0), in_=gbuf.rows(64, 96)).then_inc(s_g, 16)
                r.vector.wait_ge(s_g, s_g.count)
                r.vector.tensor_mul(gsc[0], gbuf[0], csb[0])
                r.vector.tensor_mul(shb[0], shb[0], csb[1])
                r.vector.tensor_tensor(out=gbuf[0], in0=gsc[0], in1=shb[0], op=ALU.add)
                mark(r.vector, s_v, 1)
                r.gpsimd.wait_ge(s_v, s_v.count)
            rope_tok = s_v.count
            r.tensor.wait_ge(s_v, rope_tok)
            if PHASE < 3:
                continue
            for hl in range(2):
                h = 2 * g + hl
                qh = gq.rows(64 * hl, 64 * hl + 64)
                kh = gk.rows(64 * hl, 64 * hl + 64)
                for qc in range(NQ):
                    q0 = 512 * qc
                    pout = pb[4 + (qc % 2)]
                    npan = 4 * qc + 4
                    for kpan in range(npan):
                        k0 = 128 * kpan
                        di = (k0 - q0) // 128
                        ps = pb[6 + (kpan % 2)]
                        r.tensor.matmul(ps[:, :], kh[:, k0:k0 + 128], qh[:, q0:q0 + 512],
                                        start=True, stop=True)
                        mark(r.tensor, s_pe, 1)
                        pt = pT[kpan % 2]
                        r.scalar.wait_ge(s_pe, s_pe.count)
                        r.scalar.activation(out=pt, in_=ps[:, :], func=AF.Exp, scale=0.125)
                        mark(r.scalar, s_sc, 1)
                        r.tensor.wait_ge(s_sc, s_sc.count)
                        if 0 <= di <= 3:
                            r.vector.wait_ge(s_sc, s_sc.count)
                            r.vector.tensor_mul(pt, pt, mskm[di])
                            mark(r.vector, s_v, 1)
                            r.tensor.wait_ge(s_v, s_v.count)
                        r.tensor.matmul(pout[0:65, :], vaug.at(kpan, 65 * hl, 65 * hl + 65),
                                        pt, start=(kpan == 0), stop=(kpan == npan - 1))
                        mark(r.tensor, s_pe, 1)
                        r.scalar.wait_ge(s_pe, s_pe.count)
                    r.vector.wait_ge(s_pe, s_pe.count)
                    r.vector.tensor_copy(out=u.rows(64 * hl, 64 * hl + 64, g, q0, q0 + 512), in_=pout[0:64, :])
                    r.vector.tensor_copy(out=dsb[64:65, 0:512], in_=pout[64:65, :])
                    mark(r.vector, s_v, 1)
                    r.tensor.wait_ge(s_v, s_v.count)
                    r.scalar.wait_ge(s_v, s_v.count)
                    r.scalar.dma_start(out=ds["den_dram"][h:h+1, q0:q0 + 512], in_=dsb[64:65, 0:512]).then_inc(s_s, 16)
                    r.vector.wait_ge(s_s, s_s.count)
        if PHASE < 3:
            return
        r.gpsimd.wait_ge(s_s, s_s.count)
        for n in range(NQ):
            c0 = 512 * n
            if n > 0:
                r.gpsimd.wait_ge(s_v, s_v.count)
            r.gpsimd.dma_start(out=dsb[0:16, 0:512], in_=ds["den_dram"][0:16, c0:c0 + 512]).then_inc(s_g, 16)
            r.vector.wait_ge(s_g, s_g.count)
            r.vector.reciprocal(out=dsb[0:16, 0:512], in_=dsb[0:16, 0:512]).then_inc(s_v, 1)
            r.vector.wait_ge(s_v, s_v.count)
            r.vector.engine_nop().then_inc(s_v, 1)
            r.gpsimd.wait_ge(s_v, s_v.count)
            r.gpsimd.dma_start(out=ds["den_dram"][0:16, c0:c0 + 512], in_=dsb[0:16, 0:512]).then_inc(s_g, 16)
        r.gpsimd.wait_ge(s_g, s_g.count)
        for p in range(DCH):
            r.gpsimd.wait_ge(s_v, s_v.count)
            r.gpsimd.dma_start(out=gsc.rows(0, 64, 0), in_=ds["den_dram"][2 * p:2 * p + 1, :].broadcast_to((64, T))).then_inc(s_g, 16)
            r.gpsimd.dma_start(out=gsc.rows(64, 128, 0), in_=ds["den_dram"][2 * p + 1:2 * p + 2, :].broadcast_to((64, T))).then_inc(s_g, 16)
            r.vector.wait_ge(s_g, s_g.count)
            r.vector.tensor_mul(u[p], u[p], gsc[0])
            mark(r.vector, s_v, 1)
        r.vector.wait_ge(s_pe, s_pe.count)
        for kp in range(DCH):
            for half in range(2):
                stg = stg32(0)
                r.sync.wait_ge(s_v, s_v.count)
                r.sync.dma_start(out=stg, in_=dt["wo0"][kp, :, 512 * half:512 * half + 512]).then_inc(s_w, 16)
                r.vector.wait_ge(s_w, s_w.count)
                r.vector.tensor_copy(out=wof.at(kp, 512 * half, 512 * half + 512), in_=stg)
                mark(r.vector, s_v, 1)
        r.tensor.wait_ge(s_v, s_v.count)
        for po in range(DCH):
            for n in range(NQ):
                ps = pb[(po * NQ + n) % 2]
                for kp in range(DCH):
                    r.tensor.matmul(ps[:, :], wof.at(kp, 128 * po, 128 * po + 128),
                                    u.at(kp, 512 * n, 512 * n + 512),
                                    start=(kp == 0), stop=(kp == DCH - 1))
                mark(r.tensor, s_pe, 1)
                r.scalar.wait_ge(s_pe, s_pe.count)
                tmp = gsc.rows(0, P, 0, 512 * (n % 4), 512 * (n % 4) + 512)
                r.scalar.activation(out=tmp, in_=ps[:, :], func=AF.Identity, bias=bcol("wo_b", po))
                mark(r.scalar, s_sc, 1)
                r.vector.wait_ge(s_sc, s_sc.count)
                r.vector.tensor_tensor(out=xr[:, po, 512 * n:512 * n + 512],
                                       in0=xr[:, po, 512 * n:512 * n + 512], in1=tmp, op=ALU.add)
                mark(r.vector, s_v, 1)
                r.tensor.wait_ge(s_v, s_v.count)
                r.scalar.wait_ge(s_v, s_v.count)


    def emit_l0f():
        AB.reset(AB_BASE)
        W = AB.take(3 * D, 2 * DCH)
        xsl = AB.take(T, 2)
        cT = Buf(xsl.t, xsl.off_b, 256, DCH, F32)
        stg = AB.take(512, 1, F32)
        xg = Buf(AB.t, 0, 2048, 2)
        cTbf = Buf(AB.t, 4096, 256, DCH)
        sq = Buf(AB.t, 0, T, 2)
        gates = Buf(gsc.t, gsc.off_b, 256, 3, F32)
        uT = Buf(W.t, W.off_b, T, DCH)
        wov = Buf(W.t, W.off_b + 16384, D, DCH)
        xre = Buf(W.t, W.off_b + 16384 + 8192, T, DCH)
        uov = Buf(W.t, W.off_b + 16384 + 8192, D, DCH)

        def load_w(dst_buf, slot, dram, kp, cols):
            for cc0 in range(0, cols, 512):
                r.sync.wait_ge(s_v, s_v.count)
                r.sync.dma_start(out=stg[0], in_=dram[kp, :, cc0:cc0 + 512]).then_inc(s_w, 16)
                r.vector.wait_ge(s_w, s_w.count)
                r.vector.tensor_copy(out=dst_buf.at(slot, cc0, cc0 + 512), in_=stg[0])
                mark(r.vector, s_v, 1)

        a2a_src = ds["cc_in"]
        def consume(p, slot):
            r.gpsimd.wait_ge(s_v, s_v.count)
            for j in range(NC):
                r.gpsimd.dma_start(out=a2a_src[j, p, :, :],
                                   in_=xsl.at(slot, 256 * j, 256 * j + 256)).then_inc(s_g, 16)
            r.gpsimd.dma_start(out=ds["xlnd"][p, :, :], in_=xsl.at(slot, 0, T)).then_inc(s_g, 16)
            r.vector.wait_ge(s_g, s_g.count)
        emit_ln(xsl, sq, consume=consume, nslots=2)
        r.gpsimd.collective_compute(
            "AllToAll", ALU.bypass, replica_groups=[list(range(NC))],
            ins=[a2a_src.tensor.ap().opt()], outs=[ds["cc_out"].tensor.ap().opt()],
        ).then_inc(s_cc, 1)
        r.sync.wait_ge(s_v, s_v.count)
        for p in range(DCH):
            r.sync.dma_start(out=cT[p], in_=dt["incb"][p, :, :]).then_inc(s_w, 16)
        c0_tok = s_w.count
        for kp in range(DCH):
            load_w(W, kp, dt["wfic"], kp, 3 * D)
        for kp in range(DCH):
            load_w(W, DCH + kp, dt["ufic"], kp, 3 * D)
        wtok = s_v.count
        r.vector.wait_ge(s_w, c0_tok)
        for p in range(DCH):
            r.vector.tensor_copy(out=cTbf[p], in_=cT[p])
        mark(r.vector, s_v, 1)
        r.gpsimd.wait_ge(s_cc, s_cc.count)
        GB = [0, 8, 16]
        for t in range(NC):
            r.gpsimd.wait_ge(s_pe, s_pe.count)
            for kp in range(DCH):
                r.gpsimd.dma_start(out=xg.at(t % 2, 256 * kp, 256 * kp + 256),
                                   in_=ds["cc_out"][t, kp, :, :]).then_inc(s_g, 16)
            xg_tok = s_g.count
            r.tensor.wait_ge(s_g, xg_tok)
            if t == 0:
                r.tensor.wait_ge(s_v, wtok)
                r.tensor.wait_ge(s_v, s_v.count)
            for p in range(DCH):
                for gi in range(3):
                    ps = pb[gi]
                    gcol = 1024 * gi + 128 * p
                    for kp in range(DCH):
                        r.tensor.matmul(ps[:, 0:256], W.at(kp, gcol, gcol + 128),
                                        xg.at(t % 2, 256 * kp, 256 * kp + 256),
                                        start=(kp == 0), stop=False)
                    for kp in range(DCH):
                        r.tensor.matmul(ps[:, 0:256], W.at(DCH + kp, gcol, gcol + 128),
                                        cTbf[kp], start=False, stop=(kp == DCH - 1))
                mark(r.tensor, s_pe, 1)
                gtok = s_pe.count
                r.scalar.wait_ge(s_pe, gtok)
                if True:
                    r.scalar.wait_ge(s_v, s_v.count)
                r.scalar.activation(out=gates[0], in_=pb[0][:, 0:256], func=AF.Sigmoid, bias=bcol("bfic", p))
                r.scalar.activation(out=gates[1], in_=pb[1][:, 0:256], func=AF.Sigmoid, bias=bcol("bfic", 8 + p))
                r.scalar.activation(out=gates[2], in_=pb[2][:, 0:256], func=AF.Tanh, bias=bcol("bfic", 16 + p))
                mark(r.scalar, s_sc, 1)
                atok = s_sc.count
                r.tensor.wait_ge(s_sc, atok)
                r.vector.wait_ge(s_sc, atok)
                r.vector.tensor_mul(cT[p], gates[0], cT[p])
                r.vector.tensor_mul(gates[1], gates[1], gates[2])
                r.vector.tensor_tensor(out=cT[p], in0=cT[p], in1=gates[1], op=ALU.add)
                mark(r.vector, s_v, 1)
            if t < NC - 1:
                r.vector.tensor_copy(out=cTbf.at(0, 0, 0 + 2048) if False else cTbf[0], in_=cT[0])
                for p in range(1, DCH):
                    r.vector.tensor_copy(out=cTbf[p], in_=cT[p])
                mark(r.vector, s_v, 1)
                r.tensor.wait_ge(s_v, s_v.count)
        r.vector.tensor_copy(out=cTbf[0], in_=cT[0])
        for p in range(1, DCH):
            r.vector.tensor_copy(out=cTbf[p], in_=cT[p])
        mark(r.vector, s_v, 1)
        cf_tok = s_v.count
        r.sync.wait_ge(s_pe, s_pe.count)
        for kp in range(DCH):
            load_w(uov, kp, dt["uovT"], kp, D)
        uloc = Buf(AB.t, 0, 256, DCH)
        r.tensor.wait_ge(s_v, s_v.count)
        r.tensor.wait_ge(s_v, cf_tok)
        for po in range(DCH):
            ps = pb[po % 2]
            for kp in range(DCH):
                r.tensor.matmul(ps[:, 0:256], uov.at(kp, 128 * po, 128 * po + 128),
                                cTbf[kp], start=(kp == 0), stop=(kp == DCH - 1))
            mark(r.tensor, s_pe, 1)
            r.vector.wait_ge(s_pe, s_pe.count)
            r.vector.tensor_copy(out=uloc[po], in_=ps[:, 0:256])
            mark(r.vector, s_v, 1)
            r.tensor.wait_ge(s_v, s_v.count)
        r.gpsimd.wait_ge(s_v, s_v.count)
        for kp in range(DCH):
            r.gpsimd.dma_start(out=ds["cc2_in"][kp, :, :], in_=uloc[kp]).then_inc(s_g, 16)
        r.gpsimd.wait_ge(s_g, s_g.count)
        r.gpsimd.collective_compute(
            "AllGather", ALU.bypass, replica_groups=[list(range(NC))],
            ins=[ds["cc2_in"].tensor.ap().opt()], outs=[ds["cc2_out"].tensor.ap().opt()],
        ).then_inc(s_cc, 1)
        for kp in range(DCH):
            load_w(wov, kp, dt["wovT"], kp, D)
        r.sync.wait_ge(s_v, s_v.count)
        for p in range(DCH):
            r.sync.dma_start(out=xre[p], in_=ds["xlnd"][p, :, :]).then_inc(s_w, 16)
        xre_tok = s_w.count
        r.gpsimd.wait_ge(s_cc, s_cc.count)
        for rr in range(NC):
            for po in range(DCH):
                r.gpsimd.dma_start(out=uT.at(po, 256 * rr, 256 * rr + 256),
                                   in_=ds["cc2_out"][rr * DCH + po, :, :]).then_inc(s_g, 16)
        ut_tok = s_g.count
        r.tensor.wait_ge(s_v, s_v.count)
        r.tensor.wait_ge(s_w, xre_tok)
        r.tensor.wait_ge(s_g, ut_tok)
        for po in range(DCH):
            for n in range(NQ):
                ps = pb[2 + (po * NQ + n) % 2]
                for kp in range(DCH):
                    r.tensor.matmul(ps[:, :], wov.at(kp, 128 * po, 128 * po + 128),
                                    xre.at(kp, 512 * n, 512 * n + 512),
                                    start=(kp == 0), stop=(kp == DCH - 1))
                mark(r.tensor, s_pe, 1)
                ptok = s_pe.count
                r.vector.wait_ge(s_pe, ptok)
                r.vector.tensor_tensor(out=ps[:, :], in0=ps[:, :],
                                       in1=uT.at(po, 512 * n, 512 * n + 512), op=ALU.add)
                mark(r.vector, s_v, 1)
                r.scalar.wait_ge(s_v, s_v.count)
                tmp = gsc.rows(0, P, 0, 512 * (n % 4), 512 * (n % 4) + 512)
                r.scalar.activation(out=tmp, in_=ps[:, :], func=AF.Tanh, bias=bcol("bo_f", po))
                mark(r.scalar, s_sc, 1)
                r.vector.wait_ge(s_sc, s_sc.count)
                r.vector.tensor_tensor(out=xr[:, po, 512 * n:512 * n + 512],
                                       in0=xr[:, po, 512 * n:512 * n + 512], in1=tmp, op=ALU.add)
                mark(r.vector, s_v, 1)
                r.tensor.wait_ge(s_v, s_v.count)
                r.scalar.wait_ge(s_v, s_v.count)


    def emit_l0b():
        AB.reset(AB_BASE)
        xln = AB.take(T, DCH)
        qb = AB.take(T, DCH)
        kb = AB.take(T, DCH)
        wsl = AB.take(1024, 2)
        pT = AB.take(512, 1)
        msk = AB.take(512, 4)
        bvl = AB.take(1024, 1)
        stg = AB.take(256, 2, F32)
        xti = AB.take(1024, 1)
        vb = Buf(xln.t, xln.off_b, 1024, TCH)

        r.sync.wait_ge(s_v, s_v.count)
        for di in range(4):
            r.sync.dma_start(out=msk[di], in_=dt["maskmul"][:, di, :]).then_inc(s_w, 16)
        r.sync.dma_start(out=bvl.rows(0, 1, 0), in_=dt["bvrow"][0:1, 0:D]).then_inc(s_w, 16)
        mtok = s_w.count
        addm = lambda di: gsc.rows(0, P, 0, 512 * di, 512 * di + 512)
        r.vector.wait_ge(s_w, mtok)

        def consume(p, slot):
            r.gpsimd.wait_ge(s_v, s_v.count)
            r.gpsimd.dma_start(out=ds["xlnd"][p, :, :], in_=xln[p]).then_inc(s_g, 16)
        ln_tok = emit_ln(xln, Buf(qb.t, qb.off_b, T, 2), consume=consume)
        r.tensor.wait_ge(s_v, ln_tok)
        r.sync.wait_ge(s_v, ln_tok)
        for di in range(4):
            r.vector.tensor_scalar(out=addm(di), in0=msk[di], scalar1=-1.0, scalar2=1e30,
                                   op0=ALU.add, op1=ALU.mult)
        mark(r.vector, s_v, 1)
        for which, (gbuf, wname, bname) in enumerate(((qb, "wqb", "bq"), (kb, "wkb", "bk"))):
            for po in range(DCH):
                sl = po % 2
                for kp in range(DCH):
                    r.sync.wait_ge(s_v, s_v.count)
                    r.sync.dma_start(out=stg.at(kp % 2, 0, 128), in_=dt[wname][kp, :, 128 * po:128 * po + 128]).then_inc(s_w, 16)
                    r.vector.wait_ge(s_w, s_w.count)
                    r.vector.tensor_copy(out=wsl.at(sl, 128 * kp, 128 * kp + 128), in_=stg.at(kp % 2, 0, 128))
                    mark(r.vector, s_v, 1)
                wtok = s_v.count
                r.tensor.wait_ge(s_v, wtok)
                for n in range(NQ):
                    ps = pb[n % 2]
                    for kp in range(DCH):
                        r.tensor.matmul(ps[:, :], wsl.at(sl, 128 * kp, 128 * kp + 128),
                                        xln.at(kp, 512 * n, 512 * n + 512),
                                        start=(kp == 0), stop=(kp == DCH - 1))
                    mark(r.tensor, s_pe, 1)
                    r.scalar.wait_ge(s_pe, s_pe.count)
                    r.scalar.activation(out=gbuf.at(po, 512 * n, 512 * n + 512), in_=ps[:, :],
                                        func=AF.Identity, bias=bcol(bname, po))
                    mark(r.scalar, s_sc, 1)
                    r.tensor.wait_ge(s_sc, s_sc.count)
        dtmp = dsb[:, 20:21]
        for jp in range(TCH):
            j0 = 128 * jp
            nic = NQ - j0 // 512
            for ici in range(nic):
                i0 = T - 512 * (ici + 1) if False else 512 * (NQ - nic + ici)
                ps = pb[4 + ici % 2]
                for kp in range(DCH):
                    r.tensor.matmul(ps[:, :], kb.at(kp, j0, j0 + 128), qb.at(kp, i0, i0 + 512),
                                    start=(kp == 0), stop=(kp == DCH - 1))
                mark(r.tensor, s_pe, 1)
                di = (j0 - i0) // 128
                if 0 <= di <= 3:
                    r.vector.wait_ge(s_pe, s_pe.count)
                    r.vector.tensor_tensor(out=ps[:, :], in0=ps[:, :], in1=addm(di), op=ALU.add)
                    mark(r.vector, s_v, 1)
                    r.scalar.wait_ge(s_v, s_v.count)
                r.scalar.wait_ge(s_pe, s_pe.count)
                acc = dsb[:, jp:jp + 1] if ici == 0 else dtmp
                r.scalar.activation(out=pT[0], in_=ps[:, :], func=AF.Exp,
                                    scale=0.03125, accum_out=acc)
                mark(r.scalar, s_sc, 1)
                r.tensor.wait_ge(s_sc, s_sc.count)
                if ici > 0:
                    r.vector.wait_ge(s_sc, s_sc.count)
                    r.vector.tensor_add(dsb[:, jp:jp + 1], dsb[:, jp:jp + 1], dtmp)
                    mark(r.vector, s_v, 1)
                    r.scalar.wait_ge(s_v, s_v.count)
        r.scalar.wait_ge(s_v, s_v.count)
        r.scalar.activation(out=dsb[:, 0:16], in_=dsb[:, 0:16], func=AF.Ln)
        mark(r.scalar, s_sc, 1)
        r.vector.wait_ge(s_sc, s_sc.count)
        r.vector.tensor_scalar(out=dsb[:, 0:16], in0=dsb[:, 0:16], scalar1=-1.0, scalar2=None, op0=ALU.mult)
        mark(r.vector, s_v, 1)
        for tc in range(TCH):
            sl = 0
            r.sync.wait_ge(s_pe, s_pe.count)
            for kp in range(DCH):
                r.sync.dma_start(out=xti.at(sl, 128 * kp, 128 * kp + 128),
                                 in_=ds["xlnd"][kp, :, 128 * tc:128 * tc + 128]).then_inc(s_w, 16)
            xt_tok = s_w.count
            for n in range(2):
                ps = pb[n % 2]
                for kp in range(DCH):
                    r.sync.wait_ge(s_v, s_v.count)
                    r.sync.dma_start(out=stg[0], in_=dt["wvb"][kp, :, 512 * n:512 * n + 256]).then_inc(s_w, 16)
                    r.sync.dma_start(out=stg[1], in_=dt["wvb"][kp, :, 512 * n + 256:512 * n + 512]).then_inc(s_w, 16)
                    r.vector.wait_ge(s_w, s_w.count)
                    r.vector.wait_ge(s_pe, s_pe.count)
                    r.vector.tensor_copy(out=wsl.at(0, 0, 256), in_=stg[0])
                    r.vector.tensor_copy(out=wsl.at(0, 256, 512), in_=stg[1])
                    mark(r.vector, s_v, 1)
                    r.tensor.wait_ge(s_v, s_v.count)
                    r.tensor.wait_ge(s_w, xt_tok)
                    r.tensor.matmul(ps[:, :], xti.at(sl, 128 * kp, 128 * kp + 128),
                                    wsl.at(0, 0, 512), start=(kp == 0), stop=False)
                    mark(r.tensor, s_pe, 1)
                r.tensor.matmul(ps[:, :], ones_r[0:1, 0:128], bvl.rows(0, 1, 0, 512 * n, 512 * n + 512),
                                start=False, stop=True)
                mark(r.tensor, s_pe, 1)
                r.vector.wait_ge(s_pe, s_pe.count)
                r.vector.tensor_copy(out=vb.at(tc, 512 * n, 512 * n + 512), in_=ps[:, :])
                mark(r.vector, s_v, 1)
                r.tensor.wait_ge(s_v, s_v.count)
        for ici in range(NQ):
            i0 = 512 * ici
            npan = 4 * ici + 4
            for oh in range(2):
                for jp in range(npan):
                    j0 = 128 * jp
                    ps = pb[4 + jp % 2]
                    for kp in range(DCH):
                        r.tensor.matmul(ps[:, :], kb.at(kp, j0, j0 + 128), qb.at(kp, i0, i0 + 512),
                                        start=(kp == 0), stop=(kp == DCH - 1))
                    mark(r.tensor, s_pe, 1)
                    di = (j0 - i0) // 128
                    if 0 <= di <= 3:
                        r.vector.wait_ge(s_pe, s_pe.count)
                        r.vector.tensor_tensor(out=ps[:, :], in0=ps[:, :], in1=addm(di), op=ALU.add)
                        mark(r.vector, s_v, 1)
                        r.scalar.wait_ge(s_v, s_v.count)
                    r.scalar.wait_ge(s_pe, s_pe.count)
                    r.scalar.activation(out=pT[0], in_=ps[:, :], func=AF.Exp,
                                        scale=0.03125, bias=dsb[:, jp:jp + 1])
                    mark(r.scalar, s_sc, 1)
                    r.tensor.wait_ge(s_sc, s_sc.count)
                    for oc in range(4):
                        po = 4 * oh + oc
                        r.tensor.matmul(pb[oc][:, :], vb.at(jp, 128 * po, 128 * po + 128),
                                        pT[0], start=(jp == 0), stop=(jp == npan - 1))
                    mark(r.tensor, s_pe, 1)
                    r.scalar.wait_ge(s_pe, s_pe.count)
                for oc in range(4):
                    po = 4 * oh + oc
                    r.vector.wait_ge(s_pe, s_pe.count)
                    r.vector.tensor_tensor(out=xr[:, po, i0:i0 + 512],
                                           in0=xr[:, po, i0:i0 + 512], in1=pb[oc][:, :], op=ALU.add)
                mark(r.vector, s_v, 1)
                r.tensor.wait_ge(s_v, s_v.count)


    def emit_l1a():
        AB.reset(AB_BASE)
        xln = AB.take(T, DCH)
        y = AB.take(T, DCH)
        qp = AB.take(T, 1); kp_ = AB.take(T, 1); vp = AB.take(T, 1)
        t1 = AB.take(T, 1); t2 = AB.take(T, 1)
        kvc = AB.take(T, 1); kc = AB.take(T, 1)
        wsl = AB.take(1024, 3)
        stg = AB.take(512, 2, F32)
        wpr = Buf(xln.t, xln.off_b, D, DCH)
        den2 = ab

        def consume(p, slot):
            pass
        ln_tok = emit_ln(xln, Buf(y.t, y.off_b, T, 2))
        r.tensor.wait_ge(s_v, ln_tok)
        r.sync.wait_ge(s_v, ln_tok)
        for p in range(DCH):
            r.vector.wait_ge(s_pe, s_pe.count)
            for wi in range(3):
                c0 = 1024 * wi + 128 * p
                for kx in range(DCH):
                    r.sync.wait_ge(s_v, s_v.count)
                    r.sync.dma_start(out=stg.at(kx % 2, 0, 128), in_=dt["wqkv"][kx, :, c0:c0 + 128]).then_inc(s_w, 16)
                    r.vector.wait_ge(s_w, s_w.count)
                    r.vector.tensor_copy(out=wsl.at(wi, 128 * kx, 128 * kx + 128), in_=stg.at(kx % 2, 0, 128))
                    mark(r.vector, s_v, 1)
            wtok = s_v.count
            r.tensor.wait_ge(s_v, wtok)
            for wi, (dst, bj) in enumerate(((qp, p), (kp_, 8 + p), (vp, 16 + p))):
                for n in range(NQ):
                    ps = pb[n % 2]
                    for kx in range(DCH):
                        r.tensor.matmul(ps[:, :], wsl.at(wi, 128 * kx, 128 * kx + 128),
                                        xln.at(kx, 512 * n, 512 * n + 512),
                                        start=(kx == 0), stop=(kx == DCH - 1))
                    mark(r.tensor, s_pe, 1)
                    r.scalar.wait_ge(s_pe, s_pe.count)
                    r.scalar.activation(out=dst.at(0, 512 * n, 512 * n + 512), in_=ps[:, :],
                                        func=AF.Identity, bias=bcol("bqkv", bj))
                    mark(r.scalar, s_sc, 1)
                    r.tensor.wait_ge(s_sc, s_sc.count)
            for buf in (qp, kp_):
                r.vector.wait_ge(s_sc, s_sc.count)
                r.vector.tensor_scalar(out=gsc[0], in0=buf[0], scalar1=sclb[:, :], scalar2=0.0,
                                       op0=ALU.mult, op1=ALU.min)
                mark(r.vector, s_v, 1)
                r.scalar.wait_ge(s_v, s_v.count)
                r.scalar.activation(out=t1[0], in_=gsc[0], func=AF.Exp)
                r.scalar.activation(out=t2[0], in_=buf[0], func=AF.Relu, scale=sclb[:, :])
                mark(r.scalar, s_sc, 1)
                r.vector.wait_ge(s_sc, s_sc.count)
                r.vector.tensor_add(buf[0], t1[0], t2[0])
                mark(r.vector, s_v, 1)
            r.vector.tensor_mul(t1[0], kp_[0], vp[0])
            r.vector.tensor_tensor_scan(out=kvc[0], data0=ones_t[:, :], data1=t1[0],
                                        initial=0.0, op0=ALU.mult, op1=ALU.add).then_inc(s_v, 1)
            r.vector.wait_ge(s_v, s_v.count)
            r.vector.tensor_tensor_scan(out=kc[0], data0=ones_t[:, :], data1=kp_[0],
                                        initial=0.0, op0=ALU.mult, op1=ALU.add).then_inc(s_v, 1)
            r.vector.wait_ge(s_v, s_v.count)
            r.vector.tensor_mul(t1[0], qp[0], kc[0])
            mark(r.vector, s_v, 1)
            r.tensor.wait_ge(s_v, s_v.count)
            for n in range(NQ):
                r.tensor.matmul(pb[4][0:2, :], hsum[:, :], t1.at(0, 512 * n, 512 * n + 512),
                                start=True, stop=True)
                mark(r.tensor, s_pe, 1)
                r.vector.wait_ge(s_pe, s_pe.count)
                r.vector.tensor_copy(out=den2.rows(0, 2, 0, 512 * n, 512 * n + 512), in_=pb[4][0:2, :])
                mark(r.vector, s_v, 1)
                r.tensor.wait_ge(s_v, s_v.count)
            r.vector.tensor_scalar(out=den2.rows(0, 2, 0), in0=den2.rows(0, 2, 0),
                                   scalar1=0.001, scalar2=None, op0=ALU.max)
            r.vector.reciprocal(out=den2.rows(0, 2, 0), in_=den2.rows(0, 2, 0)).then_inc(s_v, 1)
            r.vector.wait_ge(s_v, s_v.count)
            r.vector.engine_nop().then_inc(s_v, 1)
            r.gpsimd.wait_ge(s_v, s_v.count)
            r.gpsimd.dma_start(out=ds["den_dram"][0:2, :], in_=den2.rows(0, 2, 0)).then_inc(s_g, 16)
            r.gpsimd.wait_ge(s_g, s_g.count)
            r.gpsimd.dma_start(out=gsc.rows(0, 64, 0), in_=ds["den_dram"][0:1, :].broadcast_to((64, T))).then_inc(s_g, 16)
            r.gpsimd.dma_start(out=gsc.rows(64, 128, 0), in_=ds["den_dram"][1:2, :].broadcast_to((64, T))).then_inc(s_g, 16)
            r.vector.tensor_mul(t2[0], qp[0], kvc[0])
            r.vector.wait_ge(s_g, s_g.count)
            r.vector.tensor_mul(y[p], t2[0], gsc[0])
            mark(r.vector, s_v, 1)
            r.gpsimd.wait_ge(s_v, s_v.count)
        for kx in range(DCH):
            for half in range(2):
                r.sync.wait_ge(s_v, s_v.count)
                r.sync.dma_start(out=stg[0], in_=dt["wprj"][kx, :, 512 * half:512 * half + 512]).then_inc(s_w, 16)
                r.vector.wait_ge(s_w, s_w.count)
                r.vector.tensor_copy(out=wpr.at(kx, 512 * half, 512 * half + 512), in_=stg[0])
                mark(r.vector, s_v, 1)
        r.tensor.wait_ge(s_v, s_v.count)
        for po in range(DCH):
            for n in range(NQ):
                ps = pb[(po * NQ + n) % 2]
                for kx in range(DCH):
                    r.tensor.matmul(ps[:, :], wpr.at(kx, 128 * po, 128 * po + 128),
                                    y.at(kx, 512 * n, 512 * n + 512),
                                    start=(kx == 0), stop=(kx == DCH - 1))
                mark(r.tensor, s_pe, 1)
                r.scalar.wait_ge(s_pe, s_pe.count)
                tmp = gsc.rows(0, P, 0, 512 * (n % 4), 512 * (n % 4) + 512)
                r.scalar.activation(out=tmp, in_=ps[:, :], func=AF.Identity, bias=bcol("bproj", po))
                mark(r.scalar, s_sc, 1)
                r.vector.wait_ge(s_sc, s_sc.count)
                r.vector.tensor_tensor(out=xr[:, po, 512 * n:512 * n + 512],
                                       in0=xr[:, po, 512 * n:512 * n + 512], in1=tmp, op=ALU.add)
                mark(r.vector, s_v, 1)
                r.tensor.wait_ge(s_v, s_v.count)
                r.scalar.wait_ge(s_v, s_v.count)


    def emit_l1f(first_call):
        AB.reset(AB_BASE)
        xln = AB.take(T, DCH)
        h = AB.take(512, 32)
        wqsl = AB.take(1024, 2)
        wpsl = AB.take(1024, 2)
        mid = AB.take(T, 1)
        midp = AB.take(512, 1)
        bfc = AB.take(4096, 1)
        bpr = AB.take(1024, 1)
        afc = AB.take(16 * DCH, 1)
        apr = AB.take(16 * 32, 1)
        stg = AB.take(512, 2, F32)
        qst = Buf(h.t, h.off_b, 4096, 1, F32)
        qout = Buf(h.t, h.off_b + 8192, 4096, 1)
        qi32 = Buf(h.t, h.off_b + 12288, 2048, 1, F32)

        def consume(p, slot):
            pass
        ln_tok = emit_ln(xln, Buf(h.t, h.off_b, T, 2))
        r.sync.wait_ge(s_v, ln_tok)
        r.tensor.wait_ge(s_v, ln_tok)

        for piece in range(8):
            r.sync.wait_ge(s_v, s_v.count)
            r.sync.dma_start(out=stg.rows(0, 16, 0), in_=dt["bfcT"][0, :, 512 * piece:512 * piece + 512]).then_inc(s_w, 16)
            r.vector.wait_ge(s_w, s_w.count)
            r.vector.tensor_copy(out=bfc.rows(0, 16, 0, 512 * piece, 512 * piece + 512), in_=stg.rows(0, 16, 0))
            mark(r.vector, s_v, 1)
        for piece in range(2):
            r.sync.wait_ge(s_v, s_v.count)
            r.sync.dma_start(out=stg.rows(0, 16, 0), in_=dt["bprT"][0, :, 512 * piece:512 * piece + 512]).then_inc(s_w, 16)
            r.vector.wait_ge(s_w, s_w.count)
            r.vector.tensor_copy(out=bpr.rows(0, 16, 0, 512 * piece, 512 * piece + 512), in_=stg.rows(0, 16, 0))
            mark(r.vector, s_v, 1)
        for kx in range(DCH):
            r.sync.wait_ge(s_v, s_v.count)
            r.sync.dma_start(out=stg.at(0, 0, 16), in_=dt["afcT"][kx, :, :]).then_inc(s_w, 16)
            r.vector.wait_ge(s_w, s_w.count)
            r.vector.tensor_copy(out=afc.at(0, 16 * kx, 16 * kx + 16), in_=stg.at(0, 0, 16))
            mark(r.vector, s_v, 1)
        for fk in range(32):
            r.sync.wait_ge(s_v, s_v.count)
            r.sync.dma_start(out=stg.at(0, 0, 16), in_=dt["aprT"][fk, :, :]).then_inc(s_w, 16)
            r.vector.wait_ge(s_w, s_w.count)
            r.vector.tensor_copy(out=apr.at(0, 16 * fk, 16 * fk + 16), in_=stg.at(0, 0, 16))
            mark(r.vector, s_v, 1)

        if first_call:
            for wname, nch, cols, dst, oc_layout in (("wfcT", DCH, 4096, "wqfc_dram", True),
                                                     ("wprT", 32, D, "wqpr_dram", False)):
                for kx in range(nch):
                    r.gpsimd.wait_ge(s_v, s_v.count)
                    r.gpsimd.dma_start(out=qst.at(0, 0, cols), in_=dt[wname][kx, :, :]).then_inc(s_g, 16)
                    r.vector.wait_ge(s_g, s_g.count)
                    r.vector.reduce_max(out=dsb[:, 24 + kx % 2:25 + kx % 2], in_=qst.at(0, 0, cols),
                                        axis=mybir.AxisListType.X, apply_absolute_value=True).then_inc(s_v, 1)
                    r.vector.wait_ge(s_v, s_v.count)
                    if kx == 0:
                        r.vector.tensor_copy(out=dsb[:, 26:27], in_=dsb[:, 24:25])
                    else:
                        r.vector.tensor_max(dsb[:, 26:27], dsb[:, 26:27], dsb[:, 24 + kx % 2:25 + kx % 2])
                    mark(r.vector, s_v, 1)
                    r.gpsimd.wait_ge(s_v, s_v.count)
                r.gpsimd.tensor_reduce(out=dsb[0:1, 27:28], in_=dsb[:, 26:27],
                                       axis=mybir.AxisListType.C, op=ALU.max).then_inc(s_gc, 1)
                r.vector.wait_ge(s_gc, s_gc.count)
                r.vector.reciprocal(out=dsb[0:1, 28:29], in_=dsb[0:1, 27:28]).then_inc(s_v, 1)
                r.vector.wait_ge(s_v, s_v.count)
                r.vector.tensor_scalar(out=dsb[0:1, 28:29], in0=dsb[0:1, 28:29], scalar1=127.0, scalar2=None, op0=ALU.mult)
                r.vector.tensor_scalar(out=dsb[0:1, 29:30], in0=dsb[0:1, 27:28], scalar1=1.0 / 127.0, scalar2=None, op0=ALU.mult)
                mark(r.vector, s_v, 1)
                r.gpsimd.wait_ge(s_v, s_v.count)
                r.gpsimd.dma_start(out=ds["den_dram"][0:1, 0:2], in_=dsb[0:1, 28:30]).then_inc(s_g, 16)
                r.gpsimd.wait_ge(s_g, s_g.count)
                r.gpsimd.dma_start(out=dsb[:, 30:31], in_=ds["den_dram"][0:1, 0:1].broadcast_to((P, 1))).then_inc(s_g, 16)
                r.gpsimd.dma_start(out=dsb[:, 31:32], in_=ds["den_dram"][0:1, 1:2].broadcast_to((P, 1))).then_inc(s_g, 16)
                r.vector.wait_ge(s_g, s_g.count)
                for kx in range(nch):
                    r.gpsimd.wait_ge(s_v, s_v.count)
                    r.gpsimd.dma_start(out=qst.at(0, 0, cols), in_=dt[wname][kx, :, :]).then_inc(s_g, 16)
                    r.vector.wait_ge(s_g, s_g.count)
                    for pc in range(0, cols, 1024):
                        y32 = qst.at(0, pc, pc + 1024)
                        qi = qi32.at(0, 0, 1024).bitcast(I32) if False else Buf(qi32.t, qi32.off_b, 1024, 1, F32).at(0, 0, 1024).bitcast(I32)
                        r.vector.tensor_scalar(out=y32, in0=y32, scalar1=dsb[:, 30:31], scalar2=None, op0=ALU.mult)
                        r.vector.tensor_copy(out=qi, in_=y32)
                        r.vector.tensor_scalar(out=qout.at(0, pc, pc + 1024), in0=qi,
                                               scalar1=-128.0, scalar2=127.0, op0=ALU.max, op1=ALU.min)
                        r.vector.tensor_scalar(out=qout.at(0, pc, pc + 1024), in0=qout.at(0, pc, pc + 1024),
                                               scalar1=dsb[:, 31:32], scalar2=None, op0=ALU.mult)
                    mark(r.vector, s_v, 1)
                    r.gpsimd.wait_ge(s_v, s_v.count)
                    if oc_layout:
                        for oc in range(32):
                            r.gpsimd.dma_start(out=ds[dst][oc, :, 128 * kx:128 * kx + 128],
                                               in_=qout.at(0, 128 * oc, 128 * oc + 128)).then_inc(s_g, 16)
                    else:
                        r.gpsimd.dma_start(out=ds[dst][kx, :, :], in_=qout.at(0, 0, D)).then_inc(s_g, 16)
                    r.vector.wait_ge(s_g, s_g.count)

        r.tensor.wait_ge(s_v, s_v.count)
        for n in range(NQ):
            ps = pb[2]
            for kx in range(DCH):
                r.tensor.matmul(ps[0:16, :], afc.at(0, 16 * kx, 16 * kx + 16),
                                xln.at(kx, 512 * n, 512 * n + 512),
                                start=(kx == 0), stop=(kx == DCH - 1))
            mark(r.tensor, s_pe, 1)
            r.scalar.wait_ge(s_pe, s_pe.count)
            r.scalar.activation(out=mid.rows(0, 16, 0, 512 * n, 512 * n + 512), in_=ps[0:16, :],
                                func=AF.Identity, scale=2.0)
            mark(r.scalar, s_sc, 1)
            r.tensor.wait_ge(s_sc, s_sc.count)
        for tc in range(NQ):
            t0 = 512 * tc
            for oc in range(32):
                r.sync.wait_ge(s_pe, s_pe.count)
                r.sync.dma_start(out=wqsl[oc % 2], in_=ds["wqfc_dram"][oc, :, :]).then_inc(s_w, 16)
                r.tensor.wait_ge(s_w, s_w.count)
                ps = pb[oc % 2]
                if tc > 0 or oc > 1:
                    r.tensor.wait_ge(s_sc, s_sc.count)
                for kx in range(DCH):
                    r.tensor.matmul(ps[:, :], wqsl.at(oc % 2, 128 * kx, 128 * kx + 128),
                                    xln.at(kx, t0, t0 + 512), start=(kx == 0), stop=False)
                r.tensor.matmul(ps[:, :], bfc.rows(0, 16, 0, 128 * oc, 128 * oc + 128),
                                mid.rows(0, 16, 0, t0, t0 + 512), start=False, stop=True)
                mark(r.tensor, s_pe, 1)
                r.scalar.wait_ge(s_pe, s_pe.count)
                r.scalar.activation(out=h[oc], in_=ps[:, :], func=AF.Gelu, bias=bcol("b_fc", oc))
                mark(r.scalar, s_sc, 1)
            r.tensor.wait_ge(s_sc, s_sc.count)
            ps = pb[2]
            for fk in range(32):
                r.tensor.matmul(ps[0:16, :], apr.at(0, 16 * fk, 16 * fk + 16), h[fk],
                                start=(fk == 0), stop=(fk == 31))
            mark(r.tensor, s_pe, 1)
            r.scalar.wait_ge(s_pe, s_pe.count)
            r.scalar.activation(out=midp.rows(0, 16, 0), in_=ps[0:16, :], func=AF.Identity, scale=2.0)
            mark(r.scalar, s_sc, 1)
            r.tensor.wait_ge(s_sc, s_sc.count)
            for fk in range(32):
                r.sync.wait_ge(s_pe, s_pe.count)
                r.sync.dma_start(out=wpsl[fk % 2], in_=ds["wqpr_dram"][fk, :, :]).then_inc(s_w, 16)
                r.tensor.wait_ge(s_w, s_w.count)
                if fk < 2 and tc > 0:
                    r.tensor.wait_ge(s_v, s_v.count)
                for oc2 in range(DCH):
                    r.tensor.matmul(pb[oc2][:, :], wpsl.at(fk % 2, 128 * oc2, 128 * oc2 + 128),
                                    h[fk], start=(fk == 0), stop=False)
                mark(r.tensor, s_pe, 1)
            for oc2 in range(DCH):
                r.tensor.matmul(pb[oc2][:, :], bpr.rows(0, 16, 0, 128 * oc2, 128 * oc2 + 128),
                                midp.rows(0, 16, 0), start=False, stop=True)
            mark(r.tensor, s_pe, 1)
            ptok = s_pe.count
            for oc2 in range(DCH):
                r.scalar.wait_ge(s_pe, ptok)
                tmp = gsc.rows(0, P, 0, 512 * (oc2 % 4), 512 * (oc2 % 4) + 512)
                if oc2 >= 4:
                    r.scalar.wait_ge(s_v, s_v.count)
                r.scalar.activation(out=tmp, in_=pb[oc2][:, :], func=AF.Identity, bias=bcol("b_proj", oc2))
                mark(r.scalar, s_sc, 1)
                r.vector.wait_ge(s_sc, s_sc.count)
                r.vector.tensor_tensor(out=xr[:, oc2, t0:t0 + 512],
                                       in0=xr[:, oc2, t0:t0 + 512], in1=tmp, op=ALU.add)
                mark(r.vector, s_v, 1)
            r.tensor.wait_ge(s_v, s_v.count)
            r.scalar.wait_ge(s_v, s_v.count)


    def emit_l1b():
        AB.reset(AB_BASE)
        xln = AB.take(T, DCH)
        lat = AB.take(T, 2)
        kbf = Buf(xln.t, xln.off_b, T, DCH)
        vch = 65
        vaug = AB.take(16 * vch, TCH)
        wsl = AB.take(1024, 2)
        stg = AB.take(512, 2, F32)
        pT = AB.take(512, 2)
        qsl = AB.take(512, 2)
        uch = AB.take(T, 2)
        bvl = AB.take(1024, 1)

        def consume(p, slot):
            r.gpsimd.wait_ge(s_v, s_v.count)
            r.gpsimd.dma_start(out=ds["xlnd"][p, :, :], in_=xln[p]).then_inc(s_g, 16)
        ln_tok = emit_ln(xln, Buf(vaug.t, vaug.off_b, T, 2), consume=consume)
        r.tensor.wait_ge(s_v, ln_tok)
        r.sync.wait_ge(s_v, ln_tok)
        r.sync.dma_start(out=bvl.rows(0, 1, 0), in_=dt["bvrow"][0:1, D:2 * D]).then_inc(s_w, 16)
        for po in range(DCH):
            sl = po % 2
            r.vector.wait_ge(s_pe, s_pe.count)
            for kx in range(DCH):
                r.sync.wait_ge(s_v, s_v.count)
                r.sync.dma_start(out=stg.at(kx % 2, 0, 128), in_=dt["wqB"][kx, :, 128 * po:128 * po + 128]).then_inc(s_w, 16)
                r.vector.wait_ge(s_w, s_w.count)
                r.vector.tensor_copy(out=wsl.at(sl, 128 * kx, 128 * kx + 128), in_=stg.at(kx % 2, 0, 128))
                mark(r.vector, s_v, 1)
            r.tensor.wait_ge(s_v, s_v.count)
            for n in range(NQ):
                ps = pb[n % 2]
                for kx in range(DCH):
                    r.tensor.matmul(ps[:, :], wsl.at(sl, 128 * kx, 128 * kx + 128),
                                    xln.at(kx, 512 * n, 512 * n + 512),
                                    start=(kx == 0), stop=(kx == DCH - 1))
                mark(r.tensor, s_pe, 1)
                r.scalar.wait_ge(s_pe, s_pe.count)
                r.scalar.wait_ge(s_s, s_s.count)
                r.scalar.activation(out=qsl[n % 2], in_=ps[:, :], func=AF.Identity, bias=bcol("Bq", po))
                mark(r.scalar, s_sc, 1)
                r.tensor.wait_ge(s_sc, s_sc.count)
                r.scalar.dma_start(out=ds["q_dram"][po, :, 512 * n:512 * n + 512], in_=qsl[n % 2]).then_inc(s_s, 16)
        for po in range(2):
            r.vector.wait_ge(s_pe, s_pe.count)
            for kx in range(DCH):
                r.sync.wait_ge(s_v, s_v.count)
                r.sync.dma_start(out=stg.at(kx % 2, 0, 128), in_=dt["wlB"][kx, :, 128 * po:128 * po + 128]).then_inc(s_w, 16)
                r.vector.wait_ge(s_w, s_w.count)
                r.vector.tensor_copy(out=wsl.at(po % 2, 128 * kx, 128 * kx + 128), in_=stg.at(kx % 2, 0, 128))
                mark(r.vector, s_v, 1)
            r.tensor.wait_ge(s_v, s_v.count)
            for n in range(NQ):
                ps = pb[n % 2]
                for kx in range(DCH):
                    r.tensor.matmul(ps[:, :], wsl.at(po % 2, 128 * kx, 128 * kx + 128),
                                    xln.at(kx, 512 * n, 512 * n + 512),
                                    start=(kx == 0), stop=(kx == DCH - 1))
                mark(r.tensor, s_pe, 1)
                r.scalar.wait_ge(s_pe, s_pe.count)
                r.scalar.activation(out=lat.at(po, 512 * n, 512 * n + 512), in_=ps[:, :],
                                    func=AF.Identity, bias=bcol("Bl", po))
                mark(r.scalar, s_sc, 1)
                r.tensor.wait_ge(s_sc, s_sc.count)
        r.sync.wait_ge(s_sc, s_sc.count)
        r.vector.wait_ge(s_pe, s_pe.count)
        for kx in range(2):
            r.sync.wait_ge(s_v, s_v.count)
            r.sync.dma_start(out=stg[0], in_=dt["wkB"][kx, :, 0:512]).then_inc(s_w, 16)
            r.sync.dma_start(out=stg[1], in_=dt["wkB"][kx, :, 512:1024]).then_inc(s_w, 16)
            r.vector.wait_ge(s_w, s_w.count)
            r.vector.tensor_copy(out=wsl.at(kx, 0, 512), in_=stg[0])
            r.vector.tensor_copy(out=wsl.at(kx, 512, 1024), in_=stg[1])
            mark(r.vector, s_v, 1)
            r.sync.wait_ge(s_v, s_v.count)
        r.tensor.wait_ge(s_v, s_v.count)
        for po in range(DCH):
            for n in range(NQ):
                ps = pb[n % 2]
                for kx in range(2):
                    r.tensor.matmul(ps[:, :], wsl.at(kx, 128 * po, 128 * po + 128),
                                    lat.at(kx, 512 * n, 512 * n + 512),
                                    start=(kx == 0), stop=(kx == 1))
                mark(r.tensor, s_pe, 1)
                r.scalar.wait_ge(s_pe, s_pe.count)
                r.scalar.activation(out=kbf.at(po, 512 * n, 512 * n + 512), in_=ps[:, :],
                                    func=AF.Identity, bias=bcol("Bk", po))
                mark(r.scalar, s_sc, 1)
                r.tensor.wait_ge(s_sc, s_sc.count)
        r.vector.wait_ge(s_pe, s_pe.count)
        for kx in range(2):
            r.sync.wait_ge(s_v, s_v.count)
            r.sync.dma_start(out=stg[0], in_=dt["wvB"][kx, :, 0:512]).then_inc(s_w, 16)
            r.sync.dma_start(out=stg[1], in_=dt["wvB"][kx, :, 512:1024]).then_inc(s_w, 16)
            r.vector.wait_ge(s_w, s_w.count)
            r.vector.tensor_copy(out=wsl.at(kx, 0, 512), in_=stg[0])
            r.vector.tensor_copy(out=wsl.at(kx, 512, 1024), in_=stg[1])
            mark(r.vector, s_v, 1)
            r.sync.wait_ge(s_v, s_v.count)
        for tc in range(TCH):
            for hh in range(16):
                r.vector.tensor_copy(out=vaug.at(tc, vch * hh + 64, vch * hh + 65), in_=ones_t[:, 0:1])
        mark(r.vector, s_v, 1)
        r.tensor.wait_ge(s_v, s_v.count)
        r.tensor.wait_ge(s_w, s_w.count)
        for tc in range(TCH):
            for n in range(2):
                ps = pb[2 + n]
                for kx in range(2):
                    r.tensor.matmul(ps[:, :], lat.rows(0, P, kx, 128 * tc, 128 * tc + 128),
                                    wsl.at(kx, 512 * n, 512 * n + 512), start=(kx == 0), stop=False)
                r.tensor.matmul(ps[:, :], ones_r[0:1, 0:128], bvl.rows(0, 1, 0, 512 * n, 512 * n + 512),
                                start=False, stop=True)
                mark(r.tensor, s_pe, 1)
                r.vector.wait_ge(s_pe, s_pe.count)
                for hh in range(8):
                    h = 8 * n + hh
                    r.vector.tensor_copy(out=vaug.at(tc, vch * h, vch * h + 64), in_=ps[:, 64 * hh:64 * hh + 64])
                mark(r.vector, s_v, 1)
                r.tensor.wait_ge(s_v, s_v.count)
        if L1BPHASE < 2:
            return
        for h in range(H):
            hp, ho = h // 2, 64 * (h % 2)
            for qc in range(NQ):
                q0 = 512 * qc
                r.scalar.wait_ge(s_s, s_s.count)
                r.scalar.dma_start(out=qsl.rows(ho, ho + 64, qc % 2),
                                   in_=ds["q_dram"][hp, ho:ho + 64, q0:q0 + 512]).then_inc(s_s, 16)
                q_tok = s_s.count
                pout = pb[4 + qc % 2]
                r.tensor.wait_ge(s_s, q_tok)
                for jp in range(TCH):
                    j0 = 128 * jp
                    ps = pb[6 + jp % 2]
                    r.tensor.matmul(ps[:, :], kbf.rows(ho, ho + 64, hp, j0, j0 + 128),
                                    qsl.rows(ho, ho + 64, qc % 2), start=True, stop=True)
                    mark(r.tensor, s_pe, 1)
                    pt = pT[jp % 2]
                    r.scalar.wait_ge(s_pe, s_pe.count)
                    r.scalar.activation(out=pt, in_=ps[:, :], func=AF.Exp, scale=0.125)
                    mark(r.scalar, s_sc, 1)
                    r.tensor.wait_ge(s_sc, s_sc.count)
                    r.tensor.matmul(pout[0:65, :], vaug.at(jp, vch * h, vch * h + 65),
                                    pt, start=(jp == 0), stop=(jp == TCH - 1))
                    mark(r.tensor, s_pe, 1)
                    r.scalar.wait_ge(s_pe, s_pe.count)
                r.vector.wait_ge(s_pe, s_pe.count)
                r.vector.tensor_copy(out=uch.rows(0, 64, qc % 2, 0, 512), in_=pout[0:64, :])
                r.vector.tensor_copy(out=dsb[64:65, 0:512], in_=pout[64:65, :])
                mark(r.vector, s_v, 1)
                r.tensor.wait_ge(s_v, s_v.count)
                r.gpsimd.wait_ge(s_v, s_v.count)
                r.gpsimd.dma_start(out=ds["xlnd"][hp, ho:ho + 64, q0:q0 + 512],
                                   in_=uch.rows(0, 64, qc % 2, 0, 512)).then_inc(s_g, 16)
                r.scalar.wait_ge(s_v, s_v.count)
                r.scalar.dma_start(out=ds["den_dram"][h:h + 1, q0:q0 + 512], in_=dsb[64:65, 0:512]).then_inc(s_s, 16)
                r.vector.wait_ge(s_s, s_s.count)
                r.vector.wait_ge(s_g, s_g.count)
        if L1BPHASE < 3:
            return
        r.gpsimd.wait_ge(s_s, s_s.count)
        for n in range(NQ):
            c0 = 512 * n
            if n > 0:
                r.gpsimd.wait_ge(s_v, s_v.count)
            r.gpsimd.dma_start(out=dsb[0:16, 0:512], in_=ds["den_dram"][0:16, c0:c0 + 512]).then_inc(s_g, 16)
            r.vector.wait_ge(s_g, s_g.count)
            r.vector.reciprocal(out=dsb[0:16, 0:512], in_=dsb[0:16, 0:512]).then_inc(s_v, 1)
            r.vector.wait_ge(s_v, s_v.count)
            r.vector.engine_nop().then_inc(s_v, 1)
            r.gpsimd.wait_ge(s_v, s_v.count)
            r.gpsimd.dma_start(out=ds["den_dram"][0:16, c0:c0 + 512], in_=dsb[0:16, 0:512]).then_inc(s_g, 16)
        r.gpsimd.wait_ge(s_g, s_g.count)
        uh = Buf(xln.t, xln.off_b, T, DCH)
        for kx in range(DCH):
            r.gpsimd.wait_ge(s_v, s_v.count)
            r.gpsimd.dma_start(out=uch[kx % 2], in_=ds["xlnd"][kx, :, :]).then_inc(s_g, 16)
            r.gpsimd.dma_start(out=gsc.rows(0, 64, 0), in_=ds["den_dram"][2 * kx:2 * kx + 1, :].broadcast_to((64, T))).then_inc(s_g, 16)
            r.gpsimd.dma_start(out=gsc.rows(64, 128, 0), in_=ds["den_dram"][2 * kx + 1:2 * kx + 2, :].broadcast_to((64, T))).then_inc(s_g, 16)
            r.vector.wait_ge(s_g, s_g.count)
            r.vector.tensor_mul(uh[kx], uch[kx % 2], gsc[0])
            mark(r.vector, s_v, 1)
        for kx in range(DCH):
            for half in range(2):
                r.sync.wait_ge(s_v, s_v.count)
                r.sync.dma_start(out=stg[0], in_=dt["woB"][kx, :, 512 * half:512 * half + 512]).then_inc(s_w, 16)
                r.vector.wait_ge(s_w, s_w.count)
                r.vector.tensor_copy(out=wsl.at(kx % 2, 512 * half, 512 * half + 512), in_=stg[0]) if False else None
                r.vector.tensor_copy(out=vaug.at(kx * 2 + half, 0, 512) if False else Buf(vaug.t, vaug.off_b, 1024, TCH).at(kx, 512 * half, 512 * half + 512), in_=stg[0])
                mark(r.vector, s_v, 1)
        wof2 = Buf(vaug.t, vaug.off_b, 1024, TCH)
        r.tensor.wait_ge(s_v, s_v.count)
        for po in range(DCH):
            for n in range(NQ):
                ps = pb[(po * NQ + n) % 2]
                for kx in range(DCH):
                    r.tensor.matmul(ps[:, :], wof2.at(kx, 128 * po, 128 * po + 128),
                                    uh.at(kx, 512 * n, 512 * n + 512),
                                    start=(kx == 0), stop=(kx == DCH - 1))
                mark(r.tensor, s_pe, 1)
                r.scalar.wait_ge(s_pe, s_pe.count)
                tmp = gsc.rows(0, P, 0, 512 * (n % 4), 512 * (n % 4) + 512)
                r.scalar.activation(out=tmp, in_=ps[:, :], func=AF.Identity, bias=bcol("Bo", po))
                mark(r.scalar, s_sc, 1)
                r.vector.wait_ge(s_sc, s_sc.count)
                r.vector.tensor_tensor(out=xr[:, po, 512 * n:512 * n + 512],
                                       in0=xr[:, po, 512 * n:512 * n + 512], in1=tmp, op=ALU.add)
                mark(r.vector, s_v, 1)
                r.tensor.wait_ge(s_v, s_v.count)
                r.scalar.wait_ge(s_v, s_v.count)

    if ONLY_SUB is not None:
        {1: emit_l0a, 2: emit_l0f, 3: emit_l0b, 4: emit_l0f, 5: emit_l1a,
         6: lambda: emit_l1f(True), 7: emit_l1b, 8: lambda: emit_l1f(True)}[ONLY_SUB]()
        barrier()
        for p in range(DCH):
            r.gpsimd.dma_start(out=yT[p, :, :], in_=xr[:, p, :]).then_inc(s_g, 16)
        r.gpsimd.wait_ge(s_g, s_g.count)
        return
    if n_sub >= 1:
        emit_l0a()
        barrier()
    if n_sub >= 2:
        emit_l0f()
        barrier()
    if n_sub >= 3:
        emit_l0b()
        barrier()
    if n_sub >= 4:
        emit_l0f()
        barrier()
    if n_sub >= 5:
        emit_l1a()
        barrier()
    if n_sub >= 6:
        emit_l1f(True)
        barrier()
    if n_sub >= 7:
        emit_l1b()
        barrier()
    if n_sub >= 8:
        emit_l1f(False)
        barrier()

    for p in range(DCH):
        r.gpsimd.dma_start(out=yT[p, :, :], in_=xr[:, p, :]).then_inc(s_g, 16)
    if DBG_COLS is not None:
        r.gpsimd.dma_start(out=env["ydbg"][:, 0:DBG_COLS[1] - DBG_COLS[0]],
                           in_=AB.t[:, DBG_COLS[0]:DBG_COLS[1]]).then_inc(s_g, 16)
    r.gpsimd.wait_ge(s_g, s_g.count)
    if DBG_OVERWAIT:
        r.gpsimd.wait_ge(s_w, s_w.count + 16)


def _prep_inputs(kw):
    x = np.asarray(kw["L0_attn_a_x"])
    perm = _perm_eo()
    cosT, sinT = _rope_tables()
    mm = _masks()

    def T_(a):
        return np.ascontiguousarray(np.asarray(a).T)

    com = {}
    com["wq0"] = _wchunks(np.asarray(kw["L0_attn_a_wq"])[:, perm])
    com["wk0"] = _wchunks(np.asarray(kw["L0_attn_a_wk"])[:, perm])
    com["wv0"] = _wchunks(np.asarray(kw["L0_attn_a_wv"]))
    com["wo0"] = _wchunks(np.asarray(kw["L0_attn_a_wo_w"]))
    com["wqb"] = _wchunks(T_(kw["L0_attn_b_Wq"]))
    com["wkb"] = _wchunks(T_(kw["L0_attn_b_Wk"]))
    com["wvb"] = _wchunks(T_(kw["L0_attn_b_Wv"]))
    com["wfic"] = _wchunks(np.concatenate([T_(kw["L0_ffn_wf"]), T_(kw["L0_ffn_wi"]), T_(kw["L0_ffn_wc"])], 1))
    com["ufic"] = _wchunks(np.concatenate([T_(kw["L0_ffn_uf"]), T_(kw["L0_ffn_ui"]), T_(kw["L0_ffn_uc"])], 1))
    com["wovT"] = _wchunks(T_(kw["L0_ffn_wo"]))
    com["uovT"] = _wchunks(T_(kw["L0_ffn_uo"]))
    com["incb"] = np.ascontiguousarray(
        np.broadcast_to(np.asarray(kw["L0_ffn_in_c"]).astype(np.float32).reshape(DCH, P, 1), (DCH, P, 256)))
    com["wqkv"] = _wchunks(T_(kw["L1_attn_a_Wqkv"]))
    com["wprj"] = _wchunks(T_(kw["L1_attn_a_Wproj"]))
    com["wfcT"] = _wchunks(T_(kw["L1_ffn_W_fc"]))
    com["wprT"] = _wchunks(T_(kw["L1_ffn_W_proj"]))
    com["afcT"] = _wchunks(T_(kw["L1_ffn_A_fc"]))
    com["bfcT"] = np.ascontiguousarray(T_(kw["L1_ffn_B_fc"]).astype(np.float32)).reshape(1, 16, 4096)
    com["aprT"] = _wchunks(T_(kw["L1_ffn_A_proj"]))
    com["bprT"] = np.ascontiguousarray(T_(kw["L1_ffn_B_proj"]).astype(np.float32)).reshape(1, 16, D)
    com["wqB"] = _wchunks(np.asarray(kw["L1_attn_b_Wq"]))
    com["wlB"] = _wchunks(np.asarray(kw["L1_attn_b_Wl"]))
    com["wkB"] = _wchunks(np.asarray(kw["L1_attn_b_Wk"]))
    com["wvB"] = _wchunks(np.asarray(kw["L1_attn_b_Wv"]))
    com["woB"] = _wchunks(np.asarray(kw["L1_attn_b_Wo"]))

    biases = np.zeros((P, 200), np.float32)
    def setb(name, v):
        v = np.asarray(v).astype(np.float32)
        cols = v.size // P
        biases[:, BOFF[name]:BOFF[name] + cols] = v.reshape(cols, P).T
    setb("wo_b", kw["L0_attn_a_wo_b"])
    setb("bq", kw["L0_attn_b_bq"]); setb("bk", kw["L0_attn_b_bk"]); setb("bv", kw["L0_attn_b_bv"])
    setb("bfic", np.concatenate([np.asarray(kw["L0_ffn_bf"]), np.asarray(kw["L0_ffn_bi"]), np.asarray(kw["L0_ffn_bc"])]))
    setb("bo_f", kw["L0_ffn_bo"])
    setb("bqkv", kw["L1_attn_a_bqkv"]); setb("bproj", kw["L1_attn_a_bproj"])
    setb("b_fc", kw["L1_ffn_b_fc"]); setb("b_proj", kw["L1_ffn_b_proj"])
    setb("Bq", kw["L1_attn_b_bq"]); setb("Bk", kw["L1_attn_b_bk"]); setb("Bv", kw["L1_attn_b_bv"])
    setb("Bo", kw["L1_attn_b_bo"]); setb("Bl", kw["L1_attn_b_bl"])
    biases[:, BOFF["eps"]] = 1e-5
    biases[:, BOFF["epsa"]] = 0.001

    com["biases"] = biases
    com["cosT"] = cosT; com["sinT"] = sinT
    com["maskmul"] = mm
    com["onesb"] = np.ones((P, T), BF)
    com["onesrow"] = np.ones((1, 512), BF)
    hs = np.zeros((P, 2), np.float32); hs[0:64, 0] = 1.0; hs[64:128, 1] = 1.0
    com["headsum"] = hs.astype(BF)
    sc = np.zeros((1, 8), np.float32); sc[0, 0] = float(np.asarray(kw["L1_attn_a_scale"]))
    com["scal"] = sc
    com["scalb"] = np.ascontiguousarray(np.broadcast_to(sc[0:1, 0:1], (P, 1)))
    bvrow = np.zeros((1, 2 * D), np.float32)
    bvrow[0, 0:D] = np.asarray(kw["L0_attn_b_bv"])
    bvrow[0, D:2 * D] = np.asarray(kw["L1_attn_b_bv"])
    com["bvrow"] = bvrow.astype(BF)

    in_maps = []
    for c in range(NC):
        m = dict(com)
        m["xT"] = np.ascontiguousarray(x[c].T).reshape(DCH, P, T).astype(np.float32)
        in_maps.append(m)
    return in_maps


def kernel(**kw):
    in_maps = _prep_inputs(kw)
    nc = build_nc(N_SUB)
    res = run_bass_kernel_spmd(nc, in_maps, core_ids=list(range(NC)))
    out = np.empty((NC, T, D), np.float32)
    for c in range(NC):
        y = np.asarray(res.results[c]["yT"])
        out[c] = y.reshape(D, T).T
    return out


# version 03
# speedup vs baseline: 1.0570x; 1.0570x over previous
import numpy as np
import ml_dtypes
from contextlib import ExitStack
import concourse.bass as bass
import concourse.mybir as mybir
from concourse.bass_utils import run_bass_kernel_spmd
from concourse.aot_env import allow_ici_all_to_all

F32 = mybir.dt.float32
BF16 = mybir.dt.bfloat16
I32 = mybir.dt.int32
AF = mybir.ActivationFunctionType
ALU = mybir.AluOpType
BF = ml_dtypes.bfloat16

P = 128
T = 2048
D = 1024
NC = 8
DCH = D // P
TCH = T // P
NQ = T // 512
H = 16
HD = 64

ARENA = 69504

N_SUB = 8
PHASE = 99
DBG_COLS = None
DBG_OVERWAIT = False
LNPHASE = 99
L1BPHASE = 99
NGRP = 8
ONLY_SUB = None

class _RecInst:
    __slots__ = ("incs",)
    def __init__(self):
        self.incs = []
    def then_inc(self, sem, n=1):
        self.incs.append((sem, n))
        sem.count += n
        return self

class _EngRec:
    def __init__(self, name):
        self.name = name
        self.calls = []
    def __getattr__(self, meth):
        def call(*a, **k):
            inst = _RecInst()
            self.calls.append((meth, a, k, inst))
            return inst
        return call

class Sem:
    def __init__(self, name):
        self.name = name
        self.count = 0
        self.handle = None

class Recorder:
    def __init__(self):
        self.sync = _EngRec("sync")
        self.tensor = _EngRec("tensor")
        self.vector = _EngRec("vector")
        self.scalar = _EngRec("scalar")
        self.gpsimd = _EngRec("gpsimd")

def _replay(eng, calls):
    for meth, a, k, inst in calls:
        a = tuple(x.handle if isinstance(x, Sem) else x for x in a)
        k = {kk: (vv.handle if isinstance(vv, Sem) else vv) for kk, vv in k.items()}
        real = getattr(eng, meth)(*a, **k)
        for sem, n in inst.incs:
            real.then_inc(sem.handle, n)

class Buf:
    def __init__(self, t, off_b, w, n=1, dtype=BF16):
        self.t, self.off_b, self.w, self.n, self.dtype = t, off_b, w, n, dtype
        self.m = 2 if dtype == F32 else 1
    def at(self, i, c0, c1):
        a = self.off_b + self.m * (i * self.w + c0)
        b = self.off_b + self.m * (i * self.w + c1)
        ap = self.t[:, a:b]
        return ap.bitcast(F32) if self.m == 2 else ap
    def __getitem__(self, i):
        return self.at(i, 0, self.w)
    def rows(self, r0, r1, i=0, c0=0, c1=None):
        c1 = self.w if c1 is None else c1
        a = self.off_b + self.m * (i * self.w + c0)
        b = self.off_b + self.m * (i * self.w + c1)
        ap = self.t[r0:r1, a:b]
        return ap.bitcast(F32) if self.m == 2 else ap

class Arena:
    def __init__(self, t, cols):
        self.t, self.cols, self.off = t, cols, 0
    def take(self, w, n=1, dtype=BF16):
        m = 2 if dtype == F32 else 1
        b = Buf(self.t, self.off, w, n, dtype)
        self.off += w * n * m
        assert self.off <= self.cols, f"arena overflow: {self.off} > {self.cols}"
        return b
    def reset(self, off=0):
        self.off = off

def _wchunks(w):
    K, C = w.shape
    assert K % P == 0
    return np.ascontiguousarray(w.reshape(K // P, P, C).astype(np.float32))

def _wgroups(w):
    K, C = w.shape
    g = w.reshape(K // P, P, C // P, P).transpose(2, 1, 0, 3).reshape(C // P, P, K)
    return np.ascontiguousarray(g.astype(np.float32))

def _rope_tables():
    inv = 1.0 / (10000.0 ** (np.arange(0, HD, 2)[: HD // 2].astype(np.float32) / HD))
    ang = np.outer(np.arange(T, dtype=np.float32), inv)
    cos, sin = np.cos(ang), np.sin(ang)
    cosT = np.tile(cos.T, (4, 1)).astype(np.float32)
    sinT = np.tile(sin.T, (4, 1)).astype(np.float32)
    sgn = np.ones((P, 1), np.float32)
    sgn[0:32] = -1.0
    sgn[64:96] = -1.0
    return cosT, np.ascontiguousarray(sinT * sgn)

def _perm_eo():
    perm = []
    for h in range(H):
        perm += list(range(64 * h, 64 * h + 64, 2))
        perm += list(range(64 * h + 1, 64 * h + 64, 2))
    return np.asarray(perm)

def _masks():
    k = np.arange(P)[:, None]
    c = np.arange(896)[None, :]
    return np.ascontiguousarray((c >= k + 384).astype(np.float32).astype(BF))

BOFF = {
    "wo_b": 0, "bq": 8, "bk": 16, "bv": 24,
    "bfic": 32, "bo_f": 56,
    "bqkv": 64, "bproj": 88,
    "b_fc": 96, "b_proj": 128,
    "Bq": 136, "Bk": 144, "Bv": 152, "Bo": 160, "Bl": 168,
    "eps": 170, "epsa": 171,
}

def build_nc(n_sub=N_SUB, detect_races=False):
    nc = bass.Bass("TRN2", target_bir_lowering=False, debug=False,
                   detect_race_conditions=detect_races)
    dt = {}

    def din(name, shape, dtype=F32):
        ap = nc.dram_tensor(name, list(shape), dtype, kind="ExternalInput").ap()
        dt[name] = ap
        return ap

    din("xT", (DCH, P, T))
    din("wq0", (DCH, P, D)); din("wk0", (DCH, P, D)); din("wv0", (DCH, P, D)); din("wo0", (DCH, P, D))
    din("wqkg", (DCH, P, 3 * D))
    din("wqbg", (DCH, P, D)); din("wkbg", (DCH, P, D))
    din("wqkvg", (DCH, P, 3 * D))
    din("wqBg", (DCH, P, D))
    din("wqb", (DCH, P, D)); din("wkb", (DCH, P, D)); din("wvb", (DCH, P, D))
    din("wfic", (DCH, P, 3 * D)); din("ufic", (DCH, P, 3 * D))
    din("wovT", (DCH, P, D)); din("uovT", (DCH, P, D)); din("incb", (DCH, P, 256))
    din("wqkv", (DCH, P, 3 * D)); din("wprj", (DCH, P, D))
    din("wfcT", (DCH, P, 4096)); din("wprT", (4096 // P, P, D))
    din("afcT", (DCH, P, 16)); din("bfcT", (1, 16, 4096))
    din("aprT", (4096 // P, P, 16)); din("bprT", (1, 16, D))
    din("wqB", (DCH, P, D)); din("wlB", (DCH, P, 256))
    din("wkB", (2, P, D)); din("wvB", (2, P, D)); din("woB", (DCH, P, D))
    din("biases", (P, 200))
    din("cosT", (P, T)); din("sinT", (P, T))
    din("cosTb", (P, T), BF16); din("sinTb", (P, T), BF16)
    din("maskmul", (P, 896), BF16)
    din("onesb", (P, T), BF16); din("onesrow", (1, 512), BF16)
    din("headsum", (P, 2), BF16); din("scal", (1, 8)); din("scalb", (P, 1))
    din("bvrow", (1, 2 * D), BF16)

    yT = nc.dram_tensor("yT", [DCH, P, T], F32, kind="ExternalOutput").ap()
    ydbg = nc.dram_tensor("ydbg", [P, 16384], BF16, kind="ExternalOutput").ap()

    ds = {}
    def dints(name, shape, dtype):
        ds[name] = nc.dram_tensor(name, list(shape), dtype).ap()
    dints("cc_in", (NC, DCH, P, 256), BF16)
    dints("cc_out", (NC, DCH, P, 256), BF16)
    dints("xlnd", (DCH, P, T), BF16)
    dints("cc2_in", (DCH, P, 256), BF16)
    dints("cc2_out", (NC * DCH, P, 256), BF16)
    dints("stat_dram", (4, T), F32)
    dints("den_dram", (16, T), F32)
    dints("xln_dram", (DCH, P, T), BF16)
    dints("q_dram", (DCH, P, T), BF16)
    dints("wqfc_dram", (32, P, 1024), BF16)
    dints("wqpr_dram", (4096 // P, P, D), BF16)
    dints("ffnw_dram", (DCH, P, 8 * D), BF16)

    es = ExitStack()
    r = Recorder()
    s_w = Sem("s_w"); s_g = Sem("s_g"); s_s = Sem("s_s")
    s_pe = Sem("s_pe"); s_v = Sem("s_v"); s_sc = Sem("s_sc"); s_cc = Sem("s_cc")
    s_b = Sem("s_b"); s_gc = Sem("s_gc")
    sems = [s_w, s_g, s_s, s_pe, s_v, s_sc, s_cc, s_b, s_gc]

    with es:
        sbuf = lambda name, shape, dtype: es.enter_context(nc.sbuf_tensor(name, list(shape), dtype))
        env = dict(dt=dt, ds=ds, yT=yT, ydbg=ydbg)
        env["xr"] = sbuf("xr", (P, DCH, T), F32)
        env["ones_t"] = sbuf("ones_t", (P, T), BF16)
        env["ones_r"] = sbuf("ones_r", (1, 512), BF16)
        env["hsum"] = sbuf("hsum", (P, 2), BF16)
        env["bia"] = sbuf("bia", (P, 200), F32)
        env["scl"] = sbuf("scl", (1, 8), F32)
        env["sclb"] = sbuf("sclb", (P, 1), F32)
        env["dsb"] = sbuf("dsb", (P, 512), F32)
        env["AB"] = Arena(sbuf("AB", (P, ARENA), BF16), ARENA)
        env["pb"] = [es.enter_context(nc.psum_tensor(f"pb{i}", [P, 512], F32)) for i in range(8)]

        emit_program(r, sems, env, n_sub)

        for s in sems:
            s.handle = es.enter_context(nc.semaphore(s.name))
        es.enter_context(allow_ici_all_to_all())
        block = es.enter_context(nc.Block())
        block.sync(lambda eng: _replay(eng, r.sync.calls))
        block.tensor(lambda eng: _replay(eng, r.tensor.calls))
        block.vector(lambda eng: _replay(eng, r.vector.calls))
        block.scalar(lambda eng: _replay(eng, r.scalar.calls))
        block.gpsimd(lambda eng: _replay(eng, r.gpsimd.calls))
    return nc


def emit_program(r, sems, env, n_sub):
    global PHASE, LNPHASE, DBG_COLS, DBG_OVERWAIT, L1BPHASE, ONLY_SUB, NGRP
    s_w, s_g, s_s, s_pe, s_v, s_sc, s_cc, s_b, s_gc = sems
    dt, ds, yT = env["dt"], env["ds"], env["yT"]
    xr = env["xr"]
    ones_t, ones_r, hsum = env["ones_t"], env["ones_r"], env["hsum"]
    bia, scl, sclb, dsb = env["bia"], env["scl"], env["sclb"], env["dsb"]
    AB = env["AB"]
    pb = env["pb"]

    bcol = lambda name, j=0: bia[:, BOFF[name] + j: BOFF[name] + j + 1]

    def mark(eng_rec, sem, n=1):
        last = eng_rec.calls[-1] if eng_rec.calls else None
        if last is not None and not last[3].incs and last[0] != "wait_ge":
            last[3].then_inc(sem, n)
        else:
            nop = "engine_nop" if eng_rec.name in ("vector", "gpsimd") else "nop"
            getattr(eng_rec, nop)().then_inc(sem, n)

    ab = AB.take(T, 2, F32)
    gsc = AB.take(T, 1, F32)
    AB_BASE = AB.off

    def barrier():
        r.sync.wait_ge(s_w, s_w.count)
        r.sync.nop().then_inc(s_b, 1)
        r.gpsimd.wait_ge(s_g, s_g.count)
        r.gpsimd.engine_nop().then_inc(s_b, 1)
        r.scalar.wait_ge(s_s, s_s.count)
        r.scalar.nop().then_inc(s_b, 1)
        r.vector.engine_nop().then_inc(s_b, 1)
        r.tensor.nop().then_inc(s_b, 1)
        v = s_b.count
        for e in (r.tensor, r.vector, r.scalar, r.gpsimd, r.sync):
            e.wait_ge(s_b, v)

    r.sync.dma_start(out=ones_t[:, :], in_=dt["onesb"][:, :]).then_inc(s_w, 16)
    r.sync.dma_start(out=ones_r[0:1, :], in_=dt["onesrow"][0:1, :]).then_inc(s_w, 16)
    r.sync.dma_start(out=hsum[:, :], in_=dt["headsum"][:, :]).then_inc(s_w, 16)
    r.sync.dma_start(out=bia[:, :], in_=dt["biases"][:, :]).then_inc(s_w, 16)
    r.sync.dma_start(out=scl[0:1, :], in_=dt["scal"][0:1, :]).then_inc(s_w, 16)
    r.sync.dma_start(out=sclb[:, :], in_=dt["scalb"][:, :]).then_inc(s_w, 16)
    for p in range(DCH):
        r.sync.dma_start(out=xr[:, p, :], in_=dt["xT"][p, :, :]).then_inc(s_w, 16)
    tok = s_w.count
    for e in (r.tensor, r.vector, r.scalar, r.gpsimd):
        e.wait_ge(s_w, tok)

    def emit_ln(xln, sq, consume=None, nslots=None):
        ns = nslots or DCH
        ones1 = ones_t[:, 0:1]
        for p in range(DCH):
            if p >= ns:
                r.vector.wait_ge(s_pe, s_pe.count)
            r.vector.tensor_copy(out=xln[p % ns], in_=xr[:, p, :])
            mark(r.vector, s_v, 1)
            r.scalar.wait_ge(s_v, s_v.count)
            if p >= 2:
                r.scalar.wait_ge(s_pe, s_pe.count)
            r.scalar.activation(out=sq[p % 2], in_=xln[p % ns], func=AF.Square)
            mark(r.scalar, s_sc, 1)
            r.tensor.wait_ge(s_sc, s_sc.count)
            r.tensor.wait_ge(s_v, s_v.count)
            for n in range(NQ):
                r.tensor.matmul(pb[n][0:1, :], ones1, xln.at(p % ns, 512 * n, 512 * n + 512),
                                start=(p == 0), stop=(p == DCH - 1))
            for n in range(NQ):
                r.tensor.matmul(pb[4 + n][0:1, :], ones1, sq.at(p % 2, 512 * n, 512 * n + 512),
                                start=(p == 0), stop=(p == DCH - 1))
            mark(r.tensor, s_pe, 1)
        mmtok = s_pe.count
        if LNPHASE < 1:
            return s_v.count
        m = ab.rows(0, 1, 0)
        s2 = ab.rows(0, 1, 1)
        tmp = gsc.rows(0, 1, 0)
        r.vector.wait_ge(s_pe, mmtok)
        for n in range(NQ):
            r.vector.tensor_copy(out=ab.rows(0, 1, 0, 512 * n, 512 * n + 512), in_=pb[n][0:1, :])
            r.vector.tensor_copy(out=ab.rows(0, 1, 1, 512 * n, 512 * n + 512), in_=pb[4 + n][0:1, :])
        mark(r.vector, s_v, 1)
        r.tensor.wait_ge(s_v, s_v.count)
        r.vector.tensor_scalar(out=m, in0=m, scalar1=1.0 / D, scalar2=None, op0=ALU.mult)
        r.vector.tensor_scalar(out=s2, in0=s2, scalar1=1.0 / D, scalar2=None, op0=ALU.mult)
        r.vector.tensor_mul(tmp, m, m)
        r.vector.tensor_sub(s2, s2, tmp)
        mark(r.vector, s_v, 1)
        if LNPHASE < 2:
            return s_v.count
        r.scalar.wait_ge(s_v, s_v.count)
        r.scalar.activation(out=s2, in_=s2, func=AF.Sqrt, bias=bcol("eps")[0:1, :])
        mark(r.scalar, s_sc, 1)
        r.vector.wait_ge(s_sc, s_sc.count)
        r.vector.reciprocal(out=s2, in_=s2).then_inc(s_v, 1)
        r.vector.wait_ge(s_v, s_v.count)
        r.vector.scalar_tensor_tensor(out=m, in0=m, scalar=-1.0, in1=s2,
                                      op0=ALU.mult, op1=ALU.mult)
        mark(r.vector, s_v, 1)
        if LNPHASE < 3:
            return s_v.count
        r.gpsimd.wait_ge(s_v, s_v.count)
        r.gpsimd.dma_start(out=ds["stat_dram"][0:1, :], in_=s2).then_inc(s_g, 16)
        r.gpsimd.dma_start(out=ds["stat_dram"][1:2, :], in_=m).then_inc(s_g, 16)
        r.gpsimd.wait_ge(s_g, s_g.count)
        r.gpsimd.dma_start(out=ab[0], in_=ds["stat_dram"][0:1, :].broadcast_to((P, T))).then_inc(s_g, 16)
        r.gpsimd.dma_start(out=ab[1], in_=ds["stat_dram"][1:2, :].broadcast_to((P, T))).then_inc(s_g, 16)
        r.vector.wait_ge(s_g, s_g.count)
        if LNPHASE < 4:
            return s_v.count
        for p in range(DCH):
            r.vector.tensor_mul(gsc[0], xr[:, p, :], ab[0])
            r.vector.tensor_tensor(out=xln[p % ns], in0=gsc[0], in1=ab[1], op=ALU.add)
            if consume is not None:
                mark(r.vector, s_v, 1)
                consume(p, p % ns)
        mark(r.vector, s_v, 1)
        return s_v.count

    def emit_l0a():
        AB.reset(AB_BASE)
        xln = AB.take(T, DCH)
        u = C = AB.take(T, DCH)
        gq = AB.take(T, 1); gk = AB.take(T, 1)
        shb = AB.take(T, 1)
        pT = AB.take(512, 2)
        vaug = AB.take(130, 16)
        wslot = AB.take(D, 3)
        wstg = AB.take(1024, 1, F32)
        mskm = AB.take(896, 1)
        csb = AB.take(T, 2, F32)
        wof = Buf(xln.t, xln.off_b, D, DCH)
        stg32 = lambda i: wstg[i]
        vch = 130

        ln_tok = emit_ln(xln, Buf(u.t, u.off_b, T, 2))
        r.tensor.wait_ge(s_v, ln_tok)
        r.sync.wait_ge(s_v, ln_tok)
        r.sync.dma_start(out=csb[0], in_=dt["cosT"][:, :]).then_inc(s_w, 16)
        r.sync.dma_start(out=csb[1], in_=dt["sinT"][:, :]).then_inc(s_w, 16)
        r.sync.dma_start(out=mskm[0], in_=dt["maskmul"][:, :]).then_inc(s_w, 16)
        r.vector.wait_ge(s_w, s_w.count)

        stok = [(s_v, 0)]
        for g in range(NGRP):
            r.vector.wait_ge(s_pe, s_pe.count)
            for wi in range(3):
                r.sync.wait_ge(*stok[0])
                r.sync.dma_start(out=stg32(0), in_=dt["wqkg"][g, :, D * wi:D * wi + D]).then_inc(s_w, 16)
                r.vector.wait_ge(s_w, s_w.count)
                r.vector.tensor_copy(out=wslot[wi], in_=stg32(0))
                mark(r.vector, s_v, 1)
                stok[0] = (s_v, s_v.count)
            for tc in range(TCH):
                r.vector.tensor_copy(out=vaug.at(tc, 64, 65), in_=ones_t[:, 0:1])
                r.vector.tensor_copy(out=vaug.at(tc, 129, 130), in_=ones_t[:, 0:1])
            mark(r.vector, s_v, 1)
            r.tensor.wait_ge(s_v, s_v.count)
            toks = []
            items = [(w, gb, n) for w, gb in ((0, gq), (1, gk)) for n in range(NQ)]
            for i, (which, gbuf, n) in enumerate(items):
                if i >= 3:
                    r.tensor.wait_ge(*toks[i - 3])
                ps = pb[i % 4]
                for kp in range(DCH):
                    r.tensor.matmul(ps[:, :], wslot.at(which, 128 * kp, 128 * kp + 128),
                                    xln.at(kp, 512 * n, 512 * n + 512),
                                    start=(kp == 0), stop=(kp == DCH - 1))
                mark(r.tensor, s_pe, 1)
                r.vector.wait_ge(s_pe, s_pe.count)
                r.vector.tensor_copy(out=gbuf.at(0, 512 * n, 512 * n + 512), in_=ps[:, :])
                mark(r.vector, s_v, 1)
                toks.append((s_v, s_v.count))
            vtoks = []
            for tc in range(TCH):
                if tc >= 3:
                    r.tensor.wait_ge(*vtoks[tc - 3])
                ps = pb[4 + tc % 4]
                for kp in range(DCH):
                    r.tensor.matmul(ps[:, 0:128], xln.at(kp, 128 * tc, 128 * tc + 128),
                                    wslot.at(2, 128 * kp, 128 * kp + 128),
                                    start=(kp == 0), stop=(kp == DCH - 1))
                mark(r.tensor, s_pe, 1)
                r.vector.wait_ge(s_pe, s_pe.count)
                r.vector.tensor_copy(out=vaug.at(tc, 0, 64), in_=ps[:, 0:64])
                r.vector.tensor_copy(out=vaug.at(tc, 65, 129), in_=ps[:, 64:128])
                mark(r.vector, s_v, 1)
                vtoks.append((s_v, s_v.count))
            for gi, gbuf in enumerate((gq, gk)):
                r.gpsimd.wait_ge(s_v, s_v.count)
                if g > 0 or gi > 0:
                    r.gpsimd.wait_ge(s_pe, s_pe.count)
                r.gpsimd.dma_start(out=shb.rows(0, 32, 0), in_=gbuf.rows(32, 64)).then_inc(s_g, 16)
                r.gpsimd.dma_start(out=shb.rows(64, 96, 0), in_=gbuf.rows(96, 128)).then_inc(s_g, 16)
                r.gpsimd.dma_start(out=shb.rows(32, 64, 0), in_=gbuf.rows(0, 32)).then_inc(s_g, 16)
                r.gpsimd.dma_start(out=shb.rows(96, 128, 0), in_=gbuf.rows(64, 96)).then_inc(s_g, 16)
                r.vector.wait_ge(s_g, s_g.count)
                r.vector.tensor_mul(gsc[0], gbuf[0], csb[0])
                r.vector.tensor_mul(shb[0], shb[0], csb[1])
                r.vector.tensor_tensor(out=gbuf[0], in0=gsc[0], in1=shb[0], op=ALU.add)
                mark(r.vector, s_v, 1)
                r.gpsimd.wait_ge(s_v, s_v.count)
            rope_tok = s_v.count
            r.tensor.wait_ge(s_v, rope_tok)
            for hl in range(2):
                h = 2 * g + hl
                qh = gq.rows(64 * hl, 64 * hl + 64)
                kh = gk.rows(64 * hl, 64 * hl + 64)
                for qc in range(NQ):
                    q0 = 512 * qc
                    pout = pb[4 + (qc % 2)]
                    npan = 4 * qc + 4
                    etoks = []
                    otoks = []
                    for kpan in range(npan):
                        k0 = 128 * kpan
                        di = (k0 - q0) // 128
                        if kpan >= 3:
                            r.tensor.wait_ge(*etoks[kpan - 3])
                        ps = pb[kpan % 4]
                        r.tensor.matmul(ps[:, :], kh[:, k0:k0 + 128], qh[:, q0:q0 + 512],
                                        start=True, stop=True)
                        mark(r.tensor, s_pe, 1)
                        pt = pT[kpan % 2]
                        r.scalar.wait_ge(s_pe, s_pe.count)
                        if kpan >= 2:
                            r.scalar.wait_ge(*otoks[kpan - 2])
                        r.scalar.activation(out=pt, in_=ps[:, :], func=AF.Exp, scale=0.125)
                        mark(r.scalar, s_sc, 1)
                        etok = (s_sc, s_sc.count)
                        if 0 <= di <= 3:
                            r.vector.wait_ge(s_sc, s_sc.count)
                            r.vector.tensor_mul(pt, pt, mskm.at(0, 384 - 128 * di, 896 - 128 * di))
                            mark(r.vector, s_v, 1)
                            etok = (s_v, s_v.count)
                        etoks.append(etok)
                        r.tensor.wait_ge(*etok)
                        r.tensor.matmul(pout[0:65, :], vaug.at(kpan, 65 * hl, 65 * hl + 65),
                                        pt, start=(kpan == 0), stop=(kpan == npan - 1))
                        mark(r.tensor, s_pe, 1)
                        otoks.append((s_pe, s_pe.count))
                    r.vector.wait_ge(s_pe, s_pe.count)
                    r.vector.tensor_copy(out=u.rows(64 * hl, 64 * hl + 64, g, q0, q0 + 512), in_=pout[0:64, :])
                    r.vector.tensor_copy(out=dsb[64:65, 0:512], in_=pout[64:65, :])
                    mark(r.vector, s_v, 1)
                    r.tensor.wait_ge(s_v, s_v.count)
                    r.scalar.wait_ge(s_v, s_v.count)
                    r.scalar.dma_start(out=ds["den_dram"][h:h+1, q0:q0 + 512], in_=dsb[64:65, 0:512]).then_inc(s_s, 16)
                    r.vector.wait_ge(s_s, s_s.count)
        if PHASE < 3:
            return
        r.gpsimd.wait_ge(s_s, s_s.count)
        for n in range(NQ):
            c0 = 512 * n
            if n > 0:
                r.gpsimd.wait_ge(s_v, s_v.count)
            r.gpsimd.dma_start(out=dsb[0:16, 0:512], in_=ds["den_dram"][0:16, c0:c0 + 512]).then_inc(s_g, 16)
            r.vector.wait_ge(s_g, s_g.count)
            r.vector.reciprocal(out=dsb[0:16, 0:512], in_=dsb[0:16, 0:512]).then_inc(s_v, 1)
            r.vector.wait_ge(s_v, s_v.count)
            r.vector.engine_nop().then_inc(s_v, 1)
            r.gpsimd.wait_ge(s_v, s_v.count)
            r.gpsimd.dma_start(out=ds["den_dram"][0:16, c0:c0 + 512], in_=dsb[0:16, 0:512]).then_inc(s_g, 16)
        r.gpsimd.wait_ge(s_g, s_g.count)
        for p in range(DCH):
            r.gpsimd.wait_ge(s_v, s_v.count)
            r.gpsimd.dma_start(out=gsc.rows(0, 64, 0), in_=ds["den_dram"][2 * p:2 * p + 1, :].broadcast_to((64, T))).then_inc(s_g, 16)
            r.gpsimd.dma_start(out=gsc.rows(64, 128, 0), in_=ds["den_dram"][2 * p + 1:2 * p + 2, :].broadcast_to((64, T))).then_inc(s_g, 16)
            r.vector.wait_ge(s_g, s_g.count)
            r.vector.tensor_mul(u[p], u[p], gsc[0])
            mark(r.vector, s_v, 1)
        if PHASE < 4:
            return
        r.vector.wait_ge(s_pe, s_pe.count)
        wtoks = list(stok)
        for kp in range(DCH):
            r.sync.wait_ge(*wtoks[0])
            r.sync.dma_start(out=stg32(0), in_=dt["wo0"][kp, :, :]).then_inc(s_w, 16)
            r.vector.wait_ge(s_w, s_w.count)
            r.vector.tensor_copy(out=wof.at(kp, 0, D), in_=stg32(0))
            mark(r.vector, s_v, 1)
            wtoks[0] = (s_v, s_v.count)
        r.tensor.wait_ge(s_v, s_v.count)
        if PHASE < 5:
            return
        rtoks = []
        items = [(po, n) for po in range(DCH) for n in range(NQ)]
        for i, (po, n) in enumerate(items):
            if i >= 3:
                r.tensor.wait_ge(*rtoks[i - 3])
            ps = pb[i % 4]
            for kp in range(DCH):
                r.tensor.matmul(ps[:, :], wof.at(kp, 128 * po, 128 * po + 128),
                                u.at(kp, 512 * n, 512 * n + 512),
                                start=(kp == 0), stop=(kp == DCH - 1))
            mark(r.tensor, s_pe, 1)
            r.scalar.wait_ge(s_pe, s_pe.count)
            if i >= 2:
                r.scalar.wait_ge(*rtoks[i - 2])
            tmp = gsc.rows(0, P, 0, 512 * (i % 2), 512 * (i % 2) + 512)
            r.scalar.activation(out=tmp, in_=ps[:, :], func=AF.Identity, bias=bcol("wo_b", po))
            mark(r.scalar, s_sc, 1)
            r.vector.wait_ge(s_sc, s_sc.count)
            r.vector.tensor_tensor(out=xr[:, po, 512 * n:512 * n + 512],
                                   in0=xr[:, po, 512 * n:512 * n + 512], in1=tmp, op=ALU.add)
            mark(r.vector, s_v, 1)
            rtoks.append((s_v, s_v.count))

    def emit_l0f():
        AB.reset(AB_BASE)
        W = AB.take(3 * D, 2 * DCH)
        xsl = AB.take(T, 2)
        cT = Buf(xsl.t, xsl.off_b, 256, DCH, F32)
        stg = AB.take(512, 1, F32)
        xg = Buf(AB.t, 0, 2048, 2)
        cTbf = Buf(AB.t, 4096, 256, DCH)
        sq = Buf(AB.t, 0, T, 2)
        gates = Buf(gsc.t, gsc.off_b, 256, 3, F32)
        uT = Buf(W.t, W.off_b, T, DCH)
        wov = Buf(W.t, W.off_b + 16384, D, DCH)
        xre = Buf(W.t, W.off_b + 16384 + 8192, T, DCH)
        uov = Buf(W.t, W.off_b + 16384 + 8192, D, DCH)

        def load_w(dst_buf, slot, dram, kp, cols):
            for cc0 in range(0, cols, 512):
                r.sync.wait_ge(s_v, s_v.count)
                r.sync.dma_start(out=stg[0], in_=dram[kp, :, cc0:cc0 + 512]).then_inc(s_w, 16)
                r.vector.wait_ge(s_w, s_w.count)
                r.vector.tensor_copy(out=dst_buf.at(slot, cc0, cc0 + 512), in_=stg[0])
                mark(r.vector, s_v, 1)

        a2a_src = ds["cc_in"]
        def consume(p, slot):
            r.gpsimd.wait_ge(s_v, s_v.count)
            for j in range(NC):
                r.gpsimd.dma_start(out=a2a_src[j, p, :, :],
                                   in_=xsl.at(slot, 256 * j, 256 * j + 256)).then_inc(s_g, 16)
            r.gpsimd.dma_start(out=ds["xlnd"][p, :, :], in_=xsl.at(slot, 0, T)).then_inc(s_g, 16)
            r.vector.wait_ge(s_g, s_g.count)
        emit_ln(xsl, sq, consume=consume, nslots=2)
        r.gpsimd.collective_compute(
            "AllToAll", ALU.bypass, replica_groups=[list(range(NC))],
            ins=[a2a_src.tensor.ap().opt()], outs=[ds["cc_out"].tensor.ap().opt()],
        ).then_inc(s_cc, 1)
        r.sync.wait_ge(s_v, s_v.count)
        for p in range(DCH):
            r.sync.dma_start(out=cT[p], in_=dt["incb"][p, :, :]).then_inc(s_w, 16)
        c0_tok = s_w.count
        for kp in range(DCH):
            load_w(W, kp, dt["wfic"], kp, 3 * D)
        for kp in range(DCH):
            load_w(W, DCH + kp, dt["ufic"], kp, 3 * D)
        wtok = s_v.count
        r.vector.wait_ge(s_w, c0_tok)
        for p in range(DCH):
            r.vector.tensor_copy(out=cTbf[p], in_=cT[p])
        mark(r.vector, s_v, 1)
        r.gpsimd.wait_ge(s_cc, s_cc.count)
        GB = [0, 8, 16]
        for t in range(NC):
            r.gpsimd.wait_ge(s_pe, s_pe.count)
            for kp in range(DCH):
                r.gpsimd.dma_start(out=xg.at(t % 2, 256 * kp, 256 * kp + 256),
                                   in_=ds["cc_out"][t, kp, :, :]).then_inc(s_g, 16)
            xg_tok = s_g.count
            r.tensor.wait_ge(s_g, xg_tok)
            if t == 0:
                r.tensor.wait_ge(s_v, wtok)
                r.tensor.wait_ge(s_v, s_v.count)
            for p in range(DCH):
                for gi in range(3):
                    ps = pb[gi]
                    gcol = 1024 * gi + 128 * p
                    for kp in range(DCH):
                        r.tensor.matmul(ps[:, 0:256], W.at(kp, gcol, gcol + 128),
                                        xg.at(t % 2, 256 * kp, 256 * kp + 256),
                                        start=(kp == 0), stop=False)
                    for kp in range(DCH):
                        r.tensor.matmul(ps[:, 0:256], W.at(DCH + kp, gcol, gcol + 128),
                                        cTbf[kp], start=False, stop=(kp == DCH - 1))
                mark(r.tensor, s_pe, 1)
                gtok = s_pe.count
                r.scalar.wait_ge(s_pe, gtok)
                if True:
                    r.scalar.wait_ge(s_v, s_v.count)
                r.scalar.activation(out=gates[0], in_=pb[0][:, 0:256], func=AF.Sigmoid, bias=bcol("bfic", p))
                r.scalar.activation(out=gates[1], in_=pb[1][:, 0:256], func=AF.Sigmoid, bias=bcol("bfic", 8 + p))
                r.scalar.activation(out=gates[2], in_=pb[2][:, 0:256], func=AF.Tanh, bias=bcol("bfic", 16 + p))
                mark(r.scalar, s_sc, 1)
                atok = s_sc.count
                r.tensor.wait_ge(s_sc, atok)
                r.vector.wait_ge(s_sc, atok)
                r.vector.tensor_mul(cT[p], gates[0], cT[p])
                r.vector.tensor_mul(gates[1], gates[1], gates[2])
                r.vector.tensor_tensor(out=cT[p], in0=cT[p], in1=gates[1], op=ALU.add)
                mark(r.vector, s_v, 1)
            if t < NC - 1:
                r.vector.tensor_copy(out=cTbf.at(0, 0, 0 + 2048) if False else cTbf[0], in_=cT[0])
                for p in range(1, DCH):
                    r.vector.tensor_copy(out=cTbf[p], in_=cT[p])
                mark(r.vector, s_v, 1)
                r.tensor.wait_ge(s_v, s_v.count)
        r.vector.tensor_copy(out=cTbf[0], in_=cT[0])
        for p in range(1, DCH):
            r.vector.tensor_copy(out=cTbf[p], in_=cT[p])
        mark(r.vector, s_v, 1)
        cf_tok = s_v.count
        r.sync.wait_ge(s_pe, s_pe.count)
        for kp in range(DCH):
            load_w(uov, kp, dt["uovT"], kp, D)
        uloc = Buf(AB.t, 0, 256, DCH)
        r.tensor.wait_ge(s_v, s_v.count)
        r.tensor.wait_ge(s_v, cf_tok)
        for po in range(DCH):
            ps = pb[po % 2]
            for kp in range(DCH):
                r.tensor.matmul(ps[:, 0:256], uov.at(kp, 128 * po, 128 * po + 128),
                                cTbf[kp], start=(kp == 0), stop=(kp == DCH - 1))
            mark(r.tensor, s_pe, 1)
            r.vector.wait_ge(s_pe, s_pe.count)
            r.vector.tensor_copy(out=uloc[po], in_=ps[:, 0:256])
            mark(r.vector, s_v, 1)
            r.tensor.wait_ge(s_v, s_v.count)
        r.gpsimd.wait_ge(s_v, s_v.count)
        for kp in range(DCH):
            r.gpsimd.dma_start(out=ds["cc2_in"][kp, :, :], in_=uloc[kp]).then_inc(s_g, 16)
        r.gpsimd.wait_ge(s_g, s_g.count)
        r.gpsimd.collective_compute(
            "AllGather", ALU.bypass, replica_groups=[list(range(NC))],
            ins=[ds["cc2_in"].tensor.ap().opt()], outs=[ds["cc2_out"].tensor.ap().opt()],
        ).then_inc(s_cc, 1)
        for kp in range(DCH):
            load_w(wov, kp, dt["wovT"], kp, D)
        r.sync.wait_ge(s_v, s_v.count)
        for p in range(DCH):
            r.sync.dma_start(out=xre[p], in_=ds["xlnd"][p, :, :]).then_inc(s_w, 16)
        xre_tok = s_w.count
        r.gpsimd.wait_ge(s_cc, s_cc.count)
        for rr in range(NC):
            for po in range(DCH):
                r.gpsimd.dma_start(out=uT.at(po, 256 * rr, 256 * rr + 256),
                                   in_=ds["cc2_out"][rr * DCH + po, :, :]).then_inc(s_g, 16)
        ut_tok = s_g.count
        r.tensor.wait_ge(s_v, s_v.count)
        r.tensor.wait_ge(s_w, xre_tok)
        r.tensor.wait_ge(s_g, ut_tok)
        for po in range(DCH):
            for n in range(NQ):
                ps = pb[2 + (po * NQ + n) % 2]
                for kp in range(DCH):
                    r.tensor.matmul(ps[:, :], wov.at(kp, 128 * po, 128 * po + 128),
                                    xre.at(kp, 512 * n, 512 * n + 512),
                                    start=(kp == 0), stop=(kp == DCH - 1))
                mark(r.tensor, s_pe, 1)
                ptok = s_pe.count
                r.vector.wait_ge(s_pe, ptok)
                r.vector.tensor_tensor(out=ps[:, :], in0=ps[:, :],
                                       in1=uT.at(po, 512 * n, 512 * n + 512), op=ALU.add)
                mark(r.vector, s_v, 1)
                r.scalar.wait_ge(s_v, s_v.count)
                tmp = gsc.rows(0, P, 0, 512 * (n % 4), 512 * (n % 4) + 512)
                r.scalar.activation(out=tmp, in_=ps[:, :], func=AF.Tanh, bias=bcol("bo_f", po))
                mark(r.scalar, s_sc, 1)
                r.vector.wait_ge(s_sc, s_sc.count)
                r.vector.tensor_tensor(out=xr[:, po, 512 * n:512 * n + 512],
                                       in0=xr[:, po, 512 * n:512 * n + 512], in1=tmp, op=ALU.add)
                mark(r.vector, s_v, 1)
                r.tensor.wait_ge(s_v, s_v.count)
                r.scalar.wait_ge(s_v, s_v.count)


    def emit_l0b():
        AB.reset(AB_BASE)
        xln = AB.take(T, DCH)
        qb = AB.take(T, DCH)
        kb = AB.take(T, DCH)
        wsl = AB.take(1024, 2)
        pT = AB.take(512, 1)
        msk = AB.take(896, 1)
        bvl = AB.take(1024, 1)
        stgw = AB.take(1024, 1, F32)
        stg = Buf(stgw.t, stgw.off_b, 512, 1, F32)
        xti = AB.take(1024, 1)
        vb = Buf(xln.t, xln.off_b, 1024, TCH)

        r.sync.wait_ge(s_v, s_v.count)
        r.sync.dma_start(out=msk[0], in_=dt["maskmul"][:, 0:896]).then_inc(s_w, 16)
        r.sync.dma_start(out=bvl.rows(0, 1, 0), in_=dt["bvrow"][0:1, 0:D]).then_inc(s_w, 16)
        mtok = s_w.count
        addm = lambda di: gsc.rows(0, P, 0, 512 * di, 512 * di + 512)
        r.vector.wait_ge(s_w, mtok)

        def consume(p, slot):
            r.gpsimd.wait_ge(s_v, s_v.count)
            r.gpsimd.dma_start(out=ds["xlnd"][p, :, :], in_=xln[p]).then_inc(s_g, 16)
        ln_tok = emit_ln(xln, Buf(qb.t, qb.off_b, T, 2), consume=consume)
        r.tensor.wait_ge(s_v, ln_tok)
        r.sync.wait_ge(s_v, ln_tok)
        for di in range(4):
            r.vector.tensor_scalar(out=addm(di), in0=msk.rows(0, P, 0, 384 - 128 * di, 896 - 128 * di),
                                   scalar1=-1.0, scalar2=1e30, op0=ALU.add, op1=ALU.mult)
        mark(r.vector, s_v, 1)
        etoks = []
        stokb = [(s_v, 0)]
        for which, (gbuf, wname, bname) in enumerate(((qb, "wqbg", "bq"), (kb, "wkbg", "bk"))):
            for po in range(DCH):
                sl = po % 2
                r.sync.wait_ge(*stokb[0])
                r.sync.dma_start(out=stgw[0], in_=dt[wname][po, :, :]).then_inc(s_w, 16)
                r.vector.wait_ge(s_w, s_w.count)
                if which > 0 or po > 1:
                    r.vector.wait_ge(s_pe, s_pe.count)
                r.vector.tensor_copy(out=wsl[sl], in_=stgw[0])
                mark(r.vector, s_v, 1)
                stokb[0] = (s_v, s_v.count)
                r.tensor.wait_ge(s_v, s_v.count)
                for n in range(NQ):
                    i = len(etoks)
                    if i >= 3:
                        r.tensor.wait_ge(*etoks[i - 3])
                    ps = pb[i % 4]
                    for kp in range(DCH):
                        r.tensor.matmul(ps[:, :], wsl.at(sl, 128 * kp, 128 * kp + 128),
                                        xln.at(kp, 512 * n, 512 * n + 512),
                                        start=(kp == 0), stop=(kp == DCH - 1))
                    mark(r.tensor, s_pe, 1)
                    r.scalar.wait_ge(s_pe, s_pe.count)
                    r.scalar.activation(out=gbuf.at(po, 512 * n, 512 * n + 512), in_=ps[:, :],
                                        func=AF.Identity, bias=bcol(bname, po))
                    mark(r.scalar, s_sc, 1)
                    etoks.append((s_sc, s_sc.count))
        dtmp = dsb[:, 20:21]
        for jp in range(TCH):
            j0 = 128 * jp
            nic = NQ - j0 // 512
            for ici in range(nic):
                i0 = T - 512 * (ici + 1) if False else 512 * (NQ - nic + ici)
                ps = pb[4 + ici % 2]
                for kp in range(DCH):
                    r.tensor.matmul(ps[:, :], kb.at(kp, j0, j0 + 128), qb.at(kp, i0, i0 + 512),
                                    start=(kp == 0), stop=(kp == DCH - 1))
                mark(r.tensor, s_pe, 1)
                di = (j0 - i0) // 128
                if 0 <= di <= 3:
                    r.vector.wait_ge(s_pe, s_pe.count)
                    r.vector.tensor_tensor(out=ps[:, :], in0=ps[:, :], in1=addm(di), op=ALU.add)
                    mark(r.vector, s_v, 1)
                    r.scalar.wait_ge(s_v, s_v.count)
                r.scalar.wait_ge(s_pe, s_pe.count)
                acc = dsb[:, jp:jp + 1] if ici == 0 else dtmp
                r.scalar.activation(out=pT[0], in_=ps[:, :], func=AF.Exp,
                                    scale=0.03125, accum_out=acc)
                mark(r.scalar, s_sc, 1)
                r.tensor.wait_ge(s_sc, s_sc.count)
                if ici > 0:
                    r.vector.wait_ge(s_sc, s_sc.count)
                    r.vector.tensor_add(dsb[:, jp:jp + 1], dsb[:, jp:jp + 1], dtmp)
                    mark(r.vector, s_v, 1)
                    r.scalar.wait_ge(s_v, s_v.count)
        r.scalar.wait_ge(s_v, s_v.count)
        r.scalar.activation(out=dsb[:, 0:16], in_=dsb[:, 0:16], func=AF.Ln)
        mark(r.scalar, s_sc, 1)
        r.vector.wait_ge(s_sc, s_sc.count)
        r.vector.tensor_scalar(out=dsb[:, 0:16], in0=dsb[:, 0:16], scalar1=-1.0, scalar2=None, op0=ALU.mult)
        mark(r.vector, s_v, 1)
        for tc in range(TCH):
            sl = 0
            r.sync.wait_ge(s_pe, s_pe.count)
            for kp in range(DCH):
                r.sync.dma_start(out=xti.at(sl, 128 * kp, 128 * kp + 128),
                                 in_=ds["xlnd"][kp, :, 128 * tc:128 * tc + 128]).then_inc(s_w, 16)
            xt_tok = s_w.count
            for n in range(2):
                ps = pb[n % 2]
                for kp in range(DCH):
                    r.sync.wait_ge(s_v, s_v.count)
                    r.sync.dma_start(out=stg[0], in_=dt["wvb"][kp, :, 512 * n:512 * n + 512]).then_inc(s_w, 16)
                    r.vector.wait_ge(s_w, s_w.count)
                    r.vector.wait_ge(s_pe, s_pe.count)
                    r.vector.tensor_copy(out=wsl.at(0, 0, 512), in_=stg[0])
                    mark(r.vector, s_v, 1)
                    r.tensor.wait_ge(s_v, s_v.count)
                    r.tensor.wait_ge(s_w, xt_tok)
                    r.tensor.matmul(ps[:, :], xti.at(sl, 128 * kp, 128 * kp + 128),
                                    wsl.at(0, 0, 512), start=(kp == 0), stop=False)
                    mark(r.tensor, s_pe, 1)
                r.tensor.matmul(ps[:, :], ones_r[0:1, 0:128], bvl.rows(0, 1, 0, 512 * n, 512 * n + 512),
                                start=False, stop=True)
                mark(r.tensor, s_pe, 1)
                r.vector.wait_ge(s_pe, s_pe.count)
                r.vector.tensor_copy(out=vb.at(tc, 512 * n, 512 * n + 512), in_=ps[:, :])
                mark(r.vector, s_v, 1)
                r.tensor.wait_ge(s_v, s_v.count)
        for ici in range(NQ):
            i0 = 512 * ici
            npan = 4 * ici + 4
            for oh in range(2):
                for jp in range(npan):
                    j0 = 128 * jp
                    ps = pb[4 + jp % 2]
                    for kp in range(DCH):
                        r.tensor.matmul(ps[:, :], kb.at(kp, j0, j0 + 128), qb.at(kp, i0, i0 + 512),
                                        start=(kp == 0), stop=(kp == DCH - 1))
                    mark(r.tensor, s_pe, 1)
                    di = (j0 - i0) // 128
                    if 0 <= di <= 3:
                        r.vector.wait_ge(s_pe, s_pe.count)
                        r.vector.tensor_tensor(out=ps[:, :], in0=ps[:, :], in1=addm(di), op=ALU.add)
                        mark(r.vector, s_v, 1)
                        r.scalar.wait_ge(s_v, s_v.count)
                    r.scalar.wait_ge(s_pe, s_pe.count)
                    r.scalar.activation(out=pT[0], in_=ps[:, :], func=AF.Exp,
                                        scale=0.03125, bias=dsb[:, jp:jp + 1])
                    mark(r.scalar, s_sc, 1)
                    r.tensor.wait_ge(s_sc, s_sc.count)
                    for oc in range(4):
                        po = 4 * oh + oc
                        r.tensor.matmul(pb[oc][:, :], vb.at(jp, 128 * po, 128 * po + 128),
                                        pT[0], start=(jp == 0), stop=(jp == npan - 1))
                    mark(r.tensor, s_pe, 1)
                    r.scalar.wait_ge(s_pe, s_pe.count)
                for oc in range(4):
                    po = 4 * oh + oc
                    r.vector.wait_ge(s_pe, s_pe.count)
                    r.vector.tensor_tensor(out=xr[:, po, i0:i0 + 512],
                                           in0=xr[:, po, i0:i0 + 512], in1=pb[oc][:, :], op=ALU.add)
                mark(r.vector, s_v, 1)
                r.tensor.wait_ge(s_v, s_v.count)


    def emit_l1a():
        AB.reset(AB_BASE)
        xln = AB.take(T, DCH)
        y = AB.take(T, DCH)
        qp = AB.take(T, 1); kp_ = AB.take(T, 1); vp = AB.take(T, 1)
        t1 = AB.take(T, 1); t2 = AB.take(T, 1)
        kvc = AB.take(T, 1); kc = AB.take(T, 1)
        wsl = AB.take(1024, 3)
        stg = AB.take(512, 2, F32)
        wpr = Buf(xln.t, xln.off_b, D, DCH)
        den2 = ab

        def consume(p, slot):
            pass
        ln_tok = emit_ln(xln, Buf(y.t, y.off_b, T, 2))
        r.tensor.wait_ge(s_v, ln_tok)
        r.sync.wait_ge(s_v, ln_tok)
        for p in range(DCH):
            r.vector.wait_ge(s_pe, s_pe.count)
            for wi in range(3):
                c0 = 1024 * wi + 128 * p
                for kx in range(DCH):
                    r.sync.wait_ge(s_v, s_v.count)
                    r.sync.dma_start(out=stg.at(kx % 2, 0, 128), in_=dt["wqkv"][kx, :, c0:c0 + 128]).then_inc(s_w, 16)
                    r.vector.wait_ge(s_w, s_w.count)
                    r.vector.tensor_copy(out=wsl.at(wi, 128 * kx, 128 * kx + 128), in_=stg.at(kx % 2, 0, 128))
                    mark(r.vector, s_v, 1)
            wtok = s_v.count
            r.tensor.wait_ge(s_v, wtok)
            for wi, (dst, bj) in enumerate(((qp, p), (kp_, 8 + p), (vp, 16 + p))):
                for n in range(NQ):
                    ps = pb[n % 2]
                    for kx in range(DCH):
                        r.tensor.matmul(ps[:, :], wsl.at(wi, 128 * kx, 128 * kx + 128),
                                        xln.at(kx, 512 * n, 512 * n + 512),
                                        start=(kx == 0), stop=(kx == DCH - 1))
                    mark(r.tensor, s_pe, 1)
                    r.scalar.wait_ge(s_pe, s_pe.count)
                    r.scalar.activation(out=dst.at(0, 512 * n, 512 * n + 512), in_=ps[:, :],
                                        func=AF.Identity, bias=bcol("bqkv", bj))
                    mark(r.scalar, s_sc, 1)
                    r.tensor.wait_ge(s_sc, s_sc.count)
            for buf in (qp, kp_):
                r.vector.wait_ge(s_sc, s_sc.count)
                r.vector.tensor_scalar(out=gsc[0], in0=buf[0], scalar1=sclb[:, :], scalar2=0.0,
                                       op0=ALU.mult, op1=ALU.min)
                mark(r.vector, s_v, 1)
                r.scalar.wait_ge(s_v, s_v.count)
                r.scalar.activation(out=t1[0], in_=gsc[0], func=AF.Exp)
                r.scalar.activation(out=t2[0], in_=buf[0], func=AF.Relu, scale=sclb[:, :])
                mark(r.scalar, s_sc, 1)
                r.vector.wait_ge(s_sc, s_sc.count)
                r.vector.tensor_add(buf[0], t1[0], t2[0])
                mark(r.vector, s_v, 1)
            r.vector.tensor_mul(t1[0], kp_[0], vp[0])
            r.vector.tensor_tensor_scan(out=kvc[0], data0=ones_t[:, :], data1=t1[0],
                                        initial=0.0, op0=ALU.mult, op1=ALU.add).then_inc(s_v, 1)
            r.vector.wait_ge(s_v, s_v.count)
            r.vector.tensor_tensor_scan(out=kc[0], data0=ones_t[:, :], data1=kp_[0],
                                        initial=0.0, op0=ALU.mult, op1=ALU.add).then_inc(s_v, 1)
            r.vector.wait_ge(s_v, s_v.count)
            r.vector.tensor_mul(t1[0], qp[0], kc[0])
            mark(r.vector, s_v, 1)
            r.tensor.wait_ge(s_v, s_v.count)
            for n in range(NQ):
                r.tensor.matmul(pb[4][0:2, :], hsum[:, :], t1.at(0, 512 * n, 512 * n + 512),
                                start=True, stop=True)
                mark(r.tensor, s_pe, 1)
                r.vector.wait_ge(s_pe, s_pe.count)
                r.vector.tensor_copy(out=den2.rows(0, 2, 0, 512 * n, 512 * n + 512), in_=pb[4][0:2, :])
                mark(r.vector, s_v, 1)
                r.tensor.wait_ge(s_v, s_v.count)
            r.vector.tensor_scalar(out=den2.rows(0, 2, 0), in0=den2.rows(0, 2, 0),
                                   scalar1=0.001, scalar2=None, op0=ALU.max)
            r.vector.reciprocal(out=den2.rows(0, 2, 0), in_=den2.rows(0, 2, 0)).then_inc(s_v, 1)
            r.vector.wait_ge(s_v, s_v.count)
            r.vector.engine_nop().then_inc(s_v, 1)
            r.gpsimd.wait_ge(s_v, s_v.count)
            r.gpsimd.dma_start(out=ds["den_dram"][0:2, :], in_=den2.rows(0, 2, 0)).then_inc(s_g, 16)
            r.gpsimd.wait_ge(s_g, s_g.count)
            r.gpsimd.dma_start(out=gsc.rows(0, 64, 0), in_=ds["den_dram"][0:1, :].broadcast_to((64, T))).then_inc(s_g, 16)
            r.gpsimd.dma_start(out=gsc.rows(64, 128, 0), in_=ds["den_dram"][1:2, :].broadcast_to((64, T))).then_inc(s_g, 16)
            r.vector.tensor_mul(t2[0], qp[0], kvc[0])
            r.vector.wait_ge(s_g, s_g.count)
            r.vector.tensor_mul(y[p], t2[0], gsc[0])
            mark(r.vector, s_v, 1)
            r.gpsimd.wait_ge(s_v, s_v.count)
        for kx in range(DCH):
            for half in range(2):
                r.sync.wait_ge(s_v, s_v.count)
                r.sync.dma_start(out=stg[0], in_=dt["wprj"][kx, :, 512 * half:512 * half + 512]).then_inc(s_w, 16)
                r.vector.wait_ge(s_w, s_w.count)
                r.vector.tensor_copy(out=wpr.at(kx, 512 * half, 512 * half + 512), in_=stg[0])
                mark(r.vector, s_v, 1)
        r.tensor.wait_ge(s_v, s_v.count)
        for po in range(DCH):
            for n in range(NQ):
                ps = pb[(po * NQ + n) % 2]
                for kx in range(DCH):
                    r.tensor.matmul(ps[:, :], wpr.at(kx, 128 * po, 128 * po + 128),
                                    y.at(kx, 512 * n, 512 * n + 512),
                                    start=(kx == 0), stop=(kx == DCH - 1))
                mark(r.tensor, s_pe, 1)
                r.scalar.wait_ge(s_pe, s_pe.count)
                tmp = gsc.rows(0, P, 0, 512 * (n % 4), 512 * (n % 4) + 512)
                r.scalar.activation(out=tmp, in_=ps[:, :], func=AF.Identity, bias=bcol("bproj", po))
                mark(r.scalar, s_sc, 1)
                r.vector.wait_ge(s_sc, s_sc.count)
                r.vector.tensor_tensor(out=xr[:, po, 512 * n:512 * n + 512],
                                       in0=xr[:, po, 512 * n:512 * n + 512], in1=tmp, op=ALU.add)
                mark(r.vector, s_v, 1)
                r.tensor.wait_ge(s_v, s_v.count)
                r.scalar.wait_ge(s_v, s_v.count)


    def emit_l1f(first_call):
        AB.reset(AB_BASE)
        xln = AB.take(T, DCH)
        h = AB.take(512, 32)
        wqsl = AB.take(1024, 2)
        wpsl = AB.take(1024, 2)
        mid = AB.take(T, 1)
        midp = AB.take(512, 1)
        bfc = AB.take(4096, 1)
        bpr = AB.take(1024, 1)
        afc = AB.take(16 * DCH, 1)
        apr = AB.take(16 * 32, 1)
        stg = AB.take(512, 2, F32)
        qst = Buf(h.t, h.off_b, 4096, 1, F32)
        qout = Buf(h.t, h.off_b + 8192, 4096, 1)
        qi32 = Buf(h.t, h.off_b + 12288, 2048, 1, F32)

        def consume(p, slot):
            pass
        ln_tok = emit_ln(xln, Buf(h.t, h.off_b, T, 2))
        r.sync.wait_ge(s_v, ln_tok)
        r.tensor.wait_ge(s_v, ln_tok)

        for piece in range(8):
            r.sync.wait_ge(s_v, s_v.count)
            r.sync.dma_start(out=stg.rows(0, 16, 0), in_=dt["bfcT"][0, :, 512 * piece:512 * piece + 512]).then_inc(s_w, 16)
            r.vector.wait_ge(s_w, s_w.count)
            r.vector.tensor_copy(out=bfc.rows(0, 16, 0, 512 * piece, 512 * piece + 512), in_=stg.rows(0, 16, 0))
            mark(r.vector, s_v, 1)
        for piece in range(2):
            r.sync.wait_ge(s_v, s_v.count)
            r.sync.dma_start(out=stg.rows(0, 16, 0), in_=dt["bprT"][0, :, 512 * piece:512 * piece + 512]).then_inc(s_w, 16)
            r.vector.wait_ge(s_w, s_w.count)
            r.vector.tensor_copy(out=bpr.rows(0, 16, 0, 512 * piece, 512 * piece + 512), in_=stg.rows(0, 16, 0))
            mark(r.vector, s_v, 1)
        for kx in range(DCH):
            r.sync.wait_ge(s_v, s_v.count)
            r.sync.dma_start(out=stg.at(0, 0, 16), in_=dt["afcT"][kx, :, :]).then_inc(s_w, 16)
            r.vector.wait_ge(s_w, s_w.count)
            r.vector.tensor_copy(out=afc.at(0, 16 * kx, 16 * kx + 16), in_=stg.at(0, 0, 16))
            mark(r.vector, s_v, 1)
        for fk in range(32):
            r.sync.wait_ge(s_v, s_v.count)
            r.sync.dma_start(out=stg.at(0, 0, 16), in_=dt["aprT"][fk, :, :]).then_inc(s_w, 16)
            r.vector.wait_ge(s_w, s_w.count)
            r.vector.tensor_copy(out=apr.at(0, 16 * fk, 16 * fk + 16), in_=stg.at(0, 0, 16))
            mark(r.vector, s_v, 1)

        if first_call:
            for wname, nch, cols, dst, oc_layout in (("wfcT", DCH, 4096, "wqfc_dram", True),
                                                     ("wprT", 32, D, "wqpr_dram", False)):
                for kx in range(nch):
                    r.gpsimd.wait_ge(s_v, s_v.count)
                    r.gpsimd.dma_start(out=qst.at(0, 0, cols), in_=dt[wname][kx, :, :]).then_inc(s_g, 16)
                    r.vector.wait_ge(s_g, s_g.count)
                    r.vector.reduce_max(out=dsb[:, 24 + kx % 2:25 + kx % 2], in_=qst.at(0, 0, cols),
                                        axis=mybir.AxisListType.X, apply_absolute_value=True).then_inc(s_v, 1)
                    r.vector.wait_ge(s_v, s_v.count)
                    if kx == 0:
                        r.vector.tensor_copy(out=dsb[:, 26:27], in_=dsb[:, 24:25])
                    else:
                        r.vector.tensor_max(dsb[:, 26:27], dsb[:, 26:27], dsb[:, 24 + kx % 2:25 + kx % 2])
                    mark(r.vector, s_v, 1)
                    r.gpsimd.wait_ge(s_v, s_v.count)
                r.gpsimd.tensor_reduce(out=dsb[0:1, 27:28], in_=dsb[:, 26:27],
                                       axis=mybir.AxisListType.C, op=ALU.max).then_inc(s_gc, 1)
                r.vector.wait_ge(s_gc, s_gc.count)
                r.vector.reciprocal(out=dsb[0:1, 28:29], in_=dsb[0:1, 27:28]).then_inc(s_v, 1)
                r.vector.wait_ge(s_v, s_v.count)
                r.vector.tensor_scalar(out=dsb[0:1, 28:29], in0=dsb[0:1, 28:29], scalar1=127.0, scalar2=None, op0=ALU.mult)
                r.vector.tensor_scalar(out=dsb[0:1, 29:30], in0=dsb[0:1, 27:28], scalar1=1.0 / 127.0, scalar2=None, op0=ALU.mult)
                mark(r.vector, s_v, 1)
                r.gpsimd.wait_ge(s_v, s_v.count)
                r.gpsimd.dma_start(out=ds["den_dram"][0:1, 0:2], in_=dsb[0:1, 28:30]).then_inc(s_g, 16)
                r.gpsimd.wait_ge(s_g, s_g.count)
                r.gpsimd.dma_start(out=dsb[:, 30:31], in_=ds["den_dram"][0:1, 0:1].broadcast_to((P, 1))).then_inc(s_g, 16)
                r.gpsimd.dma_start(out=dsb[:, 31:32], in_=ds["den_dram"][0:1, 1:2].broadcast_to((P, 1))).then_inc(s_g, 16)
                r.vector.wait_ge(s_g, s_g.count)
                for kx in range(nch):
                    r.gpsimd.wait_ge(s_v, s_v.count)
                    r.gpsimd.dma_start(out=qst.at(0, 0, cols), in_=dt[wname][kx, :, :]).then_inc(s_g, 16)
                    r.vector.wait_ge(s_g, s_g.count)
                    for pc in range(0, cols, 1024):
                        y32 = qst.at(0, pc, pc + 1024)
                        qi = qi32.at(0, 0, 1024).bitcast(I32) if False else Buf(qi32.t, qi32.off_b, 1024, 1, F32).at(0, 0, 1024).bitcast(I32)
                        r.vector.tensor_scalar(out=y32, in0=y32, scalar1=dsb[:, 30:31], scalar2=None, op0=ALU.mult)
                        r.vector.tensor_copy(out=qi, in_=y32)
                        r.vector.tensor_scalar(out=qout.at(0, pc, pc + 1024), in0=qi,
                                               scalar1=-128.0, scalar2=127.0, op0=ALU.max, op1=ALU.min)
                        r.vector.tensor_scalar(out=qout.at(0, pc, pc + 1024), in0=qout.at(0, pc, pc + 1024),
                                               scalar1=dsb[:, 31:32], scalar2=None, op0=ALU.mult)
                    mark(r.vector, s_v, 1)
                    r.gpsimd.wait_ge(s_v, s_v.count)
                    if oc_layout:
                        for oc in range(32):
                            r.gpsimd.dma_start(out=ds[dst][oc, :, 128 * kx:128 * kx + 128],
                                               in_=qout.at(0, 128 * oc, 128 * oc + 128)).then_inc(s_g, 16)
                    else:
                        r.gpsimd.dma_start(out=ds[dst][kx, :, :], in_=qout.at(0, 0, D)).then_inc(s_g, 16)
                    r.vector.wait_ge(s_g, s_g.count)

        r.tensor.wait_ge(s_v, s_v.count)
        for n in range(NQ):
            ps = pb[2]
            for kx in range(DCH):
                r.tensor.matmul(ps[0:16, :], afc.at(0, 16 * kx, 16 * kx + 16),
                                xln.at(kx, 512 * n, 512 * n + 512),
                                start=(kx == 0), stop=(kx == DCH - 1))
            mark(r.tensor, s_pe, 1)
            r.scalar.wait_ge(s_pe, s_pe.count)
            r.scalar.activation(out=mid.rows(0, 16, 0, 512 * n, 512 * n + 512), in_=ps[0:16, :],
                                func=AF.Identity, scale=2.0)
            mark(r.scalar, s_sc, 1)
            r.tensor.wait_ge(s_sc, s_sc.count)
        for tc in range(NQ):
            t0 = 512 * tc
            for oc in range(32):
                r.sync.wait_ge(s_pe, s_pe.count)
                r.sync.dma_start(out=wqsl[oc % 2], in_=ds["wqfc_dram"][oc, :, :]).then_inc(s_w, 16)
                r.tensor.wait_ge(s_w, s_w.count)
                ps = pb[oc % 2]
                if tc > 0 or oc > 1:
                    r.tensor.wait_ge(s_sc, s_sc.count)
                for kx in range(DCH):
                    r.tensor.matmul(ps[:, :], wqsl.at(oc % 2, 128 * kx, 128 * kx + 128),
                                    xln.at(kx, t0, t0 + 512), start=(kx == 0), stop=False)
                r.tensor.matmul(ps[:, :], bfc.rows(0, 16, 0, 128 * oc, 128 * oc + 128),
                                mid.rows(0, 16, 0, t0, t0 + 512), start=False, stop=True)
                mark(r.tensor, s_pe, 1)
                r.scalar.wait_ge(s_pe, s_pe.count)
                r.scalar.activation(out=h[oc], in_=ps[:, :], func=AF.Gelu, bias=bcol("b_fc", oc))
                mark(r.scalar, s_sc, 1)
            r.tensor.wait_ge(s_sc, s_sc.count)
            ps = pb[2]
            for fk in range(32):
                r.tensor.matmul(ps[0:16, :], apr.at(0, 16 * fk, 16 * fk + 16), h[fk],
                                start=(fk == 0), stop=(fk == 31))
            mark(r.tensor, s_pe, 1)
            r.scalar.wait_ge(s_pe, s_pe.count)
            r.scalar.activation(out=midp.rows(0, 16, 0), in_=ps[0:16, :], func=AF.Identity, scale=2.0)
            mark(r.scalar, s_sc, 1)
            r.tensor.wait_ge(s_sc, s_sc.count)
            for fk in range(32):
                r.sync.wait_ge(s_pe, s_pe.count)
                r.sync.dma_start(out=wpsl[fk % 2], in_=ds["wqpr_dram"][fk, :, :]).then_inc(s_w, 16)
                r.tensor.wait_ge(s_w, s_w.count)
                if fk < 2 and tc > 0:
                    r.tensor.wait_ge(s_v, s_v.count)
                for oc2 in range(DCH):
                    r.tensor.matmul(pb[oc2][:, :], wpsl.at(fk % 2, 128 * oc2, 128 * oc2 + 128),
                                    h[fk], start=(fk == 0), stop=False)
                mark(r.tensor, s_pe, 1)
            for oc2 in range(DCH):
                r.tensor.matmul(pb[oc2][:, :], bpr.rows(0, 16, 0, 128 * oc2, 128 * oc2 + 128),
                                midp.rows(0, 16, 0), start=False, stop=True)
            mark(r.tensor, s_pe, 1)
            ptok = s_pe.count
            for oc2 in range(DCH):
                r.scalar.wait_ge(s_pe, ptok)
                tmp = gsc.rows(0, P, 0, 512 * (oc2 % 4), 512 * (oc2 % 4) + 512)
                if oc2 >= 4:
                    r.scalar.wait_ge(s_v, s_v.count)
                r.scalar.activation(out=tmp, in_=pb[oc2][:, :], func=AF.Identity, bias=bcol("b_proj", oc2))
                mark(r.scalar, s_sc, 1)
                r.vector.wait_ge(s_sc, s_sc.count)
                r.vector.tensor_tensor(out=xr[:, oc2, t0:t0 + 512],
                                       in0=xr[:, oc2, t0:t0 + 512], in1=tmp, op=ALU.add)
                mark(r.vector, s_v, 1)
            r.tensor.wait_ge(s_v, s_v.count)
            r.scalar.wait_ge(s_v, s_v.count)


    def emit_l1b():
        AB.reset(AB_BASE)
        xln = AB.take(T, DCH)
        lat = AB.take(T, 2)
        kbf = Buf(xln.t, xln.off_b, T, DCH)
        vch = 65
        vaug = AB.take(16 * vch, TCH)
        wsl = AB.take(1024, 2)
        stg = AB.take(512, 2, F32)
        pT = AB.take(512, 2)
        qsl = AB.take(512, 2)
        uch = AB.take(T, 2)
        bvl = AB.take(1024, 1)

        def consume(p, slot):
            r.gpsimd.wait_ge(s_v, s_v.count)
            r.gpsimd.dma_start(out=ds["xlnd"][p, :, :], in_=xln[p]).then_inc(s_g, 16)
        ln_tok = emit_ln(xln, Buf(vaug.t, vaug.off_b, T, 2), consume=consume)
        r.tensor.wait_ge(s_v, ln_tok)
        r.sync.wait_ge(s_v, ln_tok)
        r.sync.dma_start(out=bvl.rows(0, 1, 0), in_=dt["bvrow"][0:1, D:2 * D]).then_inc(s_w, 16)
        for po in range(DCH):
            sl = po % 2
            r.vector.wait_ge(s_pe, s_pe.count)
            for kx in range(DCH):
                r.sync.wait_ge(s_v, s_v.count)
                r.sync.dma_start(out=stg.at(kx % 2, 0, 128), in_=dt["wqB"][kx, :, 128 * po:128 * po + 128]).then_inc(s_w, 16)
                r.vector.wait_ge(s_w, s_w.count)
                r.vector.tensor_copy(out=wsl.at(sl, 128 * kx, 128 * kx + 128), in_=stg.at(kx % 2, 0, 128))
                mark(r.vector, s_v, 1)
            r.tensor.wait_ge(s_v, s_v.count)
            for n in range(NQ):
                ps = pb[n % 2]
                for kx in range(DCH):
                    r.tensor.matmul(ps[:, :], wsl.at(sl, 128 * kx, 128 * kx + 128),
                                    xln.at(kx, 512 * n, 512 * n + 512),
                                    start=(kx == 0), stop=(kx == DCH - 1))
                mark(r.tensor, s_pe, 1)
                r.scalar.wait_ge(s_pe, s_pe.count)
                r.scalar.wait_ge(s_s, s_s.count)
                r.scalar.activation(out=qsl[n % 2], in_=ps[:, :], func=AF.Identity, bias=bcol("Bq", po))
                mark(r.scalar, s_sc, 1)
                r.tensor.wait_ge(s_sc, s_sc.count)
                r.scalar.dma_start(out=ds["q_dram"][po, :, 512 * n:512 * n + 512], in_=qsl[n % 2]).then_inc(s_s, 16)
        for po in range(2):
            r.vector.wait_ge(s_pe, s_pe.count)
            for kx in range(DCH):
                r.sync.wait_ge(s_v, s_v.count)
                r.sync.dma_start(out=stg.at(kx % 2, 0, 128), in_=dt["wlB"][kx, :, 128 * po:128 * po + 128]).then_inc(s_w, 16)
                r.vector.wait_ge(s_w, s_w.count)
                r.vector.tensor_copy(out=wsl.at(po % 2, 128 * kx, 128 * kx + 128), in_=stg.at(kx % 2, 0, 128))
                mark(r.vector, s_v, 1)
            r.tensor.wait_ge(s_v, s_v.count)
            for n in range(NQ):
                ps = pb[n % 2]
                for kx in range(DCH):
                    r.tensor.matmul(ps[:, :], wsl.at(po % 2, 128 * kx, 128 * kx + 128),
                                    xln.at(kx, 512 * n, 512 * n + 512),
                                    start=(kx == 0), stop=(kx == DCH - 1))
                mark(r.tensor, s_pe, 1)
                r.scalar.wait_ge(s_pe, s_pe.count)
                r.scalar.activation(out=lat.at(po, 512 * n, 512 * n + 512), in_=ps[:, :],
                                    func=AF.Identity, bias=bcol("Bl", po))
                mark(r.scalar, s_sc, 1)
                r.tensor.wait_ge(s_sc, s_sc.count)
        r.sync.wait_ge(s_sc, s_sc.count)
        r.vector.wait_ge(s_pe, s_pe.count)
        for kx in range(2):
            r.sync.wait_ge(s_v, s_v.count)
            r.sync.dma_start(out=stg[0], in_=dt["wkB"][kx, :, 0:512]).then_inc(s_w, 16)
            r.sync.dma_start(out=stg[1], in_=dt["wkB"][kx, :, 512:1024]).then_inc(s_w, 16)
            r.vector.wait_ge(s_w, s_w.count)
            r.vector.tensor_copy(out=wsl.at(kx, 0, 512), in_=stg[0])
            r.vector.tensor_copy(out=wsl.at(kx, 512, 1024), in_=stg[1])
            mark(r.vector, s_v, 1)
            r.sync.wait_ge(s_v, s_v.count)
        r.tensor.wait_ge(s_v, s_v.count)
        for po in range(DCH):
            for n in range(NQ):
                ps = pb[n % 2]
                for kx in range(2):
                    r.tensor.matmul(ps[:, :], wsl.at(kx, 128 * po, 128 * po + 128),
                                    lat.at(kx, 512 * n, 512 * n + 512),
                                    start=(kx == 0), stop=(kx == 1))
                mark(r.tensor, s_pe, 1)
                r.scalar.wait_ge(s_pe, s_pe.count)
                r.scalar.activation(out=kbf.at(po, 512 * n, 512 * n + 512), in_=ps[:, :],
                                    func=AF.Identity, bias=bcol("Bk", po))
                mark(r.scalar, s_sc, 1)
                r.tensor.wait_ge(s_sc, s_sc.count)
        r.vector.wait_ge(s_pe, s_pe.count)
        for kx in range(2):
            r.sync.wait_ge(s_v, s_v.count)
            r.sync.dma_start(out=stg[0], in_=dt["wvB"][kx, :, 0:512]).then_inc(s_w, 16)
            r.sync.dma_start(out=stg[1], in_=dt["wvB"][kx, :, 512:1024]).then_inc(s_w, 16)
            r.vector.wait_ge(s_w, s_w.count)
            r.vector.tensor_copy(out=wsl.at(kx, 0, 512), in_=stg[0])
            r.vector.tensor_copy(out=wsl.at(kx, 512, 1024), in_=stg[1])
            mark(r.vector, s_v, 1)
            r.sync.wait_ge(s_v, s_v.count)
        for tc in range(TCH):
            for hh in range(16):
                r.vector.tensor_copy(out=vaug.at(tc, vch * hh + 64, vch * hh + 65), in_=ones_t[:, 0:1])
        mark(r.vector, s_v, 1)
        r.tensor.wait_ge(s_v, s_v.count)
        r.tensor.wait_ge(s_w, s_w.count)
        for tc in range(TCH):
            for n in range(2):
                ps = pb[2 + n]
                for kx in range(2):
                    r.tensor.matmul(ps[:, :], lat.rows(0, P, kx, 128 * tc, 128 * tc + 128),
                                    wsl.at(kx, 512 * n, 512 * n + 512), start=(kx == 0), stop=False)
                r.tensor.matmul(ps[:, :], ones_r[0:1, 0:128], bvl.rows(0, 1, 0, 512 * n, 512 * n + 512),
                                start=False, stop=True)
                mark(r.tensor, s_pe, 1)
                r.vector.wait_ge(s_pe, s_pe.count)
                for hh in range(8):
                    h = 8 * n + hh
                    r.vector.tensor_copy(out=vaug.at(tc, vch * h, vch * h + 64), in_=ps[:, 64 * hh:64 * hh + 64])
                mark(r.vector, s_v, 1)
                r.tensor.wait_ge(s_v, s_v.count)
        if L1BPHASE < 2:
            return
        for h in range(H):
            hp, ho = h // 2, 64 * (h % 2)
            for qc in range(NQ):
                q0 = 512 * qc
                r.scalar.wait_ge(s_s, s_s.count)
                r.scalar.dma_start(out=qsl.rows(ho, ho + 64, qc % 2),
                                   in_=ds["q_dram"][hp, ho:ho + 64, q0:q0 + 512]).then_inc(s_s, 16)
                q_tok = s_s.count
                pout = pb[4 + qc % 2]
                r.tensor.wait_ge(s_s, q_tok)
                for jp in range(TCH):
                    j0 = 128 * jp
                    ps = pb[6 + jp % 2]
                    r.tensor.matmul(ps[:, :], kbf.rows(ho, ho + 64, hp, j0, j0 + 128),
                                    qsl.rows(ho, ho + 64, qc % 2), start=True, stop=True)
                    mark(r.tensor, s_pe, 1)
                    pt = pT[jp % 2]
                    r.scalar.wait_ge(s_pe, s_pe.count)
                    r.scalar.activation(out=pt, in_=ps[:, :], func=AF.Exp, scale=0.125)
                    mark(r.scalar, s_sc, 1)
                    r.tensor.wait_ge(s_sc, s_sc.count)
                    r.tensor.matmul(pout[0:65, :], vaug.at(jp, vch * h, vch * h + 65),
                                    pt, start=(jp == 0), stop=(jp == TCH - 1))
                    mark(r.tensor, s_pe, 1)
                    r.scalar.wait_ge(s_pe, s_pe.count)
                r.vector.wait_ge(s_pe, s_pe.count)
                r.vector.tensor_copy(out=uch.rows(0, 64, qc % 2, 0, 512), in_=pout[0:64, :])
                r.vector.tensor_copy(out=dsb[64:65, 0:512], in_=pout[64:65, :])
                mark(r.vector, s_v, 1)
                r.tensor.wait_ge(s_v, s_v.count)
                r.gpsimd.wait_ge(s_v, s_v.count)
                r.gpsimd.dma_start(out=ds["xlnd"][hp, ho:ho + 64, q0:q0 + 512],
                                   in_=uch.rows(0, 64, qc % 2, 0, 512)).then_inc(s_g, 16)
                r.scalar.wait_ge(s_v, s_v.count)
                r.scalar.dma_start(out=ds["den_dram"][h:h + 1, q0:q0 + 512], in_=dsb[64:65, 0:512]).then_inc(s_s, 16)
                r.vector.wait_ge(s_s, s_s.count)
                r.vector.wait_ge(s_g, s_g.count)
        if L1BPHASE < 3:
            return
        r.gpsimd.wait_ge(s_s, s_s.count)
        for n in range(NQ):
            c0 = 512 * n
            if n > 0:
                r.gpsimd.wait_ge(s_v, s_v.count)
            r.gpsimd.dma_start(out=dsb[0:16, 0:512], in_=ds["den_dram"][0:16, c0:c0 + 512]).then_inc(s_g, 16)
            r.vector.wait_ge(s_g, s_g.count)
            r.vector.reciprocal(out=dsb[0:16, 0:512], in_=dsb[0:16, 0:512]).then_inc(s_v, 1)
            r.vector.wait_ge(s_v, s_v.count)
            r.vector.engine_nop().then_inc(s_v, 1)
            r.gpsimd.wait_ge(s_v, s_v.count)
            r.gpsimd.dma_start(out=ds["den_dram"][0:16, c0:c0 + 512], in_=dsb[0:16, 0:512]).then_inc(s_g, 16)
        r.gpsimd.wait_ge(s_g, s_g.count)
        uh = Buf(xln.t, xln.off_b, T, DCH)
        for kx in range(DCH):
            r.gpsimd.wait_ge(s_v, s_v.count)
            r.gpsimd.dma_start(out=uch[kx % 2], in_=ds["xlnd"][kx, :, :]).then_inc(s_g, 16)
            r.gpsimd.dma_start(out=gsc.rows(0, 64, 0), in_=ds["den_dram"][2 * kx:2 * kx + 1, :].broadcast_to((64, T))).then_inc(s_g, 16)
            r.gpsimd.dma_start(out=gsc.rows(64, 128, 0), in_=ds["den_dram"][2 * kx + 1:2 * kx + 2, :].broadcast_to((64, T))).then_inc(s_g, 16)
            r.vector.wait_ge(s_g, s_g.count)
            r.vector.tensor_mul(uh[kx], uch[kx % 2], gsc[0])
            mark(r.vector, s_v, 1)
        for kx in range(DCH):
            for half in range(2):
                r.sync.wait_ge(s_v, s_v.count)
                r.sync.dma_start(out=stg[0], in_=dt["woB"][kx, :, 512 * half:512 * half + 512]).then_inc(s_w, 16)
                r.vector.wait_ge(s_w, s_w.count)
                r.vector.tensor_copy(out=wsl.at(kx % 2, 512 * half, 512 * half + 512), in_=stg[0]) if False else None
                r.vector.tensor_copy(out=vaug.at(kx * 2 + half, 0, 512) if False else Buf(vaug.t, vaug.off_b, 1024, TCH).at(kx, 512 * half, 512 * half + 512), in_=stg[0])
                mark(r.vector, s_v, 1)
        wof2 = Buf(vaug.t, vaug.off_b, 1024, TCH)
        r.tensor.wait_ge(s_v, s_v.count)
        for po in range(DCH):
            for n in range(NQ):
                ps = pb[(po * NQ + n) % 2]
                for kx in range(DCH):
                    r.tensor.matmul(ps[:, :], wof2.at(kx, 128 * po, 128 * po + 128),
                                    uh.at(kx, 512 * n, 512 * n + 512),
                                    start=(kx == 0), stop=(kx == DCH - 1))
                mark(r.tensor, s_pe, 1)
                r.scalar.wait_ge(s_pe, s_pe.count)
                tmp = gsc.rows(0, P, 0, 512 * (n % 4), 512 * (n % 4) + 512)
                r.scalar.activation(out=tmp, in_=ps[:, :], func=AF.Identity, bias=bcol("Bo", po))
                mark(r.scalar, s_sc, 1)
                r.vector.wait_ge(s_sc, s_sc.count)
                r.vector.tensor_tensor(out=xr[:, po, 512 * n:512 * n + 512],
                                       in0=xr[:, po, 512 * n:512 * n + 512], in1=tmp, op=ALU.add)
                mark(r.vector, s_v, 1)
                r.tensor.wait_ge(s_v, s_v.count)
                r.scalar.wait_ge(s_v, s_v.count)

    if ONLY_SUB is not None:
        {1: emit_l0a, 2: emit_l0f, 3: emit_l0b, 4: emit_l0f, 5: emit_l1a,
         6: lambda: emit_l1f(True), 7: emit_l1b, 8: lambda: emit_l1f(True)}[ONLY_SUB]()
        barrier()
        for p in range(DCH):
            r.gpsimd.dma_start(out=yT[p, :, :], in_=xr[:, p, :]).then_inc(s_g, 16)
        r.gpsimd.wait_ge(s_g, s_g.count)
        return
    if n_sub >= 1:
        emit_l0a()
        barrier()
    if n_sub >= 2:
        emit_l0f()
        barrier()
    if n_sub >= 3:
        emit_l0b()
        barrier()
    if n_sub >= 4:
        emit_l0f()
        barrier()
    if n_sub >= 5:
        emit_l1a()
        barrier()
    if n_sub >= 6:
        emit_l1f(True)
        barrier()
    if n_sub >= 7:
        emit_l1b()
        barrier()
    if n_sub >= 8:
        emit_l1f(False)
        barrier()

    for p in range(DCH):
        r.gpsimd.dma_start(out=yT[p, :, :], in_=xr[:, p, :]).then_inc(s_g, 16)
    if DBG_COLS is not None:
        r.gpsimd.dma_start(out=env["ydbg"][:, 0:DBG_COLS[1] - DBG_COLS[0]],
                           in_=AB.t[:, DBG_COLS[0]:DBG_COLS[1]]).then_inc(s_g, 16)
    r.gpsimd.wait_ge(s_g, s_g.count)
    if DBG_OVERWAIT:
        r.gpsimd.wait_ge(s_w, s_w.count + 16)


def _prep_inputs(kw):
    x = np.asarray(kw["L0_attn_a_x"])
    perm = _perm_eo()
    cosT, sinT = _rope_tables()
    mm = _masks()

    def T_(a):
        return np.ascontiguousarray(np.asarray(a).T)

    com = {}
    com["wq0"] = _wchunks(np.asarray(kw["L0_attn_a_wq"])[:, perm])
    com["wk0"] = _wchunks(np.asarray(kw["L0_attn_a_wk"])[:, perm])
    com["wv0"] = _wchunks(np.asarray(kw["L0_attn_a_wv"]))
    com["wo0"] = _wchunks(np.asarray(kw["L0_attn_a_wo_w"]))
    com["wqkg"] = np.ascontiguousarray(np.concatenate(
        [_wgroups(np.asarray(kw["L0_attn_a_wq"])[:, perm]),
         _wgroups(np.asarray(kw["L0_attn_a_wk"])[:, perm]),
         _wgroups(np.asarray(kw["L0_attn_a_wv"]))], axis=2))
    com["wqbg"] = _wgroups(T_(kw["L0_attn_b_Wq"]))
    com["wkbg"] = _wgroups(T_(kw["L0_attn_b_Wk"]))
    _wqkv_t = T_(kw["L1_attn_a_Wqkv"])
    com["wqkvg"] = np.ascontiguousarray(np.concatenate(
        [_wgroups(_wqkv_t[:, 0:D]), _wgroups(_wqkv_t[:, D:2*D]), _wgroups(_wqkv_t[:, 2*D:3*D])], axis=2))
    com["wqBg"] = _wgroups(np.asarray(kw["L1_attn_b_Wq"]))
    com["wqb"] = _wchunks(T_(kw["L0_attn_b_Wq"]))
    com["wkb"] = _wchunks(T_(kw["L0_attn_b_Wk"]))
    com["wvb"] = _wchunks(T_(kw["L0_attn_b_Wv"]))
    com["wfic"] = _wchunks(np.concatenate([T_(kw["L0_ffn_wf"]), T_(kw["L0_ffn_wi"]), T_(kw["L0_ffn_wc"])], 1))
    com["ufic"] = _wchunks(np.concatenate([T_(kw["L0_ffn_uf"]), T_(kw["L0_ffn_ui"]), T_(kw["L0_ffn_uc"])], 1))
    com["wovT"] = _wchunks(T_(kw["L0_ffn_wo"]))
    com["uovT"] = _wchunks(T_(kw["L0_ffn_uo"]))
    com["incb"] = np.ascontiguousarray(
        np.broadcast_to(np.asarray(kw["L0_ffn_in_c"]).astype(np.float32).reshape(DCH, P, 1), (DCH, P, 256)))
    com["wqkv"] = _wchunks(T_(kw["L1_attn_a_Wqkv"]))
    com["wprj"] = _wchunks(T_(kw["L1_attn_a_Wproj"]))
    com["wfcT"] = _wchunks(T_(kw["L1_ffn_W_fc"]))
    com["wprT"] = _wchunks(T_(kw["L1_ffn_W_proj"]))
    com["afcT"] = _wchunks(T_(kw["L1_ffn_A_fc"]))
    com["bfcT"] = np.ascontiguousarray(T_(kw["L1_ffn_B_fc"]).astype(np.float32)).reshape(1, 16, 4096)
    com["aprT"] = _wchunks(T_(kw["L1_ffn_A_proj"]))
    com["bprT"] = np.ascontiguousarray(T_(kw["L1_ffn_B_proj"]).astype(np.float32)).reshape(1, 16, D)
    com["wqB"] = _wchunks(np.asarray(kw["L1_attn_b_Wq"]))
    com["wlB"] = _wchunks(np.asarray(kw["L1_attn_b_Wl"]))
    com["wkB"] = _wchunks(np.asarray(kw["L1_attn_b_Wk"]))
    com["wvB"] = _wchunks(np.asarray(kw["L1_attn_b_Wv"]))
    com["woB"] = _wchunks(np.asarray(kw["L1_attn_b_Wo"]))

    biases = np.zeros((P, 200), np.float32)
    def setb(name, v):
        v = np.asarray(v).astype(np.float32)
        cols = v.size // P
        biases[:, BOFF[name]:BOFF[name] + cols] = v.reshape(cols, P).T
    setb("wo_b", kw["L0_attn_a_wo_b"])
    setb("bq", kw["L0_attn_b_bq"]); setb("bk", kw["L0_attn_b_bk"]); setb("bv", kw["L0_attn_b_bv"])
    setb("bfic", np.concatenate([np.asarray(kw["L0_ffn_bf"]), np.asarray(kw["L0_ffn_bi"]), np.asarray(kw["L0_ffn_bc"])]))
    setb("bo_f", kw["L0_ffn_bo"])
    setb("bqkv", kw["L1_attn_a_bqkv"]); setb("bproj", kw["L1_attn_a_bproj"])
    setb("b_fc", kw["L1_ffn_b_fc"]); setb("b_proj", kw["L1_ffn_b_proj"])
    setb("Bq", kw["L1_attn_b_bq"]); setb("Bk", kw["L1_attn_b_bk"]); setb("Bv", kw["L1_attn_b_bv"])
    setb("Bo", kw["L1_attn_b_bo"]); setb("Bl", kw["L1_attn_b_bl"])
    biases[:, BOFF["eps"]] = 1e-5
    biases[:, BOFF["epsa"]] = 0.001

    com["biases"] = biases
    com["cosT"] = cosT; com["sinT"] = sinT
    com["cosTb"] = np.ascontiguousarray(cosT.astype(BF)); com["sinTb"] = np.ascontiguousarray(sinT.astype(BF))
    com["maskmul"] = mm
    com["onesb"] = np.ones((P, T), BF)
    com["onesrow"] = np.ones((1, 512), BF)
    hs = np.zeros((P, 2), np.float32); hs[0:64, 0] = 1.0; hs[64:128, 1] = 1.0
    com["headsum"] = hs.astype(BF)
    sc = np.zeros((1, 8), np.float32); sc[0, 0] = float(np.asarray(kw["L1_attn_a_scale"]))
    com["scal"] = sc
    com["scalb"] = np.ascontiguousarray(np.broadcast_to(sc[0:1, 0:1], (P, 1)))
    bvrow = np.zeros((1, 2 * D), np.float32)
    bvrow[0, 0:D] = np.asarray(kw["L0_attn_b_bv"])
    bvrow[0, D:2 * D] = np.asarray(kw["L1_attn_b_bv"])
    com["bvrow"] = bvrow.astype(BF)

    in_maps = []
    for c in range(NC):
        m = dict(com)
        m["xT"] = np.ascontiguousarray(x[c].T).reshape(DCH, P, T).astype(np.float32)
        in_maps.append(m)
    return in_maps


def kernel(**kw):
    in_maps = _prep_inputs(kw)
    nc = build_nc(N_SUB)
    res = run_bass_kernel_spmd(nc, in_maps, core_ids=list(range(NC)))
    out = np.empty((NC, T, D), np.float32)
    for c in range(NC):
        y = np.asarray(res.results[c]["yT"])
        out[c] = y.reshape(D, T).T
    return out


# version 11 (final)
# speedup vs baseline: 1.5507x; 1.4671x over previous
import numpy as np
import ml_dtypes
from contextlib import ExitStack
import concourse.bass as bass
import concourse.mybir as mybir
from concourse.bass_utils import run_bass_kernel_spmd
from concourse.aot_env import allow_ici_all_to_all

F32 = mybir.dt.float32
BF16 = mybir.dt.bfloat16
I32 = mybir.dt.int32
AF = mybir.ActivationFunctionType
ALU = mybir.AluOpType
BF = ml_dtypes.bfloat16

P = 128
T = 2048
D = 1024
NC = 8
DCH = D // P
TCH = T // P
NQ = T // 512
H = 16
HD = 64

ARENA = 69504

N_SUB = 8
PHASE = 99
DBG_COLS = None
DBG_OVERWAIT = False
LNPHASE = 99
L1BPHASE = 99
NGRP = 8
ONLY_SUB = None

class _RecInst:
    __slots__ = ("incs",)
    def __init__(self):
        self.incs = []
    def then_inc(self, sem, n=1):
        self.incs.append((sem, n))
        sem.count += n
        return self

class _EngRec:
    def __init__(self, name):
        self.name = name
        self.calls = []
    def __getattr__(self, meth):
        def call(*a, **k):
            inst = _RecInst()
            self.calls.append((meth, a, k, inst))
            return inst
        return call

class Sem:
    def __init__(self, name):
        self.name = name
        self.count = 0
        self.handle = None

class Recorder:
    def __init__(self):
        self.sync = _EngRec("sync")
        self.tensor = _EngRec("tensor")
        self.vector = _EngRec("vector")
        self.scalar = _EngRec("scalar")
        self.gpsimd = _EngRec("gpsimd")

def _replay(eng, calls):
    for meth, a, k, inst in calls:
        a = tuple(x.handle if isinstance(x, Sem) else x for x in a)
        k = {kk: (vv.handle if isinstance(vv, Sem) else vv) for kk, vv in k.items()}
        real = getattr(eng, meth)(*a, **k)
        for sem, n in inst.incs:
            real.then_inc(sem.handle, n)

class Buf:
    def __init__(self, t, off_b, w, n=1, dtype=BF16):
        self.t, self.off_b, self.w, self.n, self.dtype = t, off_b, w, n, dtype
        self.m = 2 if dtype == F32 else 1
    def at(self, i, c0, c1):
        a = self.off_b + self.m * (i * self.w + c0)
        b = self.off_b + self.m * (i * self.w + c1)
        ap = self.t[:, a:b]
        return ap.bitcast(F32) if self.m == 2 else ap
    def __getitem__(self, i):
        return self.at(i, 0, self.w)
    def rows(self, r0, r1, i=0, c0=0, c1=None):
        c1 = self.w if c1 is None else c1
        a = self.off_b + self.m * (i * self.w + c0)
        b = self.off_b + self.m * (i * self.w + c1)
        ap = self.t[r0:r1, a:b]
        return ap.bitcast(F32) if self.m == 2 else ap

class Arena:
    def __init__(self, t, cols):
        self.t, self.cols, self.off = t, cols, 0
    def take(self, w, n=1, dtype=BF16):
        m = 2 if dtype == F32 else 1
        b = Buf(self.t, self.off, w, n, dtype)
        self.off += w * n * m
        assert self.off <= self.cols, f"arena overflow: {self.off} > {self.cols}"
        return b
    def reset(self, off=0):
        self.off = off

def _wchunks(w):
    K, C = w.shape
    assert K % P == 0
    return np.ascontiguousarray(w.reshape(K // P, P, C).astype(np.float32))

def _wgroups(w):
    K, C = w.shape
    g = w.reshape(K // P, P, C // P, P).transpose(2, 1, 0, 3).reshape(C // P, P, K)
    return np.ascontiguousarray(g.astype(np.float32))

def _rope_tables():
    inv = 1.0 / (10000.0 ** (np.arange(0, HD, 2)[: HD // 2].astype(np.float32) / HD))
    ang = np.outer(np.arange(T, dtype=np.float32), inv)
    cos, sin = np.cos(ang), np.sin(ang)
    cosT = np.tile(cos.T, (4, 1)).astype(np.float32)
    sinT = np.tile(sin.T, (4, 1)).astype(np.float32)
    sgn = np.ones((P, 1), np.float32)
    sgn[0:32] = -1.0
    sgn[64:96] = -1.0
    return cosT, np.ascontiguousarray(sinT * sgn)

def _perm_eo():
    perm = []
    for h in range(H):
        perm += list(range(64 * h, 64 * h + 64, 2))
        perm += list(range(64 * h + 1, 64 * h + 64, 2))
    return np.asarray(perm)

def _masks():
    k = np.arange(P)[:, None]
    c = np.arange(896)[None, :]
    return np.ascontiguousarray((c >= k + 384).astype(np.float32).astype(BF))

BOFF = {
    "wo_b": 0, "bq": 8, "bk": 16, "bv": 24,
    "bfic": 32, "bo_f": 56,
    "bqkv": 64, "bproj": 88,
    "b_fc": 96, "b_proj": 128,
    "Bq": 136, "Bk": 144, "Bv": 152, "Bo": 160, "Bl": 168,
    "eps": 170, "epsa": 171,
}

def build_nc(n_sub=N_SUB, detect_races=False):
    nc = bass.Bass("TRN2", target_bir_lowering=False, debug=False,
                   detect_race_conditions=detect_races)
    dt = {}

    def din(name, shape, dtype=F32):
        ap = nc.dram_tensor(name, list(shape), dtype, kind="ExternalInput").ap()
        dt[name] = ap
        return ap

    din("xT", (DCH, P, T))
    din("wq0", (DCH, P, D)); din("wk0", (DCH, P, D)); din("wv0", (DCH, P, D)); din("wo0", (DCH, P, D))
    din("wqkg", (DCH, P, 3 * D))
    din("wqbg", (DCH, P, D)); din("wkbg", (DCH, P, D))
    din("wqkvg", (DCH, P, 3 * D))
    din("wqBg", (DCH, P, D))
    din("wqb", (DCH, P, D)); din("wkb", (DCH, P, D)); din("wvb", (DCH, P, D))
    din("wfic", (DCH, P, 3 * D)); din("ufic", (DCH, P, 3 * D))
    din("wovT", (DCH, P, D)); din("uovT", (DCH, P, D)); din("incb", (DCH, P, 256))
    din("wqkv", (DCH, P, 3 * D)); din("wprj", (DCH, P, D))
    din("wfcT", (DCH, P, 4096)); din("wprT", (4096 // P, P, D))
    din("afcT", (DCH, P, 16)); din("bfcT", (1, 16, 4096))
    din("aprT", (4096 // P, P, 16)); din("bprT", (1, 16, D))
    din("wqB", (DCH, P, D)); din("wlB", (DCH, P, 256)); din("wlBg", (2, P, D))
    din("wkB", (2, P, D)); din("wvB", (2, P, D)); din("woB", (DCH, P, D))
    din("biases", (P, 200))
    din("cosT", (P, T)); din("sinT", (P, T))
    din("cosTb", (P, T), BF16); din("sinTb", (P, T), BF16)
    din("maskmul", (P, 896), BF16)
    din("onesb", (P, T), BF16); din("onesrow", (1, 512), BF16)
    din("headsum", (P, 2), BF16); din("scal", (1, 8)); din("scalb", (P, 1))
    din("bvrow", (1, 2 * D), BF16)

    yT = nc.dram_tensor("yT", [DCH, P, T], F32, kind="ExternalOutput").ap()
    ydbg = nc.dram_tensor("ydbg", [P, 16384], BF16, kind="ExternalOutput").ap()

    ds = {}
    def dints(name, shape, dtype):
        ds[name] = nc.dram_tensor(name, list(shape), dtype).ap()
    dints("cc_in", (NC, DCH, P, 256), BF16)
    dints("cc_out", (NC, DCH, P, 256), BF16)
    dints("xlnd", (DCH, P, T), BF16)
    dints("cc2_in", (DCH, P, 256), BF16)
    dints("cc2_out", (NC * DCH, P, 256), BF16)
    dints("stat_dram", (4, T), F32)
    dints("den_dram", (16, T), F32)
    dints("xln_dram", (DCH, P, T), BF16)
    dints("q_dram", (DCH, P, T), BF16)
    dints("wqfc_dram", (32, P, 1024), BF16)
    dints("wqpr_dram", (4096 // P, P, D), BF16)
    dints("ffnw_dram", (DCH, P, 8 * D), BF16)

    es = ExitStack()
    r = Recorder()
    s_w = Sem("s_w"); s_g = Sem("s_g"); s_s = Sem("s_s")
    s_pe = Sem("s_pe"); s_v = Sem("s_v"); s_sc = Sem("s_sc"); s_cc = Sem("s_cc")
    s_b = Sem("s_b"); s_gc = Sem("s_gc")
    sems = [s_w, s_g, s_s, s_pe, s_v, s_sc, s_cc, s_b, s_gc]

    with es:
        sbuf = lambda name, shape, dtype: es.enter_context(nc.sbuf_tensor(name, list(shape), dtype))
        env = dict(dt=dt, ds=ds, yT=yT, ydbg=ydbg)
        env["xr"] = sbuf("xr", (P, DCH, T), F32)
        env["ones_t"] = sbuf("ones_t", (P, T), BF16)
        env["ones_r"] = sbuf("ones_r", (1, 512), BF16)
        env["hsum"] = sbuf("hsum", (P, 2), BF16)
        env["bia"] = sbuf("bia", (P, 200), F32)
        env["scl"] = sbuf("scl", (1, 8), F32)
        env["sclb"] = sbuf("sclb", (P, 1), F32)
        env["dsb"] = sbuf("dsb", (P, 512), F32)
        env["AB"] = Arena(sbuf("AB", (P, ARENA), BF16), ARENA)
        env["pb"] = [es.enter_context(nc.psum_tensor(f"pb{i}", [P, 512], F32)) for i in range(8)]

        emit_program(r, sems, env, n_sub)

        for s in sems:
            s.handle = es.enter_context(nc.semaphore(s.name))
        es.enter_context(allow_ici_all_to_all())
        block = es.enter_context(nc.Block())
        block.sync(lambda eng: _replay(eng, r.sync.calls))
        block.tensor(lambda eng: _replay(eng, r.tensor.calls))
        block.vector(lambda eng: _replay(eng, r.vector.calls))
        block.scalar(lambda eng: _replay(eng, r.scalar.calls))
        block.gpsimd(lambda eng: _replay(eng, r.gpsimd.calls))
    return nc


def emit_program(r, sems, env, n_sub):
    global PHASE, LNPHASE, DBG_COLS, DBG_OVERWAIT, L1BPHASE, ONLY_SUB, NGRP
    s_w, s_g, s_s, s_pe, s_v, s_sc, s_cc, s_b, s_gc = sems
    dt, ds, yT = env["dt"], env["ds"], env["yT"]
    xr = env["xr"]
    ones_t, ones_r, hsum = env["ones_t"], env["ones_r"], env["hsum"]
    bia, scl, sclb, dsb = env["bia"], env["scl"], env["sclb"], env["dsb"]
    AB = env["AB"]
    pb = env["pb"]

    bcol = lambda name, j=0: bia[:, BOFF[name] + j: BOFF[name] + j + 1]

    def mark(eng_rec, sem, n=1):
        last = eng_rec.calls[-1] if eng_rec.calls else None
        if last is not None and not last[3].incs and last[0] != "wait_ge":
            last[3].then_inc(sem, n)
        else:
            nop = "engine_nop" if eng_rec.name in ("vector", "gpsimd") else "nop"
            getattr(eng_rec, nop)().then_inc(sem, n)

    ab = AB.take(T, 2, F32)
    gsc = AB.take(T, 1, F32)
    AB_BASE = AB.off

    def barrier():
        r.sync.wait_ge(s_w, s_w.count)
        r.sync.nop().then_inc(s_b, 1)
        r.gpsimd.wait_ge(s_g, s_g.count)
        r.gpsimd.engine_nop().then_inc(s_b, 1)
        r.scalar.wait_ge(s_s, s_s.count)
        r.scalar.nop().then_inc(s_b, 1)
        r.vector.engine_nop().then_inc(s_b, 1)
        r.tensor.nop().then_inc(s_b, 1)
        v = s_b.count
        for e in (r.tensor, r.vector, r.scalar, r.gpsimd, r.sync):
            e.wait_ge(s_b, v)

    r.sync.dma_start(out=ones_t[:, :], in_=dt["onesb"][:, :]).then_inc(s_w, 16)
    r.sync.dma_start(out=ones_r[0:1, :], in_=dt["onesrow"][0:1, :]).then_inc(s_w, 16)
    r.sync.dma_start(out=hsum[:, :], in_=dt["headsum"][:, :]).then_inc(s_w, 16)
    r.sync.dma_start(out=bia[:, :], in_=dt["biases"][:, :]).then_inc(s_w, 16)
    r.sync.dma_start(out=scl[0:1, :], in_=dt["scal"][0:1, :]).then_inc(s_w, 16)
    r.sync.dma_start(out=sclb[:, :], in_=dt["scalb"][:, :]).then_inc(s_w, 16)
    for p in range(DCH):
        r.sync.dma_start(out=xr[:, p, :], in_=dt["xT"][p, :, :]).then_inc(s_w, 16)
    tok = s_w.count
    for e in (r.tensor, r.vector, r.scalar, r.gpsimd):
        e.wait_ge(s_w, tok)

    def emit_ln(xln, sq, consume=None, nslots=None):
        ns = nslots or DCH
        ones1 = ones_t[:, 0:1]
        for p in range(DCH):
            if p >= ns:
                r.vector.wait_ge(s_pe, s_pe.count)
            r.vector.tensor_copy(out=xln[p % ns], in_=xr[:, p, :])
            mark(r.vector, s_v, 1)
            r.scalar.wait_ge(s_v, s_v.count)
            if p >= 2:
                r.scalar.wait_ge(s_pe, s_pe.count)
            r.scalar.activation(out=sq[p % 2], in_=xln[p % ns], func=AF.Square)
            mark(r.scalar, s_sc, 1)
            r.tensor.wait_ge(s_sc, s_sc.count)
            r.tensor.wait_ge(s_v, s_v.count)
            for n in range(NQ):
                r.tensor.matmul(pb[n][0:1, :], ones1, xln.at(p % ns, 512 * n, 512 * n + 512),
                                start=(p == 0), stop=(p == DCH - 1))
            for n in range(NQ):
                r.tensor.matmul(pb[4 + n][0:1, :], ones1, sq.at(p % 2, 512 * n, 512 * n + 512),
                                start=(p == 0), stop=(p == DCH - 1))
            mark(r.tensor, s_pe, 1)
        mmtok = s_pe.count
        if LNPHASE < 1:
            return s_v.count
        m = ab.rows(0, 1, 0)
        s2 = ab.rows(0, 1, 1)
        tmp = gsc.rows(0, 1, 0)
        r.vector.wait_ge(s_pe, mmtok)
        for n in range(NQ):
            r.vector.tensor_copy(out=ab.rows(0, 1, 0, 512 * n, 512 * n + 512), in_=pb[n][0:1, :])
            r.vector.tensor_copy(out=ab.rows(0, 1, 1, 512 * n, 512 * n + 512), in_=pb[4 + n][0:1, :])
        mark(r.vector, s_v, 1)
        r.tensor.wait_ge(s_v, s_v.count)
        r.vector.tensor_scalar(out=m, in0=m, scalar1=1.0 / D, scalar2=None, op0=ALU.mult)
        r.vector.tensor_scalar(out=s2, in0=s2, scalar1=1.0 / D, scalar2=None, op0=ALU.mult)
        r.vector.tensor_mul(tmp, m, m)
        r.vector.tensor_sub(s2, s2, tmp)
        mark(r.vector, s_v, 1)
        if LNPHASE < 2:
            return s_v.count
        r.scalar.wait_ge(s_v, s_v.count)
        r.scalar.activation(out=s2, in_=s2, func=AF.Sqrt, bias=bcol("eps")[0:1, :])
        mark(r.scalar, s_sc, 1)
        r.vector.wait_ge(s_sc, s_sc.count)
        r.vector.reciprocal(out=s2, in_=s2).then_inc(s_v, 1)
        r.vector.wait_ge(s_v, s_v.count)
        r.vector.scalar_tensor_tensor(out=m, in0=m, scalar=-1.0, in1=s2,
                                      op0=ALU.mult, op1=ALU.mult)
        mark(r.vector, s_v, 1)
        if LNPHASE < 3:
            return s_v.count
        r.gpsimd.wait_ge(s_v, s_v.count)
        r.gpsimd.dma_start(out=ds["stat_dram"][0:1, :], in_=s2).then_inc(s_g, 16)
        r.gpsimd.dma_start(out=ds["stat_dram"][1:2, :], in_=m).then_inc(s_g, 16)
        r.gpsimd.wait_ge(s_g, s_g.count)
        r.gpsimd.dma_start(out=ab[0], in_=ds["stat_dram"][0:1, :].broadcast_to((P, T))).then_inc(s_g, 16)
        r.gpsimd.dma_start(out=ab[1], in_=ds["stat_dram"][1:2, :].broadcast_to((P, T))).then_inc(s_g, 16)
        r.vector.wait_ge(s_g, s_g.count)
        if LNPHASE < 4:
            return s_v.count
        for p in range(DCH):
            r.vector.tensor_mul(gsc[0], xr[:, p, :], ab[0])
            r.vector.tensor_tensor(out=xln[p % ns], in0=gsc[0], in1=ab[1], op=ALU.add)
            if consume is not None:
                mark(r.vector, s_v, 1)
                consume(p, p % ns)
        mark(r.vector, s_v, 1)
        return s_v.count

    def emit_l0a():
        AB.reset(AB_BASE)
        xln = AB.take(T, DCH)
        u = C = AB.take(T, DCH)
        gq = AB.take(T, 1); gk = AB.take(T, 1)
        shb = AB.take(T, 1)
        pT = AB.take(512, 2)
        vaug = AB.take(130, 16)
        wslot = AB.take(D, 3)
        wstg = AB.take(1024, 1, F32)
        mskm = AB.take(896, 1)
        csb = AB.take(T, 2, F32)
        wof = Buf(xln.t, xln.off_b, D, DCH)
        stg32 = lambda i: wstg[i]
        vch = 130

        ln_tok = emit_ln(xln, Buf(u.t, u.off_b, T, 2))
        r.tensor.wait_ge(s_v, ln_tok)
        r.sync.wait_ge(s_v, ln_tok)
        r.sync.dma_start(out=csb[0], in_=dt["cosT"][:, :]).then_inc(s_w, 16)
        r.sync.dma_start(out=csb[1], in_=dt["sinT"][:, :]).then_inc(s_w, 16)
        r.sync.dma_start(out=mskm[0], in_=dt["maskmul"][:, :]).then_inc(s_w, 16)
        r.vector.wait_ge(s_w, s_w.count)

        stok = [(s_v, 0)]
        for g in range(NGRP):
            r.vector.wait_ge(s_pe, s_pe.count)
            for wi in range(3):
                r.sync.wait_ge(*stok[0])
                r.sync.dma_start(out=stg32(0), in_=dt["wqkg"][g, :, D * wi:D * wi + D]).then_inc(s_w, 16)
                r.vector.wait_ge(s_w, s_w.count)
                r.vector.tensor_copy(out=wslot[wi], in_=stg32(0))
                mark(r.vector, s_v, 1)
                stok[0] = (s_v, s_v.count)
            for tc in range(TCH):
                r.vector.tensor_copy(out=vaug.at(tc, 64, 65), in_=ones_t[:, 0:1])
                r.vector.tensor_copy(out=vaug.at(tc, 129, 130), in_=ones_t[:, 0:1])
            mark(r.vector, s_v, 1)
            r.tensor.wait_ge(s_v, s_v.count)
            toks = []
            items = [(w, gb, n) for w, gb in ((0, gq), (1, gk)) for n in range(NQ)]
            for i, (which, gbuf, n) in enumerate(items):
                if i >= 3:
                    r.tensor.wait_ge(*toks[i - 3])
                ps = pb[i % 4]
                for kp in range(DCH):
                    r.tensor.matmul(ps[:, :], wslot.at(which, 128 * kp, 128 * kp + 128),
                                    xln.at(kp, 512 * n, 512 * n + 512),
                                    start=(kp == 0), stop=(kp == DCH - 1))
                mark(r.tensor, s_pe, 1)
                r.vector.wait_ge(s_pe, s_pe.count)
                r.vector.tensor_copy(out=gbuf.at(0, 512 * n, 512 * n + 512), in_=ps[:, :])
                mark(r.vector, s_v, 1)
                toks.append((s_v, s_v.count))
            vtoks = []
            for tc in range(TCH):
                if tc >= 3:
                    r.tensor.wait_ge(*vtoks[tc - 3])
                ps = pb[4 + tc % 4]
                for kp in range(DCH):
                    r.tensor.matmul(ps[:, 0:128], xln.at(kp, 128 * tc, 128 * tc + 128),
                                    wslot.at(2, 128 * kp, 128 * kp + 128),
                                    start=(kp == 0), stop=(kp == DCH - 1))
                mark(r.tensor, s_pe, 1)
                r.vector.wait_ge(s_pe, s_pe.count)
                r.vector.tensor_copy(out=vaug.at(tc, 0, 64), in_=ps[:, 0:64])
                r.vector.tensor_copy(out=vaug.at(tc, 65, 129), in_=ps[:, 64:128])
                mark(r.vector, s_v, 1)
                vtoks.append((s_v, s_v.count))
            for gi, gbuf in enumerate((gq, gk)):
                r.gpsimd.wait_ge(s_v, s_v.count)
                if g > 0 or gi > 0:
                    r.gpsimd.wait_ge(s_pe, s_pe.count)
                r.gpsimd.dma_start(out=shb.rows(0, 32, 0), in_=gbuf.rows(32, 64)).then_inc(s_g, 16)
                r.gpsimd.dma_start(out=shb.rows(64, 96, 0), in_=gbuf.rows(96, 128)).then_inc(s_g, 16)
                r.gpsimd.dma_start(out=shb.rows(32, 64, 0), in_=gbuf.rows(0, 32)).then_inc(s_g, 16)
                r.gpsimd.dma_start(out=shb.rows(96, 128, 0), in_=gbuf.rows(64, 96)).then_inc(s_g, 16)
                r.vector.wait_ge(s_g, s_g.count)
                r.vector.tensor_mul(gsc[0], gbuf[0], csb[0])
                r.vector.tensor_mul(shb[0], shb[0], csb[1])
                r.vector.tensor_tensor(out=gbuf[0], in0=gsc[0], in1=shb[0], op=ALU.add)
                mark(r.vector, s_v, 1)
                r.gpsimd.wait_ge(s_v, s_v.count)
            rope_tok = s_v.count
            r.tensor.wait_ge(s_v, rope_tok)
            for hl in range(2):
                h = 2 * g + hl
                qh = gq.rows(64 * hl, 64 * hl + 64)
                kh = gk.rows(64 * hl, 64 * hl + 64)
                for qc in range(NQ):
                    q0 = 512 * qc
                    pout = pb[4 + (qc % 2)]
                    npan = 4 * qc + 4
                    etoks = []
                    otoks = []
                    for kpan in range(npan):
                        k0 = 128 * kpan
                        di = (k0 - q0) // 128
                        if kpan >= 3:
                            r.tensor.wait_ge(*etoks[kpan - 3])
                        ps = pb[kpan % 4]
                        r.tensor.matmul(ps[:, :], kh[:, k0:k0 + 128], qh[:, q0:q0 + 512],
                                        start=True, stop=True)
                        mark(r.tensor, s_pe, 1)
                        pt = pT[kpan % 2]
                        r.scalar.wait_ge(s_pe, s_pe.count)
                        if kpan >= 2:
                            r.scalar.wait_ge(*otoks[kpan - 2])
                        r.scalar.activation(out=pt, in_=ps[:, :], func=AF.Exp, scale=0.125)
                        mark(r.scalar, s_sc, 1)
                        etok = (s_sc, s_sc.count)
                        if 0 <= di <= 3:
                            r.vector.wait_ge(s_sc, s_sc.count)
                            r.vector.tensor_mul(pt, pt, mskm.at(0, 384 - 128 * di, 896 - 128 * di))
                            mark(r.vector, s_v, 1)
                            etok = (s_v, s_v.count)
                        etoks.append(etok)
                        r.tensor.wait_ge(*etok)
                        r.tensor.matmul(pout[0:65, :], vaug.at(kpan, 65 * hl, 65 * hl + 65),
                                        pt, start=(kpan == 0), stop=(kpan == npan - 1))
                        mark(r.tensor, s_pe, 1)
                        otoks.append((s_pe, s_pe.count))
                    r.vector.wait_ge(s_pe, s_pe.count)
                    r.vector.tensor_copy(out=u.rows(64 * hl, 64 * hl + 64, g, q0, q0 + 512), in_=pout[0:64, :])
                    r.vector.tensor_copy(out=dsb[64:65, 0:512], in_=pout[64:65, :])
                    mark(r.vector, s_v, 1)
                    r.tensor.wait_ge(s_v, s_v.count)
                    r.scalar.wait_ge(s_v, s_v.count)
                    r.scalar.dma_start(out=ds["den_dram"][h:h+1, q0:q0 + 512], in_=dsb[64:65, 0:512]).then_inc(s_s, 16)
                    r.vector.wait_ge(s_s, s_s.count)
        if PHASE < 3:
            return
        r.gpsimd.wait_ge(s_s, s_s.count)
        for n in range(NQ):
            c0 = 512 * n
            if n > 0:
                r.gpsimd.wait_ge(s_v, s_v.count)
            r.gpsimd.dma_start(out=dsb[0:16, 0:512], in_=ds["den_dram"][0:16, c0:c0 + 512]).then_inc(s_g, 16)
            r.vector.wait_ge(s_g, s_g.count)
            r.vector.reciprocal(out=dsb[0:16, 0:512], in_=dsb[0:16, 0:512]).then_inc(s_v, 1)
            r.vector.wait_ge(s_v, s_v.count)
            r.vector.engine_nop().then_inc(s_v, 1)
            r.gpsimd.wait_ge(s_v, s_v.count)
            r.gpsimd.dma_start(out=ds["den_dram"][0:16, c0:c0 + 512], in_=dsb[0:16, 0:512]).then_inc(s_g, 16)
        r.gpsimd.wait_ge(s_g, s_g.count)
        for p in range(DCH):
            r.gpsimd.wait_ge(s_v, s_v.count)
            r.gpsimd.dma_start(out=gsc.rows(0, 64, 0), in_=ds["den_dram"][2 * p:2 * p + 1, :].broadcast_to((64, T))).then_inc(s_g, 16)
            r.gpsimd.dma_start(out=gsc.rows(64, 128, 0), in_=ds["den_dram"][2 * p + 1:2 * p + 2, :].broadcast_to((64, T))).then_inc(s_g, 16)
            r.vector.wait_ge(s_g, s_g.count)
            r.vector.tensor_mul(u[p], u[p], gsc[0])
            mark(r.vector, s_v, 1)
        if PHASE < 4:
            return
        r.vector.wait_ge(s_pe, s_pe.count)
        wtoks = list(stok)
        for kp in range(DCH):
            r.sync.wait_ge(*wtoks[0])
            r.sync.dma_start(out=stg32(0), in_=dt["wo0"][kp, :, :]).then_inc(s_w, 16)
            r.vector.wait_ge(s_w, s_w.count)
            r.vector.tensor_copy(out=wof.at(kp, 0, D), in_=stg32(0))
            mark(r.vector, s_v, 1)
            wtoks[0] = (s_v, s_v.count)
        r.tensor.wait_ge(s_v, s_v.count)
        if PHASE < 5:
            return
        rtoks = []
        items = [(po, n) for po in range(DCH) for n in range(NQ)]
        for i, (po, n) in enumerate(items):
            if i >= 3:
                r.tensor.wait_ge(*rtoks[i - 3])
            ps = pb[i % 4]
            for kp in range(DCH):
                r.tensor.matmul(ps[:, :], wof.at(kp, 128 * po, 128 * po + 128),
                                u.at(kp, 512 * n, 512 * n + 512),
                                start=(kp == 0), stop=(kp == DCH - 1))
            mark(r.tensor, s_pe, 1)
            r.scalar.wait_ge(s_pe, s_pe.count)
            if i >= 2:
                r.scalar.wait_ge(*rtoks[i - 2])
            tmp = gsc.rows(0, P, 0, 512 * (i % 2), 512 * (i % 2) + 512)
            r.scalar.activation(out=tmp, in_=ps[:, :], func=AF.Identity, bias=bcol("wo_b", po))
            mark(r.scalar, s_sc, 1)
            r.vector.wait_ge(s_sc, s_sc.count)
            r.vector.tensor_tensor(out=xr[:, po, 512 * n:512 * n + 512],
                                   in0=xr[:, po, 512 * n:512 * n + 512], in1=tmp, op=ALU.add)
            mark(r.vector, s_v, 1)
            rtoks.append((s_v, s_v.count))

    def emit_l0f():
        AB.reset(AB_BASE)
        W = AB.take(3 * D, 2 * DCH)
        xsl = AB.take(T, 2)
        cT = Buf(xsl.t, xsl.off_b, 256, DCH, F32)
        stg = AB.take(512, 1, F32)
        xg = Buf(AB.t, 0, 2048, 2)
        cTbf = Buf(AB.t, 4096, 256, DCH)
        sq = Buf(AB.t, 0, T, 2)
        gates = Buf(gsc.t, gsc.off_b, 256, 3, F32)
        uT = Buf(W.t, W.off_b, T, DCH)
        wov = Buf(W.t, W.off_b + 16384, D, DCH)
        xre = Buf(W.t, W.off_b + 16384 + 8192, T, DCH)
        uov = Buf(W.t, W.off_b + 16384 + 8192, D, DCH)

        def load_w(dst_buf, slot, dram, kp, cols):
            for cc0 in range(0, cols, 512):
                r.sync.wait_ge(s_v, s_v.count)
                r.sync.dma_start(out=stg[0], in_=dram[kp, :, cc0:cc0 + 512]).then_inc(s_w, 16)
                r.vector.wait_ge(s_w, s_w.count)
                r.vector.tensor_copy(out=dst_buf.at(slot, cc0, cc0 + 512), in_=stg[0])
                mark(r.vector, s_v, 1)

        a2a_src = ds["cc_in"]
        def consume(p, slot):
            r.gpsimd.wait_ge(s_v, s_v.count)
            for j in range(NC):
                r.gpsimd.dma_start(out=a2a_src[j, p, :, :],
                                   in_=xsl.at(slot, 256 * j, 256 * j + 256)).then_inc(s_g, 16)
            r.gpsimd.dma_start(out=ds["xlnd"][p, :, :], in_=xsl.at(slot, 0, T)).then_inc(s_g, 16)
            r.vector.wait_ge(s_g, s_g.count)
        emit_ln(xsl, sq, consume=consume, nslots=2)
        r.gpsimd.collective_compute(
            "AllToAll", ALU.bypass, replica_groups=[list(range(NC))],
            ins=[a2a_src.tensor.ap().opt()], outs=[ds["cc_out"].tensor.ap().opt()],
        ).then_inc(s_cc, 1)
        r.sync.wait_ge(s_v, s_v.count)
        for p in range(DCH):
            r.sync.dma_start(out=cT[p], in_=dt["incb"][p, :, :]).then_inc(s_w, 16)
        c0_tok = s_w.count
        for kp in range(DCH):
            load_w(W, kp, dt["wfic"], kp, 3 * D)
        for kp in range(DCH):
            load_w(W, DCH + kp, dt["ufic"], kp, 3 * D)
        wtok = s_v.count
        r.vector.wait_ge(s_w, c0_tok)
        for p in range(DCH):
            r.vector.tensor_copy(out=cTbf[p], in_=cT[p])
        mark(r.vector, s_v, 1)
        r.gpsimd.wait_ge(s_cc, s_cc.count)
        GB = [0, 8, 16]
        for t in range(NC):
            r.gpsimd.wait_ge(s_pe, s_pe.count)
            for kp in range(DCH):
                r.gpsimd.dma_start(out=xg.at(t % 2, 256 * kp, 256 * kp + 256),
                                   in_=ds["cc_out"][t, kp, :, :]).then_inc(s_g, 16)
            xg_tok = s_g.count
            r.tensor.wait_ge(s_g, xg_tok)
            if t == 0:
                r.tensor.wait_ge(s_v, wtok)
                r.tensor.wait_ge(s_v, s_v.count)
            for p in range(DCH):
                for gi in range(3):
                    ps = pb[gi]
                    gcol = 1024 * gi + 128 * p
                    for kp in range(DCH):
                        r.tensor.matmul(ps[:, 0:256], W.at(kp, gcol, gcol + 128),
                                        xg.at(t % 2, 256 * kp, 256 * kp + 256),
                                        start=(kp == 0), stop=False)
                    for kp in range(DCH):
                        r.tensor.matmul(ps[:, 0:256], W.at(DCH + kp, gcol, gcol + 128),
                                        cTbf[kp], start=False, stop=(kp == DCH - 1))
                mark(r.tensor, s_pe, 1)
                gtok = s_pe.count
                r.scalar.wait_ge(s_pe, gtok)
                if True:
                    r.scalar.wait_ge(s_v, s_v.count)
                r.scalar.activation(out=gates[0], in_=pb[0][:, 0:256], func=AF.Sigmoid, bias=bcol("bfic", p))
                r.scalar.activation(out=gates[1], in_=pb[1][:, 0:256], func=AF.Sigmoid, bias=bcol("bfic", 8 + p))
                r.scalar.activation(out=gates[2], in_=pb[2][:, 0:256], func=AF.Tanh, bias=bcol("bfic", 16 + p))
                mark(r.scalar, s_sc, 1)
                atok = s_sc.count
                r.tensor.wait_ge(s_sc, atok)
                r.vector.wait_ge(s_sc, atok)
                r.vector.tensor_mul(cT[p], gates[0], cT[p])
                r.vector.tensor_mul(gates[1], gates[1], gates[2])
                r.vector.tensor_tensor(out=cT[p], in0=cT[p], in1=gates[1], op=ALU.add)
                mark(r.vector, s_v, 1)
            if t < NC - 1:
                r.vector.tensor_copy(out=cTbf.at(0, 0, 0 + 2048) if False else cTbf[0], in_=cT[0])
                for p in range(1, DCH):
                    r.vector.tensor_copy(out=cTbf[p], in_=cT[p])
                mark(r.vector, s_v, 1)
                r.tensor.wait_ge(s_v, s_v.count)
        r.vector.tensor_copy(out=cTbf[0], in_=cT[0])
        for p in range(1, DCH):
            r.vector.tensor_copy(out=cTbf[p], in_=cT[p])
        mark(r.vector, s_v, 1)
        cf_tok = s_v.count
        r.sync.wait_ge(s_pe, s_pe.count)
        for kp in range(DCH):
            load_w(uov, kp, dt["uovT"], kp, D)
        uloc = Buf(AB.t, 0, 256, DCH)
        r.tensor.wait_ge(s_v, s_v.count)
        r.tensor.wait_ge(s_v, cf_tok)
        for po in range(DCH):
            ps = pb[po % 2]
            for kp in range(DCH):
                r.tensor.matmul(ps[:, 0:256], uov.at(kp, 128 * po, 128 * po + 128),
                                cTbf[kp], start=(kp == 0), stop=(kp == DCH - 1))
            mark(r.tensor, s_pe, 1)
            r.vector.wait_ge(s_pe, s_pe.count)
            r.vector.tensor_copy(out=uloc[po], in_=ps[:, 0:256])
            mark(r.vector, s_v, 1)
            r.tensor.wait_ge(s_v, s_v.count)
        r.gpsimd.wait_ge(s_v, s_v.count)
        for kp in range(DCH):
            r.gpsimd.dma_start(out=ds["cc2_in"][kp, :, :], in_=uloc[kp]).then_inc(s_g, 16)
        r.gpsimd.wait_ge(s_g, s_g.count)
        r.gpsimd.collective_compute(
            "AllGather", ALU.bypass, replica_groups=[list(range(NC))],
            ins=[ds["cc2_in"].tensor.ap().opt()], outs=[ds["cc2_out"].tensor.ap().opt()],
        ).then_inc(s_cc, 1)
        for kp in range(DCH):
            load_w(wov, kp, dt["wovT"], kp, D)
        r.sync.wait_ge(s_v, s_v.count)
        for p in range(DCH):
            r.sync.dma_start(out=xre[p], in_=ds["xlnd"][p, :, :]).then_inc(s_w, 16)
        xre_tok = s_w.count
        r.gpsimd.wait_ge(s_cc, s_cc.count)
        for rr in range(NC):
            for po in range(DCH):
                r.gpsimd.dma_start(out=uT.at(po, 256 * rr, 256 * rr + 256),
                                   in_=ds["cc2_out"][rr * DCH + po, :, :]).then_inc(s_g, 16)
        ut_tok = s_g.count
        r.tensor.wait_ge(s_v, s_v.count)
        r.tensor.wait_ge(s_w, xre_tok)
        r.tensor.wait_ge(s_g, ut_tok)
        for po in range(DCH):
            for n in range(NQ):
                ps = pb[2 + (po * NQ + n) % 2]
                for kp in range(DCH):
                    r.tensor.matmul(ps[:, :], wov.at(kp, 128 * po, 128 * po + 128),
                                    xre.at(kp, 512 * n, 512 * n + 512),
                                    start=(kp == 0), stop=(kp == DCH - 1))
                mark(r.tensor, s_pe, 1)
                ptok = s_pe.count
                r.vector.wait_ge(s_pe, ptok)
                r.vector.tensor_tensor(out=ps[:, :], in0=ps[:, :],
                                       in1=uT.at(po, 512 * n, 512 * n + 512), op=ALU.add)
                mark(r.vector, s_v, 1)
                r.scalar.wait_ge(s_v, s_v.count)
                tmp = gsc.rows(0, P, 0, 512 * (n % 4), 512 * (n % 4) + 512)
                r.scalar.activation(out=tmp, in_=ps[:, :], func=AF.Tanh, bias=bcol("bo_f", po))
                mark(r.scalar, s_sc, 1)
                r.vector.wait_ge(s_sc, s_sc.count)
                r.vector.tensor_tensor(out=xr[:, po, 512 * n:512 * n + 512],
                                       in0=xr[:, po, 512 * n:512 * n + 512], in1=tmp, op=ALU.add)
                mark(r.vector, s_v, 1)
                r.tensor.wait_ge(s_v, s_v.count)
                r.scalar.wait_ge(s_v, s_v.count)


    def emit_l0b():
        AB.reset(AB_BASE)
        xln = AB.take(T, DCH)
        qb = AB.take(T, DCH)
        kb = AB.take(T, DCH)
        wsl = AB.take(1024, 2)
        pT = AB.take(512, 1)
        msk = AB.take(896, 1)
        stgw = AB.take(1024, 1, F32)
        xti = AB.take(1024, 2)
        bvl = Buf(pT.t, pT.off_b, 1024, 1)
        vb = Buf(xln.t, xln.off_b, 1024, TCH)

        r.sync.wait_ge(s_v, s_v.count)
        r.sync.dma_start(out=msk[0], in_=dt["maskmul"][:, 0:896]).then_inc(s_w, 16)
        mtok = s_w.count
        addm = lambda di: gsc.rows(0, P, 0, 512 * di, 512 * di + 512)
        r.vector.wait_ge(s_w, mtok)

        def consume(p, slot):
            r.gpsimd.wait_ge(s_v, s_v.count)
            r.gpsimd.dma_start(out=ds["xlnd"][p, :, :], in_=xln[p]).then_inc(s_g, 16)
        ln_tok = emit_ln(xln, Buf(qb.t, qb.off_b, T, 2), consume=consume)
        r.tensor.wait_ge(s_v, ln_tok)
        r.sync.wait_ge(s_v, ln_tok)
        for di in range(4):
            r.vector.tensor_scalar(out=addm(di), in0=msk.rows(0, P, 0, 384 - 128 * di, 896 - 128 * di),
                                   scalar1=-1.0, scalar2=1e30, op0=ALU.add, op1=ALU.mult)
        mark(r.vector, s_v, 1)
        etoks = []
        stokb = [(s_v, 0)]
        for which, (gbuf, wname, bname) in enumerate(((qb, "wqbg", "bq"), (kb, "wkbg", "bk"))):
            for po in range(DCH):
                sl = po % 2
                r.sync.wait_ge(*stokb[0])
                r.sync.dma_start(out=stgw[0], in_=dt[wname][po, :, :]).then_inc(s_w, 16)
                r.vector.wait_ge(s_w, s_w.count)
                if which > 0 or po > 1:
                    r.vector.wait_ge(s_pe, s_pe.count)
                r.vector.tensor_copy(out=wsl[sl], in_=stgw[0])
                mark(r.vector, s_v, 1)
                stokb[0] = (s_v, s_v.count)
                r.tensor.wait_ge(s_v, s_v.count)
                for n in range(NQ):
                    i = len(etoks)
                    if i >= 3:
                        r.tensor.wait_ge(*etoks[i - 3])
                    ps = pb[i % 4]
                    for kp in range(DCH):
                        r.tensor.matmul(ps[:, :], wsl.at(sl, 128 * kp, 128 * kp + 128),
                                        xln.at(kp, 512 * n, 512 * n + 512),
                                        start=(kp == 0), stop=(kp == DCH - 1))
                    mark(r.tensor, s_pe, 1)
                    r.scalar.wait_ge(s_pe, s_pe.count)
                    r.scalar.activation(out=gbuf.at(po, 512 * n, 512 * n + 512), in_=ps[:, :],
                                        func=AF.Identity, bias=bcol(bname, po))
                    mark(r.scalar, s_sc, 1)
                    etoks.append((s_sc, s_sc.count))
        dtmp = dsb[:, 20:21]
        for jp in range(TCH):
            j0 = 128 * jp
            nic = NQ - j0 // 512
            for ici in range(nic):
                i0 = T - 512 * (ici + 1) if False else 512 * (NQ - nic + ici)
                ps = pb[4 + ici % 2]
                for kp in range(DCH):
                    r.tensor.matmul(ps[:, :], kb.at(kp, j0, j0 + 128), qb.at(kp, i0, i0 + 512),
                                    start=(kp == 0), stop=(kp == DCH - 1))
                mark(r.tensor, s_pe, 1)
                di = (j0 - i0) // 128
                if 0 <= di <= 3:
                    r.vector.wait_ge(s_pe, s_pe.count)
                    r.vector.tensor_tensor(out=ps[:, :], in0=ps[:, :], in1=addm(di), op=ALU.add)
                    mark(r.vector, s_v, 1)
                    r.scalar.wait_ge(s_v, s_v.count)
                r.scalar.wait_ge(s_pe, s_pe.count)
                acc = dsb[:, jp:jp + 1] if ici == 0 else dtmp
                r.scalar.activation(out=pT[0], in_=ps[:, :], func=AF.Exp,
                                    scale=0.03125, accum_out=acc)
                mark(r.scalar, s_sc, 1)
                r.tensor.wait_ge(s_sc, s_sc.count)
                if ici > 0:
                    r.vector.wait_ge(s_sc, s_sc.count)
                    r.vector.tensor_add(dsb[:, jp:jp + 1], dsb[:, jp:jp + 1], dtmp)
                    mark(r.vector, s_v, 1)
                    r.scalar.wait_ge(s_v, s_v.count)
        r.scalar.wait_ge(s_v, s_v.count)
        r.scalar.activation(out=dsb[:, 0:16], in_=dsb[:, 0:16], func=AF.Ln)
        mark(r.scalar, s_sc, 1)
        r.vector.wait_ge(s_sc, s_sc.count)
        r.vector.tensor_scalar(out=dsb[:, 0:16], in0=dsb[:, 0:16], scalar1=-1.0, scalar2=None, op0=ALU.mult)
        mark(r.vector, s_v, 1)
        r.sync.wait_ge(s_sc, s_sc.count)
        r.sync.wait_ge(s_g, s_g.count)
        r.sync.dma_start(out=bvl.rows(0, 1, 0), in_=dt["bvrow"][0:1, 0:D]).then_inc(s_w, 16)
        bv_tok = s_w.count
        wvsl = lambda i: wsl.at(0, 512 * i, 512 * i + 512)
        stgv = lambda i: stgw.at(0, 512 * i, 512 * i + 512)
        ctokv = [(s_v, 0), (s_v, 0)]
        mtokv = [0, 0]
        vtoks = []
        for tch in range(2):
            for n in range(2):
                if vtoks:
                    r.tensor.wait_ge(*vtoks[-1])
                for kp in range(DCH):
                    ksl = kp % 2
                    r.sync.wait_ge(*ctokv[ksl])
                    if mtokv[ksl] > 0:
                        r.sync.wait_ge(s_pe, mtokv[ksl])
                    r.sync.dma_start(out=xti[ksl], in_=ds["xlnd"][kp, :, 1024 * tch:1024 * tch + 1024]).then_inc(s_w, 16)
                    r.sync.dma_start(out=stgv(ksl), in_=dt["wvb"][kp, :, 512 * n:512 * n + 512]).then_inc(s_w, 16)
                    r.vector.wait_ge(s_w, s_w.count)
                    if mtokv[ksl] > 0:
                        r.vector.wait_ge(s_pe, mtokv[ksl])
                    r.vector.tensor_copy(out=wvsl(ksl), in_=stgv(ksl))
                    mark(r.vector, s_v, 1)
                    ctokv[ksl] = (s_v, s_v.count)
                    r.tensor.wait_ge(s_v, s_v.count)
                    for tc8 in range(DCH):
                        r.tensor.matmul(pb[tc8][:, :], xti.at(ksl, 128 * tc8, 128 * tc8 + 128),
                                        wvsl(ksl), start=(kp == 0), stop=False)
                    mark(r.tensor, s_pe, 1)
                    mtokv[ksl] = s_pe.count
                r.tensor.wait_ge(s_w, bv_tok)
                for tc8 in range(DCH):
                    r.tensor.matmul(pb[tc8][:, :], ones_r[0:1, 0:128],
                                    bvl.rows(0, 1, 0, 512 * n, 512 * n + 512), start=False, stop=True)
                mark(r.tensor, s_pe, 1)
                r.vector.wait_ge(s_pe, s_pe.count)
                for tc8 in range(DCH):
                    r.vector.tensor_copy(out=vb.at(8 * tch + tc8, 512 * n, 512 * n + 512), in_=pb[tc8][:, :])
                mark(r.vector, s_v, 1)
                vtoks.append((s_v, s_v.count))
        r.tensor.wait_ge(s_v, s_v.count)
        for ici in range(NQ):
            i0 = 512 * ici
            npan = 4 * ici + 4
            for oh in range(2):
                for jp in range(npan):
                    j0 = 128 * jp
                    ps = pb[4 + jp % 2]
                    for kp in range(DCH):
                        r.tensor.matmul(ps[:, :], kb.at(kp, j0, j0 + 128), qb.at(kp, i0, i0 + 512),
                                        start=(kp == 0), stop=(kp == DCH - 1))
                    mark(r.tensor, s_pe, 1)
                    di = (j0 - i0) // 128
                    if 0 <= di <= 3:
                        r.vector.wait_ge(s_pe, s_pe.count)
                        r.vector.tensor_tensor(out=ps[:, :], in0=ps[:, :], in1=addm(di), op=ALU.add)
                        mark(r.vector, s_v, 1)
                        r.scalar.wait_ge(s_v, s_v.count)
                    r.scalar.wait_ge(s_pe, s_pe.count)
                    r.scalar.activation(out=pT[0], in_=ps[:, :], func=AF.Exp,
                                        scale=0.03125, bias=dsb[:, jp:jp + 1])
                    mark(r.scalar, s_sc, 1)
                    r.tensor.wait_ge(s_sc, s_sc.count)
                    for oc in range(4):
                        po = 4 * oh + oc
                        r.tensor.matmul(pb[oc][:, :], vb.at(jp, 128 * po, 128 * po + 128),
                                        pT[0], start=(jp == 0), stop=(jp == npan - 1))
                    mark(r.tensor, s_pe, 1)
                    r.scalar.wait_ge(s_pe, s_pe.count)
                for oc in range(4):
                    po = 4 * oh + oc
                    r.vector.wait_ge(s_pe, s_pe.count)
                    r.vector.tensor_tensor(out=xr[:, po, i0:i0 + 512],
                                           in0=xr[:, po, i0:i0 + 512], in1=pb[oc][:, :], op=ALU.add)
                mark(r.vector, s_v, 1)
                r.tensor.wait_ge(s_v, s_v.count)


    def emit_l1a():
        AB.reset(AB_BASE)
        xln = AB.take(T, DCH)
        y = AB.take(T, DCH)
        qp = AB.take(T, 2); kp_ = AB.take(T, 2); vp = AB.take(T, 2)
        t1 = AB.take(T, 1)
        wsl = AB.take(1024, 6)
        stgw = AB.take(1024, 1, F32)
        wpr = Buf(xln.t, xln.off_b, D, DCH)
        den2 = ab

        ln_tok = emit_ln(xln, Buf(y.t, y.off_b, T, 2))
        r.tensor.wait_ge(s_v, ln_tok)
        r.sync.wait_ge(s_v, ln_tok)

        stok = [(s_v, 0)]
        wcast = [0, 0]
        wfree = [0, 0]
        act_free = [0, 0]
        zt = [0] * DCH
        dnt = [0] * DCH

        def load_w(p):
            b = 3 * (p % 2)
            for wi in range(3):
                r.sync.wait_ge(*stok[0])
                r.sync.dma_start(out=stgw[0], in_=dt["wqkvg"][p, :, D * wi:D * wi + D]).then_inc(s_w, 16)
                r.vector.wait_ge(s_w, s_w.count)
                if wfree[p % 2] > 0:
                    r.vector.wait_ge(s_pe, wfree[p % 2])
                r.vector.tensor_copy(out=wsl[b + wi], in_=stgw[0])
                mark(r.vector, s_v, 1)
                stok[0] = (s_v, s_v.count)
            wcast[p % 2] = s_v.count

        ptoks = []
        def proj_group(sl, n, out_ap, bias_ap, gate):
            i = len(ptoks)
            if i >= 3:
                r.tensor.wait_ge(*ptoks[i - 3])
            ps = pb[i % 4]
            for kx in range(DCH):
                r.tensor.matmul(ps[:, :], wsl.at(sl, 128 * kx, 128 * kx + 128),
                                xln.at(kx, 512 * n, 512 * n + 512),
                                start=(kx == 0), stop=(kx == DCH - 1))
            mark(r.tensor, s_pe, 1)
            r.scalar.wait_ge(s_pe, s_pe.count)
            if gate:
                r.scalar.wait_ge(*gate)
            r.scalar.activation(out=out_ap, in_=ps[:, :], func=AF.Identity, bias=bias_ap)
            mark(r.scalar, s_sc, 1)
            ptoks.append((s_sc, s_sc.count))

        def emit_elu(p, buf, tok):
            s = p % 2
            r.vector.wait_ge(s_sc, tok)
            r.vector.tensor_scalar(out=gsc[0], in0=buf[s], scalar1=sclb[:, :], scalar2=0.0,
                                   op0=ALU.mult, op1=ALU.min)
            mark(r.vector, s_v, 1)
            r.scalar.wait_ge(s_v, s_v.count)
            r.scalar.activation(out=gsc[0], in_=gsc[0], func=AF.Exp)
            r.scalar.activation(out=y[p], in_=buf[s], func=AF.Relu, scale=sclb[:, :])
            mark(r.scalar, s_sc, 1)
            r.vector.wait_ge(s_sc, s_sc.count)
            r.vector.tensor_add(buf[s], gsc[0], y[p])
            mark(r.vector, s_v, 1)

        def emit_products(p, vtok):
            s = p % 2
            r.vector.wait_ge(s_sc, vtok)
            if p > 0:
                r.vector.wait_ge(s_pe, dnt[p - 1])
            r.vector.tensor_mul(t1[0], kp_[s], vp[s])
            r.vector.tensor_tensor_scan(out=vp[s], data0=ones_t[:, :], data1=t1[0],
                                        initial=0.0, op0=ALU.mult, op1=ALU.add).then_inc(s_v, 1)
            r.vector.wait_ge(s_v, s_v.count)
            r.vector.tensor_tensor_scan(out=gsc[0], data0=ones_t[:, :], data1=kp_[s],
                                        initial=0.0, op0=ALU.mult, op1=ALU.add).then_inc(s_v, 1)
            r.vector.wait_ge(s_v, s_v.count)
            r.vector.tensor_mul(t1[0], qp[s], gsc[0])
            mark(r.vector, s_v, 1)
            zt[p] = s_v.count

        def emit_denom(p):
            r.tensor.wait_ge(s_v, zt[p])
            ctoks = []
            for n in range(NQ):
                if n >= 2:
                    r.tensor.wait_ge(*ctoks[n - 2])
                ps = pb[4 + n % 2]
                r.tensor.matmul(ps[0:2, :], hsum[:, :], t1.at(0, 512 * n, 512 * n + 512),
                                start=True, stop=True)
                mark(r.tensor, s_pe, 1)
                r.vector.wait_ge(s_pe, s_pe.count)
                r.vector.tensor_copy(out=den2.rows(0, 2, 0, 512 * n, 512 * n + 512), in_=ps[0:2, :])
                mark(r.vector, s_v, 1)
                ctoks.append((s_v, s_v.count))
            dnt[p] = s_pe.count

        def emit_tail(p):
            s = p % 2
            r.vector.tensor_scalar(out=den2.rows(0, 2, 0), in0=den2.rows(0, 2, 0),
                                   scalar1=0.001, scalar2=None, op0=ALU.max)
            r.vector.reciprocal(out=den2.rows(0, 2, 0), in_=den2.rows(0, 2, 0)).then_inc(s_v, 1)
            r.vector.wait_ge(s_v, s_v.count)
            r.vector.engine_nop().then_inc(s_v, 1)
            r.gpsimd.wait_ge(s_v, s_v.count)
            r.gpsimd.dma_start(out=ds["den_dram"][0:2, :], in_=den2.rows(0, 2, 0)).then_inc(s_g, 16)
            r.gpsimd.wait_ge(s_g, s_g.count)
            r.gpsimd.dma_start(out=ab.rows(0, 64, 1), in_=ds["den_dram"][0:1, :].broadcast_to((64, T))).then_inc(s_g, 16)
            r.gpsimd.dma_start(out=ab.rows(64, 128, 1), in_=ds["den_dram"][1:2, :].broadcast_to((64, T))).then_inc(s_g, 16)
            r.vector.wait_ge(s_pe, dnt[p])
            r.vector.tensor_mul(t1[0], qp[s], vp[s])
            mark(r.vector, s_v, 1)
            act_free[s] = s_v.count
            r.vector.wait_ge(s_g, s_g.count)
            r.vector.tensor_mul(y[p], t1[0], ab[1])
            mark(r.vector, s_v, 1)

        load_w(0)
        for p in range(DCH):
            s = p % 2
            if p + 1 < DCH:
                load_w(p + 1)
            r.tensor.wait_ge(s_v, wcast[s])
            gate = (s_v, act_free[s]) if act_free[s] > 0 else None
            toks3 = []
            for wi, (dst, bj) in enumerate(((qp, 0), (kp_, 8), (vp, 16))):
                for n in range(NQ):
                    proj_group(3 * s + wi, n, dst.at(s, 512 * n, 512 * n + 512),
                               bcol("bqkv", bj + p), gate)
                    gate = None
                toks3.append(ptoks[-1][1])
                if wi == 0:
                    emit_elu(p, qp, toks3[0])
                elif wi == 1:
                    emit_elu(p, kp_, toks3[1])
            wfree[s] = s_pe.count
            if p >= 1:
                emit_denom(p - 1)
                emit_tail(p - 1)
            emit_products(p, toks3[2])
        emit_denom(DCH - 1)
        emit_tail(DCH - 1)

        r.vector.wait_ge(s_pe, s_pe.count)
        wtoks = list(stok)
        for kx in range(DCH):
            r.sync.wait_ge(*wtoks[0])
            r.sync.dma_start(out=stgw[0], in_=dt["wprj"][kx, :, :]).then_inc(s_w, 16)
            r.vector.wait_ge(s_w, s_w.count)
            r.vector.tensor_copy(out=wpr.at(kx, 0, D), in_=stgw[0])
            mark(r.vector, s_v, 1)
            wtoks[0] = (s_v, s_v.count)
        r.tensor.wait_ge(s_v, s_v.count)
        r.tensor.wait_ge(s_sc, s_sc.count)
        rtoks = []
        addtoks = []
        items = [(po, n) for po in range(DCH) for n in range(NQ)]
        for i, (po, n) in enumerate(items):
            if i >= 3:
                r.tensor.wait_ge(*rtoks[i - 3])
            ps = pb[i % 4]
            for kx in range(DCH):
                r.tensor.matmul(ps[:, :], wpr.at(kx, 128 * po, 128 * po + 128),
                                y.at(kx, 512 * n, 512 * n + 512),
                                start=(kx == 0), stop=(kx == DCH - 1))
            mark(r.tensor, s_pe, 1)
            r.scalar.wait_ge(s_pe, s_pe.count)
            if i >= 4:
                r.scalar.wait_ge(*addtoks[i - 4])
            tmp = gsc.rows(0, P, 0, 512 * (i % 4), 512 * (i % 4) + 512)
            r.scalar.activation(out=tmp, in_=ps[:, :], func=AF.Identity, bias=bcol("bproj", po))
            mark(r.scalar, s_sc, 1)
            rtoks.append((s_sc, s_sc.count))
            r.vector.wait_ge(s_sc, s_sc.count)
            r.vector.tensor_tensor(out=xr[:, po, 512 * n:512 * n + 512],
                                   in0=xr[:, po, 512 * n:512 * n + 512], in1=tmp, op=ALU.add)
            mark(r.vector, s_v, 1)
            addtoks.append((s_v, s_v.count))


    def emit_l1f(first_call):
        AB.reset(AB_BASE)
        xln = AB.take(T, DCH)
        h = AB.take(512, 32)
        wqsl = AB.take(1024, 4)
        wpsl = AB.take(1024, 4)
        mid = AB.take(T, 1)
        midp = AB.take(512, 1)
        bfc = AB.take(4096, 1)
        bpr = AB.take(1024, 1)
        afc = AB.take(16 * DCH, 1)
        apr = AB.take(16 * 32, 1)
        stg = AB.take(512, 2, F32)
        qst = Buf(h.t, h.off_b, 4096, 1, F32)
        qout = Buf(h.t, h.off_b + 8192, 4096, 1)
        qi32 = Buf(h.t, h.off_b + 12288, 2048, 1, F32)

        fc_toks = []
        pj_toks = []
        fcw_toks = []
        pjw_toks = []
        NFCG = 32 * NQ
        def prefetch_fc(gi):
            if gi >= NFCG:
                return
            if gi >= 4:
                r.gpsimd.wait_ge(s_pe, fc_toks[gi - 4])
            r.gpsimd.dma_start(out=wqsl[gi % 4], in_=ds["wqfc_dram"][gi % 32, :, :]).then_inc(s_g, 16)
            fcw_toks.append(s_g.count)
        def prefetch_pj(gj):
            if gj >= NFCG:
                return
            if gj >= 4:
                r.sync.wait_ge(s_pe, pj_toks[gj - 4])
            r.sync.dma_start(out=wpsl[gj % 4], in_=ds["wqpr_dram"][gj % 32, :, :]).then_inc(s_w, 16)
            pjw_toks.append(s_w.count)

        if not first_call:
            ent_v, ent_pe = s_v.count, s_pe.count
            r.gpsimd.wait_ge(s_v, ent_v)
            r.gpsimd.wait_ge(s_pe, ent_pe)
            for gi in range(4):
                prefetch_fc(gi)
            r.sync.wait_ge(s_v, ent_v)
            r.sync.wait_ge(s_pe, ent_pe)
            for gj in range(4):
                prefetch_pj(gj)

        def consume(p, slot):
            pass
        ln_tok = emit_ln(xln, Buf(h.t, h.off_b, T, 2))
        r.sync.wait_ge(s_v, ln_tok)
        r.tensor.wait_ge(s_v, ln_tok)

        lora_items = ([("afcT", kx, afc, 16 * kx, 16) for kx in range(DCH)] +
                      [("bfcT", pc2, bfc, 512 * pc2, 512) for pc2 in range(8)] +
                      [("aprT", fk, apr, 16 * fk, 16) for fk in range(32)] +
                      [("bprT", pc2, bpr, 512 * pc2, 512) for pc2 in range(2)])
        lvt = {}
        for li, (nm, idx, dst_b, c0, w2) in enumerate(lora_items):
            if li >= 2:
                r.sync.wait_ge(s_v, lvt[li - 2])
            else:
                r.sync.wait_ge(s_v, s_v.count)
            if nm in ("bfcT", "bprT"):
                src = dt[nm][0, :, 512 * idx:512 * idx + 512]
                stga = stg.rows(0, 16, li % 2, 0, w2)
                dsta = dst_b.rows(0, 16, 0, c0, c0 + w2)
            else:
                src = dt[nm][idx, :, :]
                stga = stg.at(li % 2, 0, w2)
                dsta = dst_b.at(0, c0, c0 + w2)
            r.sync.dma_start(out=stga, in_=src).then_inc(s_w, 16)
            r.vector.wait_ge(s_w, s_w.count)
            r.vector.tensor_copy(out=dsta, in_=stga)
            mark(r.vector, s_v, 1)
            lvt[li] = s_v.count

        if first_call:
            for wname, nch, cols, dst, oc_layout in (("wfcT", DCH, 4096, "wqfc_dram", True),
                                                     ("wprT", 32, D, "wqpr_dram", False)):
                for kx in range(nch):
                    r.gpsimd.wait_ge(s_v, s_v.count)
                    r.gpsimd.dma_start(out=qst.at(0, 0, cols), in_=dt[wname][kx, :, :]).then_inc(s_g, 16)
                    r.vector.wait_ge(s_g, s_g.count)
                    r.vector.reduce_max(out=dsb[:, 24 + kx % 2:25 + kx % 2], in_=qst.at(0, 0, cols),
                                        axis=mybir.AxisListType.X, apply_absolute_value=True).then_inc(s_v, 1)
                    r.vector.wait_ge(s_v, s_v.count)
                    if kx == 0:
                        r.vector.tensor_copy(out=dsb[:, 26:27], in_=dsb[:, 24:25])
                    else:
                        r.vector.tensor_max(dsb[:, 26:27], dsb[:, 26:27], dsb[:, 24 + kx % 2:25 + kx % 2])
                    mark(r.vector, s_v, 1)
                    r.gpsimd.wait_ge(s_v, s_v.count)
                r.gpsimd.tensor_reduce(out=dsb[0:1, 27:28], in_=dsb[:, 26:27],
                                       axis=mybir.AxisListType.C, op=ALU.max).then_inc(s_gc, 1)
                r.vector.wait_ge(s_gc, s_gc.count)
                r.vector.reciprocal(out=dsb[0:1, 28:29], in_=dsb[0:1, 27:28]).then_inc(s_v, 1)
                r.vector.wait_ge(s_v, s_v.count)
                r.vector.tensor_scalar(out=dsb[0:1, 28:29], in0=dsb[0:1, 28:29], scalar1=127.0, scalar2=None, op0=ALU.mult)
                r.vector.tensor_scalar(out=dsb[0:1, 29:30], in0=dsb[0:1, 27:28], scalar1=1.0 / 127.0, scalar2=None, op0=ALU.mult)
                mark(r.vector, s_v, 1)
                r.gpsimd.wait_ge(s_v, s_v.count)
                r.gpsimd.dma_start(out=ds["den_dram"][0:1, 0:2], in_=dsb[0:1, 28:30]).then_inc(s_g, 16)
                r.gpsimd.wait_ge(s_g, s_g.count)
                r.gpsimd.dma_start(out=dsb[:, 30:31], in_=ds["den_dram"][0:1, 0:1].broadcast_to((P, 1))).then_inc(s_g, 16)
                r.gpsimd.dma_start(out=dsb[:, 31:32], in_=ds["den_dram"][0:1, 1:2].broadcast_to((P, 1))).then_inc(s_g, 16)
                r.vector.wait_ge(s_g, s_g.count)
                for kx in range(nch):
                    r.gpsimd.wait_ge(s_v, s_v.count)
                    r.gpsimd.dma_start(out=qst.at(0, 0, cols), in_=dt[wname][kx, :, :]).then_inc(s_g, 16)
                    r.vector.wait_ge(s_g, s_g.count)
                    for pc in range(0, cols, 1024):
                        y32 = qst.at(0, pc, pc + 1024)
                        qi = qi32.at(0, 0, 1024).bitcast(I32) if False else Buf(qi32.t, qi32.off_b, 1024, 1, F32).at(0, 0, 1024).bitcast(I32)
                        r.vector.tensor_scalar(out=y32, in0=y32, scalar1=dsb[:, 30:31], scalar2=None, op0=ALU.mult)
                        r.vector.tensor_copy(out=qi, in_=y32)
                        r.vector.tensor_scalar(out=qout.at(0, pc, pc + 1024), in0=qi,
                                               scalar1=-128.0, scalar2=127.0, op0=ALU.max, op1=ALU.min)
                        r.vector.tensor_scalar(out=qout.at(0, pc, pc + 1024), in0=qout.at(0, pc, pc + 1024),
                                               scalar1=dsb[:, 31:32], scalar2=None, op0=ALU.mult)
                    mark(r.vector, s_v, 1)
                    r.gpsimd.wait_ge(s_v, s_v.count)
                    if oc_layout:
                        for oc in range(32):
                            r.gpsimd.dma_start(out=ds[dst][oc, :, 128 * kx:128 * kx + 128],
                                               in_=qout.at(0, 128 * oc, 128 * oc + 128)).then_inc(s_g, 16)
                    else:
                        r.gpsimd.dma_start(out=ds[dst][kx, :, :], in_=qout.at(0, 0, D)).then_inc(s_g, 16)
                    r.vector.wait_ge(s_g, s_g.count)
            qtok = s_g.count
            for gi in range(4):
                prefetch_fc(gi)
            r.sync.wait_ge(s_g, qtok)
            for gj in range(4):
                prefetch_pj(gj)

        r.tensor.wait_ge(s_v, s_v.count)
        for n in range(NQ):
            ps = pb[2]
            for kx in range(DCH):
                r.tensor.matmul(ps[0:16, :], afc.at(0, 16 * kx, 16 * kx + 16),
                                xln.at(kx, 512 * n, 512 * n + 512),
                                start=(kx == 0), stop=(kx == DCH - 1))
            mark(r.tensor, s_pe, 1)
            r.scalar.wait_ge(s_pe, s_pe.count)
            r.scalar.activation(out=mid.rows(0, 16, 0, 512 * n, 512 * n + 512), in_=ps[0:16, :],
                                func=AF.Identity, scale=2.0)
            mark(r.scalar, s_sc, 1)
            r.tensor.wait_ge(s_sc, s_sc.count)
        mid_tok = s_sc.count
        ge_toks = []
        for tc in range(NQ):
            t0 = 512 * tc
            for oc in range(32):
                gi = len(fc_toks)
                ps = pb[gi % 4]
                r.tensor.wait_ge(s_g, fcw_toks[gi])
                if gi >= 4:
                    r.tensor.wait_ge(s_sc, ge_toks[gi - 4])
                if tc == 0 and gi == 2:
                    r.tensor.wait_ge(s_sc, mid_tok)
                for kx in range(DCH):
                    r.tensor.matmul(ps[:, :], wqsl.at(gi % 4, 128 * kx, 128 * kx + 128),
                                    xln.at(kx, t0, t0 + 512), start=(kx == 0), stop=False)
                r.tensor.matmul(ps[:, :], bfc.rows(0, 16, 0, 128 * oc, 128 * oc + 128),
                                mid.rows(0, 16, 0, t0, t0 + 512), start=False, stop=True)
                mark(r.tensor, s_pe, 1)
                fc_toks.append(s_pe.count)
                prefetch_fc(gi + 4)
                r.scalar.wait_ge(s_pe, fc_toks[-1])
                r.scalar.activation(out=h[oc], in_=ps[:, :], func=AF.Gelu, bias=bcol("b_fc", oc))
                mark(r.scalar, s_sc, 1)
                ge_toks.append(s_sc.count)
            r.tensor.wait_ge(s_sc, s_sc.count)
            ps = pb[2]
            for fk in range(32):
                r.tensor.matmul(ps[0:16, :], apr.at(0, 16 * fk, 16 * fk + 16), h[fk],
                                start=(fk == 0), stop=(fk == 31))
            mark(r.tensor, s_pe, 1)
            r.scalar.wait_ge(s_pe, s_pe.count)
            r.scalar.activation(out=midp.rows(0, 16, 0), in_=ps[0:16, :], func=AF.Identity, scale=2.0)
            mark(r.scalar, s_sc, 1)
            r.tensor.wait_ge(s_sc, s_sc.count)
            for fk in range(32):
                gj = len(pj_toks)
                r.tensor.wait_ge(s_w, pjw_toks[gj])
                for oc2 in range(DCH):
                    r.tensor.matmul(pb[oc2][:, :], wpsl.at(gj % 4, 128 * oc2, 128 * oc2 + 128),
                                    h[fk], start=(fk == 0), stop=False)
                mark(r.tensor, s_pe, 1)
                pj_toks.append(s_pe.count)
                prefetch_pj(gj + 4)
            for oc2 in range(DCH):
                r.tensor.matmul(pb[oc2][:, :], bpr.rows(0, 16, 0, 128 * oc2, 128 * oc2 + 128),
                                midp.rows(0, 16, 0), start=False, stop=True)
            mark(r.tensor, s_pe, 1)
            ptok = s_pe.count
            for oc2 in range(DCH):
                r.scalar.wait_ge(s_pe, ptok)
                tmp = gsc.rows(0, P, 0, 512 * (oc2 % 4), 512 * (oc2 % 4) + 512)
                if oc2 >= 4:
                    r.scalar.wait_ge(s_v, s_v.count)
                r.scalar.activation(out=tmp, in_=pb[oc2][:, :], func=AF.Identity, bias=bcol("b_proj", oc2))
                mark(r.scalar, s_sc, 1)
                r.vector.wait_ge(s_sc, s_sc.count)
                r.vector.tensor_tensor(out=xr[:, oc2, t0:t0 + 512],
                                       in0=xr[:, oc2, t0:t0 + 512], in1=tmp, op=ALU.add)
                mark(r.vector, s_v, 1)
            r.tensor.wait_ge(s_v, s_v.count)
            r.scalar.wait_ge(s_v, s_v.count)


    def emit_l1b():
        AB.reset(AB_BASE)
        xln = AB.take(T, DCH)
        lat = AB.take(T, 2)
        kbf = Buf(xln.t, xln.off_b, T, DCH)
        vch = 65
        vaug = AB.take(16 * vch, TCH)
        wsl = AB.take(1024, 2)
        stgw = AB.take(1024, 1, F32)
        pT = AB.take(512, 3)
        qsl = AB.take(512, 4)
        uch = AB.take(T, 2)
        bvl = AB.take(1024, 1)

        def consume(p, slot):
            r.gpsimd.wait_ge(s_v, s_v.count)
            r.gpsimd.dma_start(out=ds["xlnd"][p, :, :], in_=xln[p]).then_inc(s_g, 16)
        ln_tok = emit_ln(xln, Buf(vaug.t, vaug.off_b, T, 2), consume=consume)
        r.tensor.wait_ge(s_v, ln_tok)
        r.sync.wait_ge(s_v, ln_tok)
        r.sync.dma_start(out=bvl.rows(0, 1, 0), in_=dt["bvrow"][0:1, D:2 * D]).then_inc(s_w, 16)
        etoks = []
        dtoks = []
        stokc = [(s_v, 0)]
        wfree2 = [0, 0]
        for po in range(DCH):
            sl = po % 2
            r.sync.wait_ge(*stokc[0])
            r.sync.dma_start(out=stgw[0], in_=dt["wqBg"][po, :, :]).then_inc(s_w, 16)
            r.vector.wait_ge(s_w, s_w.count)
            if wfree2[sl] > 0:
                r.vector.wait_ge(s_pe, wfree2[sl])
            r.vector.tensor_copy(out=wsl[sl], in_=stgw[0])
            mark(r.vector, s_v, 1)
            stokc[0] = (s_v, s_v.count)
            r.tensor.wait_ge(s_v, s_v.count)
            for n in range(NQ):
                i = len(etoks)
                if i >= 3:
                    r.tensor.wait_ge(*etoks[i - 3])
                ps = pb[i % 4]
                for kx in range(DCH):
                    r.tensor.matmul(ps[:, :], wsl.at(sl, 128 * kx, 128 * kx + 128),
                                    xln.at(kx, 512 * n, 512 * n + 512),
                                    start=(kx == 0), stop=(kx == DCH - 1))
                mark(r.tensor, s_pe, 1)
                r.scalar.wait_ge(s_pe, s_pe.count)
                if i >= 2:
                    r.scalar.wait_ge(*dtoks[i - 2])
                r.scalar.activation(out=qsl[i % 2], in_=ps[:, :], func=AF.Identity, bias=bcol("Bq", po))
                mark(r.scalar, s_sc, 1)
                etoks.append((s_sc, s_sc.count))
                r.scalar.dma_start(out=ds["q_dram"][po, :, 512 * n:512 * n + 512], in_=qsl[i % 2]).then_inc(s_s, 16)
                dtoks.append((s_s, s_s.count))
            wfree2[sl] = s_pe.count
        for po in range(2):
            sl = po % 2
            r.sync.wait_ge(*stokc[0])
            r.sync.dma_start(out=stgw[0], in_=dt["wlBg"][po, :, :]).then_inc(s_w, 16)
            r.vector.wait_ge(s_w, s_w.count)
            if wfree2[sl] > 0:
                r.vector.wait_ge(s_pe, wfree2[sl])
            r.vector.tensor_copy(out=wsl[sl], in_=stgw[0])
            mark(r.vector, s_v, 1)
            stokc[0] = (s_v, s_v.count)
            wfree2[sl] = 0
            r.tensor.wait_ge(s_v, s_v.count)
            for n in range(NQ):
                ps = pb[n % 2]
                for kx in range(DCH):
                    r.tensor.matmul(ps[:, :], wsl.at(po % 2, 128 * kx, 128 * kx + 128),
                                    xln.at(kx, 512 * n, 512 * n + 512),
                                    start=(kx == 0), stop=(kx == DCH - 1))
                mark(r.tensor, s_pe, 1)
                r.scalar.wait_ge(s_pe, s_pe.count)
                r.scalar.activation(out=lat.at(po, 512 * n, 512 * n + 512), in_=ps[:, :],
                                    func=AF.Identity, bias=bcol("Bl", po))
                mark(r.scalar, s_sc, 1)
                r.tensor.wait_ge(s_sc, s_sc.count)
        r.sync.wait_ge(s_sc, s_sc.count)
        r.vector.wait_ge(s_pe, s_pe.count)
        for kx in range(2):
            r.sync.wait_ge(s_v, s_v.count)
            r.sync.dma_start(out=stgw[0], in_=dt["wkB"][kx, :, :]).then_inc(s_w, 16)
            r.vector.wait_ge(s_w, s_w.count)
            r.vector.tensor_copy(out=wsl[kx], in_=stgw[0])
            mark(r.vector, s_v, 1)
            r.sync.wait_ge(s_v, s_v.count)
        r.tensor.wait_ge(s_v, s_v.count)
        kt_toks = []
        for po in range(DCH):
            for n in range(NQ):
                i3 = len(kt_toks)
                ps = pb[i3 % 4]
                if i3 >= 4:
                    r.tensor.wait_ge(s_sc, kt_toks[i3 - 4])
                for kx in range(2):
                    r.tensor.matmul(ps[:, :], wsl.at(kx, 128 * po, 128 * po + 128),
                                    lat.at(kx, 512 * n, 512 * n + 512),
                                    start=(kx == 0), stop=(kx == 1))
                mark(r.tensor, s_pe, 1)
                r.scalar.wait_ge(s_pe, s_pe.count)
                r.scalar.activation(out=kbf.at(po, 512 * n, 512 * n + 512), in_=ps[:, :],
                                    func=AF.Identity, bias=bcol("Bk", po))
                mark(r.scalar, s_sc, 1)
                kt_toks.append(s_sc.count)
        r.vector.wait_ge(s_pe, s_pe.count)
        for kx in range(2):
            r.sync.wait_ge(s_v, s_v.count)
            r.sync.dma_start(out=stgw[0], in_=dt["wvB"][kx, :, :]).then_inc(s_w, 16)
            r.vector.wait_ge(s_w, s_w.count)
            r.vector.tensor_copy(out=wsl[kx], in_=stgw[0])
            mark(r.vector, s_v, 1)
            r.sync.wait_ge(s_v, s_v.count)
        for tc in range(TCH):
            for hh in range(16):
                r.vector.tensor_copy(out=vaug.at(tc, vch * hh + 64, vch * hh + 65), in_=ones_t[:, 0:1])
        mark(r.vector, s_v, 1)
        r.tensor.wait_ge(s_v, s_v.count)
        r.tensor.wait_ge(s_w, s_w.count)
        v_toks = []
        for tc in range(TCH):
            for n in range(2):
                i4 = len(v_toks)
                ps = pb[i4 % 4]
                if i4 >= 4:
                    r.tensor.wait_ge(s_v, v_toks[i4 - 4])
                else:
                    r.tensor.wait_ge(s_sc, kt_toks[len(kt_toks) - 4 + i4])
                for kx in range(2):
                    r.tensor.matmul(ps[:, :], lat.rows(0, P, kx, 128 * tc, 128 * tc + 128),
                                    wsl.at(kx, 512 * n, 512 * n + 512), start=(kx == 0), stop=False)
                r.tensor.matmul(ps[:, :], ones_r[0:1, 0:128], bvl.rows(0, 1, 0, 512 * n, 512 * n + 512),
                                start=False, stop=True)
                mark(r.tensor, s_pe, 1)
                r.vector.wait_ge(s_pe, s_pe.count)
                for hh in range(8):
                    h = 8 * n + hh
                    r.vector.tensor_copy(out=vaug.at(tc, vch * h, vch * h + 64), in_=ps[:, 64 * hh:64 * hh + 64])
                mark(r.vector, s_v, 1)
                v_toks.append(s_v.count)
        if L1BPHASE < 2:
            return
        hq = [(hh2, qc2) for hh2 in range(H) for qc2 in range(NQ)]
        qdma = []
        sc_last = []
        av_last = []
        vcp = []
        gdma = []
        sdma = []
        exp_toks = {}
        avt = {}
        s_s_base = s_s.count

        def issue_q(i):
            h2, qc2 = hq[i]
            hp2, ho2 = h2 // 2, 64 * (h2 % 2)
            if i >= 8:
                r.sync.wait_ge(s_pe, sc_last[i - 8])
            else:
                r.sync.wait_ge(s_s, s_s_base)
            r.sync.dma_start(out=qsl.rows(ho2, ho2 + 64, i % 4),
                             in_=ds["q_dram"][hp2, ho2:ho2 + 64, 512 * qc2:512 * qc2 + 512]).then_inc(s_w, 16)
            qdma.append(s_w.count)

        issue_q(0)
        issue_q(1)
        r.tensor.wait_ge(s_v, v_toks[-1])
        for i, (h, qc) in enumerate(hq):
            hp, ho = h // 2, 64 * (h % 2)
            q0 = 512 * qc
            if i + 2 < len(hq):
                issue_q(i + 2)
            pout = pb[4 + i % 2]
            r.tensor.wait_ge(s_w, qdma[i])
            if i >= 2:
                r.tensor.wait_ge(s_v, vcp[i - 2])
            for jp in range(TCH):
                ps = pb[6 + jp % 2]
                if jp >= 2:
                    r.tensor.wait_ge(s_sc, exp_toks[(i, jp - 2)])
                elif i > 0:
                    r.tensor.wait_ge(s_sc, exp_toks[(i - 1, TCH - 2 + jp)])
                r.tensor.matmul(ps[:, :], kbf.rows(ho, ho + 64, hp, 128 * jp, 128 * jp + 128),
                                qsl.rows(ho, ho + 64, i % 4), start=True, stop=True)
                mark(r.tensor, s_pe, 1)
                sctok = s_pe.count
                if jp == TCH - 1:
                    sc_last.append(sctok)
                if jp >= 1:
                    r.tensor.wait_ge(s_sc, exp_toks[(i, jp - 1)])
                    r.tensor.matmul(pout[0:65, :], vaug.at(jp - 1, vch * h, vch * h + 65),
                                    pT[(jp - 1) % 3], start=(jp == 1), stop=False)
                    mark(r.tensor, s_pe, 1)
                    avt[(i, jp - 1)] = s_pe.count
                r.scalar.wait_ge(s_pe, sctok)
                if jp >= 3:
                    r.scalar.wait_ge(s_pe, avt[(i, jp - 3)])
                elif i > 0:
                    r.scalar.wait_ge(s_pe, avt[(i - 1, {0: 15, 1: 13, 2: 14}[jp])])
                r.scalar.activation(out=pT[jp % 3], in_=ps[:, :], func=AF.Exp, scale=0.125)
                mark(r.scalar, s_sc, 1)
                exp_toks[(i, jp)] = s_sc.count
            r.tensor.wait_ge(s_sc, exp_toks[(i, TCH - 1)])
            r.tensor.matmul(pout[0:65, :], vaug.at(TCH - 1, vch * h, vch * h + 65),
                            pT[(TCH - 1) % 3], start=False, stop=True)
            mark(r.tensor, s_pe, 1)
            avt[(i, TCH - 1)] = s_pe.count
            av_last.append(s_pe.count)
            r.vector.wait_ge(s_pe, av_last[i])
            if i >= 2:
                r.vector.wait_ge(s_g, gdma[i - 2])
                r.vector.wait_ge(s_s, sdma[i - 2])
            r.vector.tensor_copy(out=uch.rows(0, 64, i % 2, 0, 512), in_=pout[0:64, :])
            r.vector.tensor_copy(out=dsb[64 + 32 * (i % 2):65 + 32 * (i % 2), 0:512], in_=pout[64:65, :])
            mark(r.vector, s_v, 1)
            vcp.append(s_v.count)
            r.gpsimd.wait_ge(s_v, vcp[i])
            r.gpsimd.dma_start(out=ds["xlnd"][hp, ho:ho + 64, q0:q0 + 512],
                               in_=uch.rows(0, 64, i % 2, 0, 512)).then_inc(s_g, 16)
            gdma.append(s_g.count)
            r.scalar.wait_ge(s_v, vcp[i])
            r.scalar.dma_start(out=ds["den_dram"][h:h + 1, q0:q0 + 512],
                               in_=dsb[64 + 32 * (i % 2):65 + 32 * (i % 2), 0:512]).then_inc(s_s, 16)
            sdma.append(s_s.count)
        if L1BPHASE < 3:
            return
        r.gpsimd.wait_ge(s_s, s_s.count)
        for n in range(NQ):
            c0 = 512 * n
            if n > 0:
                r.gpsimd.wait_ge(s_v, s_v.count)
            r.gpsimd.dma_start(out=dsb[0:16, 0:512], in_=ds["den_dram"][0:16, c0:c0 + 512]).then_inc(s_g, 16)
            r.vector.wait_ge(s_g, s_g.count)
            r.vector.reciprocal(out=dsb[0:16, 0:512], in_=dsb[0:16, 0:512]).then_inc(s_v, 1)
            r.vector.wait_ge(s_v, s_v.count)
            r.vector.engine_nop().then_inc(s_v, 1)
            r.gpsimd.wait_ge(s_v, s_v.count)
            r.gpsimd.dma_start(out=ds["den_dram"][0:16, c0:c0 + 512], in_=dsb[0:16, 0:512]).then_inc(s_g, 16)
        r.gpsimd.wait_ge(s_g, s_g.count)
        uh = Buf(xln.t, xln.off_b, T, DCH)
        for kx in range(DCH):
            r.gpsimd.wait_ge(s_v, s_v.count)
            r.gpsimd.dma_start(out=uch[kx % 2], in_=ds["xlnd"][kx, :, :]).then_inc(s_g, 16)
            r.gpsimd.dma_start(out=gsc.rows(0, 64, 0), in_=ds["den_dram"][2 * kx:2 * kx + 1, :].broadcast_to((64, T))).then_inc(s_g, 16)
            r.gpsimd.dma_start(out=gsc.rows(64, 128, 0), in_=ds["den_dram"][2 * kx + 1:2 * kx + 2, :].broadcast_to((64, T))).then_inc(s_g, 16)
            r.vector.wait_ge(s_g, s_g.count)
            r.vector.tensor_mul(uh[kx], uch[kx % 2], gsc[0])
            mark(r.vector, s_v, 1)
        wof2 = Buf(vaug.t, vaug.off_b, 1024, TCH)
        for kx in range(DCH):
            r.sync.wait_ge(s_v, s_v.count)
            r.sync.dma_start(out=stgw[0], in_=dt["woB"][kx, :, :]).then_inc(s_w, 16)
            r.vector.wait_ge(s_w, s_w.count)
            r.vector.tensor_copy(out=wof2.at(kx, 0, D), in_=stgw[0])
            mark(r.vector, s_v, 1)
        r.tensor.wait_ge(s_v, s_v.count)
        wo_toks = []
        wo_res = []
        for po in range(DCH):
            for n in range(NQ):
                i5 = len(wo_toks)
                if i5 >= 3:
                    r.tensor.wait_ge(s_sc, wo_toks[i5 - 3])
                ps = pb[i5 % 4]
                for kx in range(DCH):
                    r.tensor.matmul(ps[:, :], wof2.at(kx, 128 * po, 128 * po + 128),
                                    uh.at(kx, 512 * n, 512 * n + 512),
                                    start=(kx == 0), stop=(kx == DCH - 1))
                mark(r.tensor, s_pe, 1)
                r.scalar.wait_ge(s_pe, s_pe.count)
                if i5 >= 4:
                    r.scalar.wait_ge(s_v, wo_res[i5 - 4])
                tmp = gsc.rows(0, P, 0, 512 * (i5 % 4), 512 * (i5 % 4) + 512)
                r.scalar.activation(out=tmp, in_=ps[:, :], func=AF.Identity, bias=bcol("Bo", po))
                mark(r.scalar, s_sc, 1)
                wo_toks.append(s_sc.count)
                r.vector.wait_ge(s_sc, wo_toks[-1])
                r.vector.tensor_tensor(out=xr[:, po, 512 * n:512 * n + 512],
                                       in0=xr[:, po, 512 * n:512 * n + 512], in1=tmp, op=ALU.add)
                mark(r.vector, s_v, 1)
                wo_res.append(s_v.count)

    if ONLY_SUB is not None:
        {1: emit_l0a, 2: emit_l0f, 3: emit_l0b, 4: emit_l0f, 5: emit_l1a,
         6: lambda: emit_l1f(True), 7: emit_l1b, 8: lambda: emit_l1f(True)}[ONLY_SUB]()
        barrier()
        for p in range(DCH):
            r.gpsimd.dma_start(out=yT[p, :, :], in_=xr[:, p, :]).then_inc(s_g, 16)
        r.gpsimd.wait_ge(s_g, s_g.count)
        return
    if n_sub >= 1:
        emit_l0a()
        barrier()
    if n_sub >= 2:
        emit_l0f()
        barrier()
    if n_sub >= 3:
        emit_l0b()
        barrier()
    if n_sub >= 4:
        emit_l0f()
        barrier()
    if n_sub >= 5:
        emit_l1a()
        barrier()
    if n_sub >= 6:
        emit_l1f(True)
        barrier()
    if n_sub >= 7:
        emit_l1b()
        barrier()
    if n_sub >= 8:
        emit_l1f(False)
        barrier()

    for p in range(DCH):
        r.gpsimd.dma_start(out=yT[p, :, :], in_=xr[:, p, :]).then_inc(s_g, 16)
    if DBG_COLS is not None:
        r.gpsimd.dma_start(out=env["ydbg"][:, 0:DBG_COLS[1] - DBG_COLS[0]],
                           in_=AB.t[:, DBG_COLS[0]:DBG_COLS[1]]).then_inc(s_g, 16)
    r.gpsimd.wait_ge(s_g, s_g.count)
    if DBG_OVERWAIT:
        r.gpsimd.wait_ge(s_w, s_w.count + 16)


def _prep_inputs(kw):
    x = np.asarray(kw["L0_attn_a_x"])
    perm = _perm_eo()
    cosT, sinT = _rope_tables()
    mm = _masks()

    def T_(a):
        return np.ascontiguousarray(np.asarray(a).T)

    com = {}
    com["wq0"] = _wchunks(np.asarray(kw["L0_attn_a_wq"])[:, perm])
    com["wk0"] = _wchunks(np.asarray(kw["L0_attn_a_wk"])[:, perm])
    com["wv0"] = _wchunks(np.asarray(kw["L0_attn_a_wv"]))
    com["wo0"] = _wchunks(np.asarray(kw["L0_attn_a_wo_w"]))
    com["wqkg"] = np.ascontiguousarray(np.concatenate(
        [_wgroups(np.asarray(kw["L0_attn_a_wq"])[:, perm]),
         _wgroups(np.asarray(kw["L0_attn_a_wk"])[:, perm]),
         _wgroups(np.asarray(kw["L0_attn_a_wv"]))], axis=2))
    com["wqbg"] = _wgroups(T_(kw["L0_attn_b_Wq"]))
    com["wkbg"] = _wgroups(T_(kw["L0_attn_b_Wk"]))
    _wqkv_t = T_(kw["L1_attn_a_Wqkv"])
    com["wqkvg"] = np.ascontiguousarray(np.concatenate(
        [_wgroups(_wqkv_t[:, 0:D]), _wgroups(_wqkv_t[:, D:2*D]), _wgroups(_wqkv_t[:, 2*D:3*D])], axis=2))
    com["wqBg"] = _wgroups(np.asarray(kw["L1_attn_b_Wq"]))
    com["wqb"] = _wchunks(T_(kw["L0_attn_b_Wq"]))
    com["wkb"] = _wchunks(T_(kw["L0_attn_b_Wk"]))
    com["wvb"] = _wchunks(T_(kw["L0_attn_b_Wv"]))
    com["wfic"] = _wchunks(np.concatenate([T_(kw["L0_ffn_wf"]), T_(kw["L0_ffn_wi"]), T_(kw["L0_ffn_wc"])], 1))
    com["ufic"] = _wchunks(np.concatenate([T_(kw["L0_ffn_uf"]), T_(kw["L0_ffn_ui"]), T_(kw["L0_ffn_uc"])], 1))
    com["wovT"] = _wchunks(T_(kw["L0_ffn_wo"]))
    com["uovT"] = _wchunks(T_(kw["L0_ffn_uo"]))
    com["incb"] = np.ascontiguousarray(
        np.broadcast_to(np.asarray(kw["L0_ffn_in_c"]).astype(np.float32).reshape(DCH, P, 1), (DCH, P, 256)))
    com["wqkv"] = _wchunks(T_(kw["L1_attn_a_Wqkv"]))
    com["wprj"] = _wchunks(T_(kw["L1_attn_a_Wproj"]))
    com["wfcT"] = _wchunks(T_(kw["L1_ffn_W_fc"]))
    com["wprT"] = _wchunks(T_(kw["L1_ffn_W_proj"]))
    com["afcT"] = _wchunks(T_(kw["L1_ffn_A_fc"]))
    com["bfcT"] = np.ascontiguousarray(T_(kw["L1_ffn_B_fc"]).astype(np.float32)).reshape(1, 16, 4096)
    com["aprT"] = _wchunks(T_(kw["L1_ffn_A_proj"]))
    com["bprT"] = np.ascontiguousarray(T_(kw["L1_ffn_B_proj"]).astype(np.float32)).reshape(1, 16, D)
    com["wqB"] = _wchunks(np.asarray(kw["L1_attn_b_Wq"]))
    com["wlB"] = _wchunks(np.asarray(kw["L1_attn_b_Wl"]))
    com["wlBg"] = _wgroups(np.asarray(kw["L1_attn_b_Wl"]))
    com["wkB"] = _wchunks(np.asarray(kw["L1_attn_b_Wk"]))
    com["wvB"] = _wchunks(np.asarray(kw["L1_attn_b_Wv"]))
    com["woB"] = _wchunks(np.asarray(kw["L1_attn_b_Wo"]))

    biases = np.zeros((P, 200), np.float32)
    def setb(name, v):
        v = np.asarray(v).astype(np.float32)
        cols = v.size // P
        biases[:, BOFF[name]:BOFF[name] + cols] = v.reshape(cols, P).T
    setb("wo_b", kw["L0_attn_a_wo_b"])
    setb("bq", kw["L0_attn_b_bq"]); setb("bk", kw["L0_attn_b_bk"]); setb("bv", kw["L0_attn_b_bv"])
    setb("bfic", np.concatenate([np.asarray(kw["L0_ffn_bf"]), np.asarray(kw["L0_ffn_bi"]), np.asarray(kw["L0_ffn_bc"])]))
    setb("bo_f", kw["L0_ffn_bo"])
    setb("bqkv", kw["L1_attn_a_bqkv"]); setb("bproj", kw["L1_attn_a_bproj"])
    setb("b_fc", kw["L1_ffn_b_fc"]); setb("b_proj", kw["L1_ffn_b_proj"])
    setb("Bq", kw["L1_attn_b_bq"]); setb("Bk", kw["L1_attn_b_bk"]); setb("Bv", kw["L1_attn_b_bv"])
    setb("Bo", kw["L1_attn_b_bo"]); setb("Bl", kw["L1_attn_b_bl"])
    biases[:, BOFF["eps"]] = 1e-5
    biases[:, BOFF["epsa"]] = 0.001

    com["biases"] = biases
    com["cosT"] = cosT; com["sinT"] = sinT
    com["cosTb"] = np.ascontiguousarray(cosT.astype(BF)); com["sinTb"] = np.ascontiguousarray(sinT.astype(BF))
    com["maskmul"] = mm
    com["onesb"] = np.ones((P, T), BF)
    com["onesrow"] = np.ones((1, 512), BF)
    hs = np.zeros((P, 2), np.float32); hs[0:64, 0] = 1.0; hs[64:128, 1] = 1.0
    com["headsum"] = hs.astype(BF)
    sc = np.zeros((1, 8), np.float32); sc[0, 0] = float(np.asarray(kw["L1_attn_a_scale"]))
    com["scal"] = sc
    com["scalb"] = np.ascontiguousarray(np.broadcast_to(sc[0:1, 0:1], (P, 1)))
    bvrow = np.zeros((1, 2 * D), np.float32)
    bvrow[0, 0:D] = np.asarray(kw["L0_attn_b_bv"])
    bvrow[0, D:2 * D] = np.asarray(kw["L1_attn_b_bv"])
    com["bvrow"] = bvrow.astype(BF)

    in_maps = []
    for c in range(NC):
        m = dict(com)
        m["xT"] = np.ascontiguousarray(x[c].T).reshape(DCH, P, T).astype(np.float32)
        in_maps.append(m)
    return in_maps


def kernel(**kw):
    in_maps = _prep_inputs(kw)
    nc = build_nc(N_SUB)
    res = run_bass_kernel_spmd(nc, in_maps, core_ids=list(range(NC)))
    out = np.empty((NC, T, D), np.float32)
    for c in range(NC):
        y = np.asarray(res.results[c]["yT"])
        out[c] = y.reshape(D, T).T
    return out
